# version 26
# baseline (speedup 1.0000x reference)
import contextlib
import os
import numpy as np
import ml_dtypes
import concourse.bass as bass
import concourse.mybir as mybir
from concourse.bass_utils import run_bass_kernel_spmd

F32 = mybir.dt.float32
BF16 = mybir.dt.bfloat16
FP8 = mybir.dt.float8e4
AF = mybir.ActivationFunctionType
ALU = mybir.AluOpType
AX = mybir.AxisListType

DM = 1024
H = 16
DH = 64
HP = H // 2
LN_EPS = 1e-5
ROPE_BASE = 10000.0
V65 = DH + 1
VROW = H * V65

ENGINES = ("pe", "act", "dve", "pool", "sp")


class Sched:

    def __init__(self):
        self.ops = []
        self.counts = {e: 0 for e in ENGINES}
        self.dma_counts = {}
        self.ms = {}
        self.base = {}

    SELF_SYNC = ("act", "dve", "pool")

    def add(self, eng, emit, waits=(), ms=None, dma=None, raw=False,
            fence=False):
        waits = tuple(w for w in waits if w is not None)
        for w in waits:
            assert w in self.ms, f"wait on undefined milestone {w}"
        if raw:
            self.ops.append((eng, emit, waits, None, None, True, None))
            return
        if dma is None:
            self.counts[eng] += 1
            if ms is not None:
                self.ms[ms] = (eng, self.counts[eng])
        else:
            self.dma_counts[dma] = self.dma_counts.get(dma, 0) + 16
            if ms is not None:
                self.ms[ms] = (dma, self.dma_counts[dma])
        fence_val = self.counts.get(eng, 0) - (1 if dma is None else 0) \
            if (fence and ms is not None) else None
        self.ops.append((eng, emit, waits, ms, dma, False, fence_val))

    def replay(self, eng_name, sems, limit=None):
        src = self.ops if limit is None else self.ops[:limit]
        ops = [o for o in src if o[0] == eng_name]
        self_sync = eng_name in self.SELF_SYNC

        def f(e):
            waited = {}
            n_done = self.base.get(eng_name, 0)
            for (_, emit, waits, ms, dma, raw, fence_val) in ops:
                for w in waits:
                    sk, val = self.ms[w]
                    if sk == eng_name and self_sync:
                        continue
                    if waited.get(sk, -1) >= val:
                        continue
                    e.wait_ge(sems[sk], val)
                    waited[sk] = val
                if not raw and self_sync and n_done > 0:
                    e.wait_ge(sems[eng_name], n_done)
                elif fence_val is not None and fence_val > 0 and not self_sync:
                    e.wait_ge(sems[eng_name], fence_val)
                ins = emit(e)
                if raw:
                    continue
                if dma is not None:
                    ins.then_inc(sems[dma], 16)
                else:
                    ins.then_inc(sems[eng_name], 1)
                    n_done += 1
        return f


def host_tables(seq):
    theta = 1.0 / ROPE_BASE ** (np.arange(0, DH, 2, dtype=np.float64) / DH)
    pos = np.arange(seq, dtype=np.float64)
    fr = pos[:, None] * theta[None, :]
    emb = np.concatenate([fr, fr], axis=-1)
    cos = np.cos(emb)
    sin = np.sin(emb)
    ct = cos.astype(np.float32)
    st2 = np.empty_like(ct)
    st2[:, 0::2] = sin[:, 1::2].astype(np.float32)
    st2[:, 1::2] = (-sin[:, 0::2]).astype(np.float32)
    return ct, st2


def build_program(seq):
    NT = seq // 128
    NT2 = seq // 512
    NCH = seq // 128
    NSL = seq // 512

    nc = bass.Bass("TRN2", target_bir_lowering=False)

    x_hbm = nc.dram_tensor("x", [seq, DM], F32, kind="ExternalInput")
    gamma_h = nc.dram_tensor("gamma", [1, DM], F32, kind="ExternalInput")
    wq_h = nc.dram_tensor("wq", [DM, DM], F32, kind="ExternalInput")
    wkv_h = nc.dram_tensor("wkv", [DM, 2 * DM], F32, kind="ExternalInput")
    wb_h = nc.dram_tensor("wb", [128, HP * DH], F32, kind="ExternalInput")
    wo_h = nc.dram_tensor("wo", [DM, DM], F32, kind="ExternalInput")
    wf_h = nc.dram_tensor("wf", [DM, DM], F32, kind="ExternalInput")
    bias_h = nc.dram_tensor("bias", [1, DM], F32, kind="ExternalInput")
    is_h = nc.dram_tensor("is_", [1, 1], F32, kind="ExternalInput")
    ws_h = nc.dram_tensor("ws", [1, 1], F32, kind="ExternalInput")
    ct_h = nc.dram_tensor("ct", [seq, DH], F32, kind="ExternalInput")
    st_h = nc.dram_tensor("st", [seq, DH], F32, kind="ExternalInput")
    id_h = nc.dram_tensor("idn", [128, 128], F32, kind="ExternalInput")
    out_hbm = nc.dram_tensor("out", [seq, DM], F32, kind="ExternalOutput")

    qT_hbm = nc.dram_tensor("qT_s", [128, HP * seq], BF16, kind="Internal")
    v_hbm = nc.dram_tensor("v_s", [128, NT * VROW], BF16, kind="Internal")

    with contextlib.ExitStack() as stack:
        id_f32 = stack.enter_context(nc.sbuf_tensor("id_f32", [128, 128], F32))
        id_bf = stack.enter_context(nc.sbuf_tensor("id_bf", [128, 128], BF16))
        ones_r = stack.enter_context(nc.sbuf_tensor("ones_r", [1, 128], F32))
        rstd1_res = stack.enter_context(nc.sbuf_tensor("rstd1", [128, NT], F32))
        nmr1_res = stack.enter_context(nc.sbuf_tensor("nmr1", [128, NT], F32))
        qmax_run = stack.enter_context(nc.sbuf_tensor("qmaxr", [128, H], F32))
        misc = stack.enter_context(nc.sbuf_tensor("misc", [128, 16], F32))
        kt_sb = stack.enter_context(nc.sbuf_tensor("kt_sb", [128, HP * seq], BF16))
        (MC_SUM, MC_SSQ, MC_M, MC_T1, MC_T2, MC_T3, MC_R2, MC_N2,
         MC_ISWS, MC_INVIS, MC_EPS) = range(11)

        sems = {}
        for name in ("pe", "act", "dve", "pool", "sp", "bar", "dP", "dP0", "dG",
                     "dX0", "dX1", "dX2", "dSq0", "dSq1", "dSk0", "dSk1", "dSv0", "dSv1", "dP2", "dW0", "dW1",
                     "dQ", "dSo"):
            sems[name] = stack.enter_context(nc.semaphore("sem_" + name))

        with contextlib.ExitStack() as p1:
            wq_sb = p1.enter_context(nc.sbuf_tensor("wq_sb", [128, 8 * DM], BF16))
            wkv_sb = p1.enter_context(nc.sbuf_tensor("wkv_sb", [128, 16 * DM], BF16))
            wb_sb = p1.enter_context(nc.sbuf_tensor("wb_sb", [128, HP * DH], BF16))
            ct_sb = p1.enter_context(nc.sbuf_tensor("ct_sb", [128, NT * DH], F32))
            st_sb = p1.enter_context(nc.sbuf_tensor("st_sb", [128, NT * DH], F32))
            gamma_bc = p1.enter_context(nc.sbuf_tensor("gamma_bc", [128, DM], F32))
            grow = p1.enter_context(nc.sbuf_tensor("grow", [1, DM], F32))
            xbl = [p1.enter_context(nc.sbuf_tensor(f"xb{i}", [128, DM], F32)) for i in range(3)]
            xnn_sb = [p1.enter_context(nc.sbuf_tensor(f"xnn_sb{i}", [128, DM], F32)) for i in range(2)]
            sqdump = p1.enter_context(nc.sbuf_tensor("sqdump", [128, DM], BF16))
            xn2_sb = [p1.enter_context(nc.sbuf_tensor(f"xn2_sb{i}", [128, DM], BF16)) for i in range(2)]
            xn2T = [p1.enter_context(nc.sbuf_tensor(f"xn2T{i}", [128, DM], BF16)) for i in range(2)]
            ropeA = [p1.enter_context(nc.sbuf_tensor(f"ropeA{i}", [128, DM], BF16)) for i in range(2)]
            ropeB = [p1.enter_context(nc.sbuf_tensor(f"ropeB{i}", [128, DM], BF16)) for i in range(2)]
            qr_sb = [p1.enter_context(nc.sbuf_tensor(f"qr_sb{i}", [128, DM], BF16)) for i in range(2)]
            kr_sb = [p1.enter_context(nc.sbuf_tensor(f"kr_sb{i}", [128, DM], BF16)) for i in range(2)]
            qTs = [p1.enter_context(nc.sbuf_tensor(f"qTs{i}", [128, DM], BF16)) for i in range(2)]
            kTs = [p1.enter_context(nc.sbuf_tensor(f"kTs{i}", [128, DM], BF16)) for i in range(2)]
            vst = [p1.enter_context(nc.sbuf_tensor(f"vst{i}", [128, VROW], BF16)) for i in range(2)]
            qn_sb = p1.enter_context(nc.sbuf_tensor("qn_sb", [128, H], F32))
            stat = [p1.enter_context(nc.sbuf_tensor(f"stat{i}", [128, 16], F32)) for i in range(2)]

            P0 = p1.enter_context(nc.psum_tensor("P0", [128, 1024], F32))
            P1 = p1.enter_context(nc.psum_tensor("P1", [128, 1024], F32))
            P2 = p1.enter_context(nc.psum_tensor("P2", [128, 512], F32))
            P3 = p1.enter_context(nc.psum_tensor("P3", [128, 512], F32))
            PT = p1.enter_context(nc.psum_tensor("PT", [128, 1024], BF16))
            PTB = p1.enter_context(nc.psum_tensor("PTB", [128, 1024], BF16))

            S = Sched()

            S.add("dve", lambda e: e.memset(misc[0:1, 14:15], 0.0),
                ms="scr_init")
            S.add("act", lambda e: e.activation(
                misc[0:1, 15:16], misc[0:1, 14:15], AF.Square),
                waits=["scr_init"])
            S.add("pool", lambda e: e.dma_start(
                wq_sb[:, :].rearrange("p (fc o) -> p fc o", o=DM),
                wq_h[:, :].rearrange("(fc p) o -> p fc o", p=128)),
                dma="dG", ms="ld_wq_i")
            S.add("pool", lambda e: e.dma_start(
                wkv_sb[:, :].rearrange("p (fc o) -> p fc o", o=2 * DM),
                wkv_h[:, :].rearrange("(fc p) o -> p fc o", p=128)),
                dma="dG", ms="ld_wkv_i")
            S.add("pool", lambda e: e.dma_start(wb_sb[:], wb_h[:, :]),
                dma="dG", ms="wload")
            S.add("sp", lambda e: e.dma_start(id_f32[:], id_h[:, :]),
                dma="dP0", ms="ld_id_i")
            S.add("sp", lambda e: e.dma_start(grow[:], gamma_h[:, :]),
                dma="dP0", ms="p1_pre")
            S.add("sp", lambda e: e.dma_start(
                ct_sb[:, :].rearrange("p (t d) -> p t d", d=DH),
                ct_h[:, :].rearrange("(t p) d -> p t d", p=128)),
                dma="dP", ms="ld_ct_i")
            S.add("sp", lambda e: e.dma_start(
                st_sb[:, :].rearrange("p (t d) -> p t d", d=DH),
                st_h[:, :].rearrange("(t p) d -> p t d", p=128)),
                dma="dP", ms="tab_ld")
            S.add("dve", lambda e: e.memset(ones_r[:], 1.0), ms="ones_done")
            S.add("dve", lambda e: e.memset(
                misc[:, MC_EPS:MC_EPS + 1], LN_EPS), ms="eps_done")
            S.add("dve", lambda e: e.memset(vst[0][:], 1.0))
            S.add("dve", lambda e: e.memset(vst[1][:], 1.0), ms="vst_ones")
            S.add("dve", lambda e: e.memset(qmax_run[:], 0.0), ms="qmax_init")
            S.add("act", lambda e: e.activation(id_bf[:], id_f32[:], AF.Copy),
                waits=["p1_pre"], ms="idbf_done")
            S.add("dve", lambda e: e.tensor_scalar(
                wq_sb[:], wq_sb[:], 0.125, None, ALU.mult),
                waits=["wload"], ms="wq_scaled")
            for nn in range(2):
                S.add("pe", lambda e, nn=nn: e.matmul(
                    P0[:, nn * 512:(nn + 1) * 512], ones_r[:, :],
                    grow[0:1, nn * 512:(nn + 1) * 512], start=True, stop=True),
                    waits=["ones_done", "p1_pre"],
                    ms=("gbc_mm" if nn == 1 else None))
            S.add("dve", lambda e: e.tensor_copy(gamma_bc[:], P0[:, :]),
                waits=["gbc_mm"], ms="gbc_done")

            def stageAload(t):
                xb = xbl[t % 3]
                S.add("sp", lambda e, t=t, xb=xb: e.dma_start(
                    xb[:], x_hbm[t * 128:(t + 1) * 128, :]),
                    waits=([f"xnn_{t-3}", f"rs1_{t-3}"] if t >= 3 else []),
                    dma=("dX0", "dX1", "dX2")[t % 3], ms=f"ldx_{t}")

            def stageA(t):
                p = t % 2
                xb = xbl[t % 3]
                ST = stat[p]
                xnn_ = xnn_sb[p]; xn2_ = xn2_sb[p]
                S.add("dve", lambda e, xb=xb, ST=ST: e.reduce_sum(
                    ST[:, MC_SUM:MC_SUM + 1], xb[:], AX.X),
                    waits=[f"ldx_{t}"], ms=f"rs1_{t}")
                S.add("act", lambda e, xb=xb, ST=ST: e.activation(
                    sqdump[:], xb[:], AF.Square,
                    accum_out=ST[:, MC_SSQ:MC_SSQ + 1]),
                    waits=[f"ldx_{t}"] + ([f"mic_{t-2}"] if t >= 2 else []),
                    ms=f"sq1_{t}")
                S.add("dve", lambda e, ST=ST: e.tensor_scalar(
                    ST[:, MC_M:MC_M + 1], ST[:, MC_SUM:MC_SUM + 1],
                    1.0 / DM, None, ALU.mult), waits=[f"sq1_{t}"])
                S.add("dve", lambda e, ST=ST: e.tensor_tensor(
                    ST[:, MC_T1:MC_T1 + 1], ST[:, MC_M:MC_M + 1],
                    ST[:, MC_M:MC_M + 1], ALU.mult))
                S.add("dve", lambda e, ST=ST: e.tensor_scalar(
                    ST[:, MC_T2:MC_T2 + 1], ST[:, MC_SSQ:MC_SSQ + 1],
                    1.0 / DM, None, ALU.mult))
                S.add("dve", lambda e, ST=ST: e.tensor_tensor(
                    ST[:, MC_T2:MC_T2 + 1], ST[:, MC_T2:MC_T2 + 1],
                    ST[:, MC_T1:MC_T1 + 1], ALU.subtract), ms=f"var1_{t}")
                S.add("act", lambda e, ST=ST: e.activation(
                    ST[:, MC_T3:MC_T3 + 1], ST[:, MC_T2:MC_T2 + 1],
                    AF.Sqrt, bias=misc[:, MC_EPS:MC_EPS + 1]),
                    waits=[f"var1_{t}", "eps_done"], ms=f"sd1_{t}")
                S.add("dve", lambda e, t=t, ST=ST: e.reciprocal(
                    rstd1_res[:, t:t + 1], ST[:, MC_T3:MC_T3 + 1]),
                    waits=[f"sd1_{t}"])
                S.add("dve", lambda e, t=t, ST=ST: e.tensor_scalar(
                    nmr1_res[:, t:t + 1], ST[:, MC_M:MC_M + 1],
                    rstd1_res[:, t:t + 1], -1.0, ALU.mult, ALU.mult))
                S.add("dve", lambda e, ST=ST: e.tensor_tensor(
                    ST[:, MC_T1:MC_T1 + 1], ST[:, MC_T3:MC_T3 + 1],
                    ST[:, MC_T3:MC_T3 + 1], ALU.mult))
                S.add("dve", lambda e, ST=ST: e.reciprocal(
                    ST[:, MC_T1:MC_T1 + 1], ST[:, MC_T1:MC_T1 + 1]))
                S.add("dve", lambda e, ST=ST: e.tensor_scalar(
                    ST[:, MC_T1:MC_T1 + 1], ST[:, MC_T2:MC_T2 + 1],
                    ST[:, MC_T1:MC_T1 + 1], None, ALU.mult),
                    ms=f"v2_{t}")
                S.add("act", lambda e, ST=ST: e.activation(
                    ST[:, MC_T1:MC_T1 + 1], ST[:, MC_T1:MC_T1 + 1],
                    AF.Sqrt, bias=misc[:, MC_EPS:MC_EPS + 1]),
                    waits=[f"v2_{t}"], ms=f"sd2_{t}")
                S.add("dve", lambda e, ST=ST: e.reciprocal(
                    ST[:, MC_T1:MC_T1 + 1], ST[:, MC_T1:MC_T1 + 1]),
                    waits=[f"sd2_{t}"])
                S.add("dve", lambda e, t=t, ST=ST: e.tensor_scalar(
                    ST[:, MC_R2:MC_R2 + 1], rstd1_res[:, t:t + 1],
                    ST[:, MC_T1:MC_T1 + 1], None, ALU.mult))
                S.add("dve", lambda e, ST=ST: e.tensor_scalar(
                    ST[:, MC_N2:MC_N2 + 1], ST[:, MC_M:MC_M + 1],
                    ST[:, MC_R2:MC_R2 + 1], -1.0, ALU.mult, ALU.mult),
                    ms=f"mic_{t}")
                S.add("act", lambda e, xb=xb, xnn_=xnn_, ST=ST: e.activation(
                    xnn_[:], xb[:], AF.Identity,
                    bias=ST[:, MC_N2:MC_N2 + 1],
                    scale=ST[:, MC_R2:MC_R2 + 1]),
                    waits=[f"mic_{t}"] + ([f"xn2_{t-2}"] if t >= 2 else []),
                    ms=f"xnn_{t}")
                S.add("pool", lambda e, xnn_=xnn_, xn2_=xn2_: e.tensor_tensor(
                    xn2_[:], xnn_[:], gamma_bc[:], ALU.mult),
                    waits=[f"xnn_{t}", "gbc_done"] +
                          ([f"tr1_{t-2}"] if t >= 2 else []),
                    ms=f"xn2_{t}")

            def stageB(t, part):
                p = t % 2
                x2T = xn2T[p]; rA = ropeA[p]; rB = ropeB[p]
                qr_ = qr_sb[p]; kr_ = kr_sb[p]
                qT_ = qTs[p]; kT_ = kTs[p]; vs_ = vst[p]
                xn2_ = xn2_sb[p]
                ctv = ct_sb[:, t * DH:(t + 1) * DH].rearrange(
                    "p (o d) -> p o d", o=1).to_broadcast([128, H, DH])
                stv = st_sb[:, t * DH:(t + 1) * DH].rearrange(
                    "p (o d) -> p o d", o=1).to_broadcast([128, H, DH])

                def rope(src_ps, pev, ev_ms, dst, mm_ms, out_ms, guard):
                    S.add("dve", lambda e, src_ps=src_ps, ctv=ctv,
                          rA=rA: e.tensor_tensor(
                        rA[:, :].rearrange("p (h d) -> p h d", d=DH),
                        src_ps[:, :].rearrange("p (h d) -> p h d", d=DH),
                        ctv, ALU.mult),
                        waits=[mm_ms, "p1_pre", "tab_ld"] + guard)
                    S.add("dve", lambda e, src_ps=src_ps, stv=stv,
                          rB=rB: e.tensor_tensor(
                        rB[:, :].rearrange("p (h d) -> p h d", d=DH),
                        src_ps[:, :].rearrange("p (h d) -> p h d", d=DH),
                        stv, ALU.mult),
                        ms=ev_ms)
                    ra = rA[:, :].rearrange("p (n two) -> p n two", two=2)
                    rb = rB[:, :].rearrange("p (n two) -> p n two", two=2)
                    dd = dst[:, :].rearrange("p (n two) -> p n two", two=2)
                    S.add("dve", lambda e, ra=ra, rb=rb, dd=dd: e.tensor_tensor(
                        dd[:, :, 0:1], ra[:, :, 0:1], rb[:, :, 1:2], ALU.add))
                    S.add("dve", lambda e, ra=ra, rb=rb, dd=dd: e.tensor_tensor(
                        dd[:, :, 1:2], ra[:, :, 1:2], rb[:, :, 0:1], ALU.add),
                        ms=out_ms)

                if part == 1:
                    for b in range(8):
                        S.add("pe", lambda e, b=b, xn2_=xn2_: e.transpose(
                            PTB[:, b * 128:(b + 1) * 128],
                            xn2_[:, b * 128:(b + 1) * 128], id_bf[:]),
                            waits=(["idbf_done", f"xn2_{t}"] +
                                   ([f"ev2_{t-1}"] if b == 0 and t >= 1 else [])),
                            ms=(f"tr1_{t}" if b == 7 else None))
                    S.add("act", lambda e, x2T=x2T: e.activation(
                        x2T[:], PTB[:, :], AF.Copy),
                        waits=[f"tr1_{t}"] + ([f"vmm_{t-2}"] if t >= 2 else []),
                        ms=f"ev1_{t}")
                    return
                if part == 12:
                    for fc in range(8):
                        for nn in range(2):
                            S.add("pe", lambda e, fc=fc, nn=nn, x2T=x2T: e.matmul(
                                P0[:, nn * 512:(nn + 1) * 512],
                                x2T[:, fc * 128:(fc + 1) * 128],
                                wq_sb[:, fc * DM + nn * 512: fc * DM + (nn + 1) * 512],
                                start=(fc == 0), stop=(fc == 7)),
                                waits=([f"ev1_{t}", "wq_scaled"] +
                                       ([f"evq_{t-1}"] if t >= 1 else []) +
                                       (["gbc_done"] if t == 0 else [])
                                       if fc == 0 and nn == 0 else []),
                                ms=(f"qmm_{t}" if fc == 7 and nn == 1 else None))
                    for fc in range(8):
                        for nn in range(2):
                            S.add("pe", lambda e, fc=fc, nn=nn, x2T=x2T: e.matmul(
                                P1[:, nn * 512:(nn + 1) * 512],
                                x2T[:, fc * 128:(fc + 1) * 128],
                                wkv_sb[:, fc * 2 * DM + nn * 512:
                                       fc * 2 * DM + (nn + 1) * 512],
                                start=(fc == 0), stop=(fc == 7)),
                                waits=((["wload"] +
                                        ([f"evk_{t-1}"] if t >= 1 else []))
                                       if fc == 0 and nn == 0 else []),
                                ms=(f"kmm_{t}" if fc == 7 and nn == 1 else None))
                    rope(P0, None, f"evq_{t}", qr_, f"qmm_{t}", f"ropq_{t}",
                         [f"tr2_{t-2}"] if t >= 2 else [])
                    S.add("dve", lambda e, qr_=qr_, rA=rA: e.tensor_tensor(
                        rA[:], qr_[:], qr_[:], ALU.mult),
                        waits=[f"ropq_{t}"])
                    S.add("dve", lambda e, rA=rA: e.tensor_reduce(
                        qn_sb[:], rA[:, :].rearrange("p (h d) -> p h d", d=DH),
                        AX.X, ALU.add))
                    S.add("dve", lambda e: e.tensor_tensor(
                        qmax_run[:], qmax_run[:], qn_sb[:], ALU.max),
                        waits=["qmax_init"], ms=f"qn_{t}")
                    return
                if part == 15:
                    for b in range(8):
                        S.add("pe", lambda e, b=b, qr_=qr_: e.transpose(
                            PT[:, b * 128:(b + 1) * 128],
                            qr_[:, b * 128:(b + 1) * 128], id_bf[:]),
                            waits=([f"ropq_{t}"] +
                                   ([f"ev3_{t-1}"] if t >= 1 else [])
                                   if b == 0 else []),
                            ms=(f"tr2_{t}" if b == 7 else None))
                    S.add("act", lambda e, qT_=qT_: e.activation(
                        qT_[:], PT[:, :], AF.Copy),
                        waits=[f"tr2_{t}"] + ([f"stq_{t-2}"] if t >= 2 else []),
                        ms=f"ev2_{t}")
                    S.add("sp", lambda e, t=t, qT_=qT_: e.dma_start(
                        qT_hbm[:, :].rearrange("p (hp n) -> p hp n", n=seq)[
                            :, :, t * 128:(t + 1) * 128],
                        qT_[:, :].rearrange("p (hp n) -> p hp n", n=128)),
                        waits=[f"ev2_{t}"], dma=("dSq0" if t % 2 == 0 else "dSq1"), ms=f"stq_{t}")
                    rope(P1, None, f"evk_{t}", kr_, f"kmm_{t}", f"ropk_{t}",
                         [f"tr3_{t-2}"] if t >= 2 else [])
                    return
                if part == 2:
                    for fc in range(8):
                        for ps, nn in ((P2, 2), (P3, 3)):
                            S.add("pe", lambda e, fc=fc, nn=nn, ps=ps, x2T=x2T: e.matmul(
                                ps[:, :],
                                x2T[:, fc * 128:(fc + 1) * 128],
                                wkv_sb[:, fc * 2 * DM + nn * 512:
                                       fc * 2 * DM + (nn + 1) * 512],
                                start=(fc == 0), stop=(fc == 7)),
                                waits=(([f"ktev_{t-1}"] if t >= 1 else [])
                                       if fc == 0 and nn == 2 else []),
                                ms=(f"vmm_{t}" if fc == 7 and nn == 3 else None))
                    for b in range(8):
                        S.add("pe", lambda e, b=b, kr_=kr_: e.transpose(
                            PT[:, b * 128:(b + 1) * 128],
                            kr_[:, b * 128:(b + 1) * 128], id_bf[:]),
                            waits=([f"ropk_{t}", f"ev2_{t}"] if b == 0 else []),
                            ms=(f"tr3_{t}" if b == 7 else None))
                    S.add("act", lambda e, kT_=kT_: e.activation(
                        kT_[:], PT[:, :], AF.Copy),
                        waits=[f"tr3_{t}"] +
                              ([f"bil_done_{t-2}"] if t >= 2 else []),
                        ms=f"ev3_{t}")
                    vout = vs_[:, :].rearrange("p (h e) -> p h e", e=V65)
                    S.add("dve", lambda e, vout=vout: e.tensor_copy(
                        vout[:, 0:8, 0:DH],
                        P2[:, :].rearrange("p (h d) -> p h d", d=DH)),
                        waits=[f"vmm_{t}", "vst_ones"] +
                              ([f"stv_{t-2}"] if t >= 2 else []))
                    S.add("dve", lambda e, vout=vout: e.tensor_copy(
                        vout[:, 8:16, 0:DH],
                        P3[:, :].rearrange("p (h d) -> p h d", d=DH)),
                        ms=f"vev_{t}")
                    S.add("sp", lambda e, t=t, vs_=vs_: e.dma_start(
                        v_hbm[:, t * VROW:(t + 1) * VROW], vs_[:]),
                        waits=[f"vev_{t}"], dma=("dSv0" if t % 2 == 0 else "dSv1"),
                        ms=f"stv_{t}")
                    return
                for hp in range(HP):
                    pbk = P2 if hp % 2 == 0 else P3
                    slot = hp // 2
                    for par in range(2):
                        b0 = par * 64
                        S.add("pe", lambda e, hp=hp, b0=b0, slot=slot,
                              pbk=pbk, kT_=kT_: e.matmul(
                            pbk[b0:b0 + 64, slot * 128:(slot + 1) * 128],
                            wb_sb[b0:b0 + 64, hp * DH:(hp + 1) * DH],
                            kT_[b0:b0 + 64, hp * 128:(hp + 1) * 128],
                            start=True, stop=True, tile_position=(b0, b0)),
                            waits=(([f"ev3_{t}", "wload", f"vev_{t}"] +
                                    ([f"ktev_{t-1}"] if t >= 1 else []))
                                   if hp == 0 and par == 0 else []),
                            ms=(f"bil_{t}" if hp == HP - 1 and par == 1
                                else None))
                S.add("pe", lambda e: e.nop(), waits=[f"bil_{t}"],
                    ms=f"bil_done_{t}", fence=True)
                kview = kt_sb[:, :].rearrange("p (hp n) -> p hp n", n=seq)[
                    :, :, t * 128:(t + 1) * 128]
                S.add("dve", lambda e, kview=kview: e.tensor_copy(
                    kview[:, 0:8:2, :],
                    P2[:, :].rearrange("p (sl c) -> p sl c", c=128)),
                    waits=[f"bil_done_{t}"])
                S.add("dve", lambda e, kview=kview: e.tensor_copy(
                    kview[:, 1:8:2, :],
                    P3[:, :].rearrange("p (sl c) -> p sl c", c=128)),
                    ms=f"ktev_{t}")

            stageAload(0)
            stageAload(1)
            stageAload(2)
            stageA(0)
            stageA(1)
            for t in range(NT):
                stageB(t, 1)
                if t >= 1:
                    stageB(t - 1, 2)
                stageB(t, 12)
                if t >= 1:
                    stageB(t - 1, 25)
                stageB(t, 15)
                if t + 3 < NT:
                    stageAload(t + 3)
                if t + 2 < NT:
                    stageA(t + 2)
            stageB(NT - 1, 2)
            stageB(NT - 1, 25)

            store_streams = ["dSq0", "dSq1", "dSk0", "dSk1", "dSv0", "dSv1"]
            store_totals = {k: S.dma_counts.get(k, 0) for k in store_streams}
            for k in store_streams:
                S.add("act", lambda e, k=k: e.wait_ge(sems[k], store_totals[k]),
                      raw=True)
            for eng in ENGINES:
                S.add(eng, lambda e: e.nop().then_inc(sems["bar"], 1), raw=True)

            _lim1 = os.environ.get("KDBG_P1LIMIT")
            _lim1 = int(_lim1) if _lim1 else None
            with nc.Block() as blockA:
                blockA.tensor(S.replay("pe", sems, _lim1))
                blockA.scalar(S.replay("act", sems, _lim1))
                blockA.vector(S.replay("dve", sems, _lim1))
                blockA.gpsimd(S.replay("pool", sems, _lim1))
                blockA.sync(S.replay("sp", sems, _lim1))
            print(f"[kernel] phase1 ops: {len(S.ops)} (limit={_lim1})", flush=True)
            ph1_store_totals = dict(store_totals)
            ph1_dma_counts = dict(S.dma_counts)
            ph1_counts = dict(S.counts)

        with contextlib.ExitStack() as p2:
            v_sb = p2.enter_context(nc.sbuf_tensor("v_sb", [128, NT * VROW], BF16))
            wo_sb = p2.enter_context(nc.sbuf_tensor("wo_sb", [128, 8 * DM], BF16))
            wft_sb = p2.enter_context(nc.sbuf_tensor("wft_sb", [128, 8 * DM], FP8))
            qT_t = p2.enter_context(nc.sbuf_tensor("qT_t", [128, HP * 512], BF16))
            ET = p2.enter_context(nc.sbuf_tensor("ET", [128, 2048], BF16))
            pvE = p2.enter_context(nc.sbuf_tensor("pvE", [128, 1024], BF16))
            aoT = [p2.enter_context(nc.sbuf_tensor(f"aoT{i}", [128, HP * 512], BF16)) for i in range(2)]
            rlv = p2.enter_context(nc.sbuf_tensor("rlv", [128, 8], F32))
            xs0 = p2.enter_context(nc.sbuf_tensor("xs0", [128, DM], F32))
            xnf = p2.enter_context(nc.sbuf_tensor("xnf", [128, DM], F32))
            qtmp = p2.enter_context(nc.sbuf_tensor("qtmp", [128, 512], F32))
            qtmp2 = xnf
            xqT = p2.enter_context(nc.sbuf_tensor("xqT", [128, DM], FP8))
            yv_sb = p2.enter_context(nc.sbuf_tensor("yv_sb", [128, DM], F32))
            gbuf = p2.enter_context(nc.sbuf_tensor("gbuf", [128, DM], BF16))
            ao_sb = p2.enter_context(nc.sbuf_tensor("ao_sb", [128, 512], BF16))
            bias_bc = p2.enter_context(nc.sbuf_tensor("bias_bc", [128, DM], F32))
            srow = p2.enter_context(nc.sbuf_tensor("srow", [1, 8], F32))
            srow2 = p2.enter_context(nc.sbuf_tensor("srow2", [2, 1], F32))
            negc_bc = p2.enter_context(nc.sbuf_tensor("negc_bc", [128, 1], F32))
            k2run = p2.enter_context(nc.sbuf_tensor("k2run", [2, 1], F32))
            sel = p2.enter_context(nc.sbuf_tensor("sel", [128, 2], BF16))

            PA = p2.enter_context(nc.psum_tensor("PA", [128, 1024], F32))
            PB = p2.enter_context(nc.psum_tensor("PB", [128, 1024], F32))
            PVa = p2.enter_context(nc.psum_tensor("PVa", [128, 512], F32))
            PVb = p2.enter_context(nc.psum_tensor("PVb", [128, 512], F32))
            PT2 = p2.enter_context(nc.psum_tensor("PT2", [128, 1024], BF16))
            PS4 = p2.enter_context(nc.psum_tensor("PS4", [128, 512], F32))

            S = Sched()
            NBAR = len(ENGINES)
            for eng in ENGINES:
                S.add(eng, lambda e: e.wait_ge(sems["bar"], NBAR), raw=True)
            for k, v in ph1_store_totals.items():
                S.add("sp", lambda e, k=k, v=v: e.wait_ge(sems[k], v), raw=True)
            S.dma_counts.update(ph1_dma_counts)
            S.counts.update(ph1_counts)
            S.base = dict(ph1_counts)

            S.add("act", lambda e: e.activation(
                misc[0:1, 15:16], misc[0:1, MC_EPS:MC_EPS + 1], AF.Sqrt))
            VHALF = NT * VROW // 2
            for sl in range(2):
                S.add("sp", lambda e, sl=sl: e.dma_start(
                    v_sb[:, sl * VHALF:(sl + 1) * VHALF],
                    v_hbm[:, sl * VHALF:(sl + 1) * VHALF]),
                    dma=("dSv0", "dSv1")[sl], ms=f"ld_v{sl}")
            S.add("sp", lambda e: e.dma_start(bias_bc[0:1, :], bias_h[:, :]),
                dma="dP2", ms="ld_bias_i")
            S.add("sp", lambda e: e.dma_start(srow[0:1, 0:1], is_h[:, :]),
                dma="dP2", ms="ld_is_i")
            S.add("sp", lambda e: e.dma_start(srow[0:1, 1:2], ws_h[:, :]),
                dma="dP2", ms="p2_pre")
            S.add("pool", lambda e: e.dma_start(
                wo_sb[:, :].rearrange("p (fc o) -> p fc o", o=DM),
                wo_h[:, :].rearrange("(fc p) o -> p fc o", p=128)),
                dma="dG", ms="ld_wo")
            for oc in range(8):
                slab = xs0 if oc % 2 == 0 else yv_sb
                S.add("sp", lambda e, oc=oc, slab=slab: e.dma_start(
                    slab[:], wf_h[oc * 128:(oc + 1) * 128, :]),
                    waits=([f"wft_{oc-2}"] if oc >= 2 else []),
                    dma=("dW0" if oc % 2 == 0 else "dW1"), ms=f"ld_wf{oc}")
                for b in range(8):
                    pvw = PVa if b < 4 else PVb
                    S.add("pe", lambda e, b=b, slab=slab, pvw=pvw: e.transpose(
                        pvw[:, (b % 4) * 128:(b % 4 + 1) * 128],
                        slab[:, b * 128:(b + 1) * 128], id_f32[:]),
                        waits=([f"ld_wf{oc}"] +
                               ([f"wfc_{oc-1}"] if oc >= 1 else [])
                               if b == 0 else []),
                        ms=(f"wft_{oc}" if b == 7 else None))
                for kk, pvw in ((0, PVa), (1, PVb)):
                    S.add("act", lambda e, oc=oc, kk=kk, pvw=pvw: e.activation(
                        wft_sb[:, :].rearrange("p (fc oc o) -> p fc oc o",
                                               oc=8, o=128)[:, kk * 4:
                                                            (kk + 1) * 4, oc, :],
                        pvw[:, :].rearrange("p (b o) -> p b o", o=128),
                        AF.Copy),
                        waits=([f"wft_{oc}"] if kk == 0 else []),
                        ms=(f"wfc_{oc}" if kk == 1 else None))
            for nn in range(2):
                S.add("pe", lambda e, nn=nn: e.matmul(
                    PB[:, nn * 512:(nn + 1) * 512], ones_r[:, :],
                    bias_bc[0:1, nn * 512:(nn + 1) * 512], start=True, stop=True),
                    waits=["p2_pre"], ms=("bbc_mm" if nn == 1 else None))
            S.add("dve", lambda e: e.tensor_copy(bias_bc[:], PB[:, :]),
                waits=["bbc_mm"], ms="bbc_done")
            S.add("dve", lambda e: e.tensor_tensor(
                srow[0:1, 2:3], srow[0:1, 0:1], srow[0:1, 1:2], ALU.mult),
                waits=["p2_pre"], ms="isws_row")
            S.add("dve", lambda e: e.reciprocal(srow[0:1, 3:4], srow[0:1, 0:1]),
                ms="invis_row")
            S.add("pe", lambda e: e.matmul(
                PS4[:, 0:1], ones_r[:, :], srow[0:1, 2:3], start=True, stop=True),
                waits=["isws_row"], ms="isws_mm")
            S.add("pe", lambda e: e.matmul(
                PS4[:, 1:2], ones_r[:, :], srow[0:1, 3:4], start=True, stop=True),
                waits=["invis_row"], ms="invis_mm")
            S.add("dve", lambda e: e.tensor_copy(
                misc[:, MC_ISWS:MC_ISWS + 1], PS4[:, 0:1]),
                waits=["invis_mm"], ms="isws_done")
            S.add("dve", lambda e: e.tensor_copy(
                misc[:, MC_INVIS:MC_INVIS + 1], PS4[:, 1:2]),
                ms="invis_done")
            S.add("dve", lambda e: e.memset(sel[:], 0.0))
            S.add("dve", lambda e: e.memset(sel[0:64, 0:1], 1.0))
            S.add("dve", lambda e: e.memset(sel[64:128, 1:2], 1.0), ms="sel_done")
            S.add("dve", lambda e: e.memset(k2run[:], 0.0), ms="k2_init")
            prev_red = None
            prev_mm = None
            HLEN = min(2048, seq)
            NJ = HLEN // 512
            NHALF = seq // HLEN
            for hp in range(HP):
                for half in range(NHALF):
                    kh = f"{hp}_{half}"
                    w_sq = []
                    if prev_mm is not None:
                        w_sq.append(prev_mm)
                    S.add("dve", lambda e, hp=hp, half=half: e.tensor_tensor(
                        ET[:, 0:HLEN],
                        kt_sb[:, hp * seq + half * HLEN:
                              hp * seq + (half + 1) * HLEN],
                        kt_sb[:, hp * seq + half * HLEN:
                              hp * seq + (half + 1) * HLEN],
                        ALU.mult),
                        waits=w_sq, ms=f"k2sq_{kh}")
                    for j in range(NJ):
                        w_mm = []
                        if j == 0:
                            w_mm = [f"k2sq_{kh}", "sel_done"]
                            if prev_red is None:
                                w_mm += ["isws_done", "invis_done"]
                            else:
                                w_mm.append(prev_red)
                        else:
                            w_mm = [f"k2r_{kh}_{j-1}"]
                        gname = f"k2mm_{kh}_{j}"
                        S.add("pe", lambda e, j=j: e.matmul(
                            PS4[0:2, 0:512],
                            sel[:, :], ET[:, j * 512:(j + 1) * 512],
                            start=True, stop=True),
                            waits=w_mm, ms=gname)
                        prev_mm = gname
                        S.add("dve", lambda e: e.tensor_reduce(
                            srow2[:, :], PS4[0:2, 0:512],
                            AX.X, ALU.max),
                            waits=[gname])
                        rname = (f"k2red_{kh}" if j == NJ - 1
                                 else f"k2r_{kh}_{j}")
                        S.add("dve", lambda e: e.tensor_tensor(
                            k2run[:], k2run[:], srow2[:, :], ALU.max),
                            waits=["k2_init"], ms=rname)
                        prev_red = rname
            S.add("pe", lambda e: e.transpose(
                PS4[0:1, 0:2], k2run[:, :], id_f32[0:2, 0:2]),
                waits=[prev_red], ms="k2tr")
            S.add("dve", lambda e: e.tensor_reduce(
                srow[0:1, 4:5], PS4[0:1, 0:2], AX.X, ALU.max),
                waits=["k2tr"], ms="k2s_done")
            S.add("dve", lambda e: e.tensor_reduce(
                misc[:, MC_T1:MC_T1 + 1], qmax_run[:], AX.X, ALU.max),
                ms="qm1")
            S.add("pe", lambda e: e.transpose(
                PS4[0:1, 4:132], misc[:, MC_T1:MC_T1 + 1], id_f32[:]),
                waits=["qm1", "k2s_done"], ms="qmtr")
            S.add("dve", lambda e: e.tensor_reduce(
                srow[0:1, 5:6], PS4[0:1, 4:132], AX.X, ALU.max),
                waits=["qmtr"], ms="q2s_done")
            S.add("act", lambda e: e.activation(
                srow[0:1, 6:7], srow[0:1, 4:5], AF.Sqrt),
                waits=["k2s_done"], ms="ks_done")
            S.add("act", lambda e: e.activation(
                srow[0:1, 7:8], srow[0:1, 5:6], AF.Sqrt),
                waits=["q2s_done"], ms="qs_done")
            S.add("act", lambda e: e.activation(
                misc[0:1, 15:16], misc[0:1, MC_EPS:MC_EPS + 1], AF.Exp),
                waits=["qs_done"])
            S.add("dve", lambda e: e.tensor_tensor(
                srow[0:1, 4:5], srow[0:1, 6:7], srow[0:1, 7:8], ALU.mult),
                waits=["ks_done", "qs_done"])
            S.add("dve", lambda e: e.tensor_scalar(
                srow[0:1, 4:5], srow[0:1, 4:5], -1.02, None, ALU.mult),
                ms="negc_row")
            S.add("pe", lambda e: e.matmul(
                PS4[:, 2:3], ones_r[:, :], srow[0:1, 4:5], start=True, stop=True),
                waits=["negc_row"], ms="negc_mm")
            S.add("dve", lambda e: e.tensor_copy(negc_bc[:], PS4[:, 2:3]),
                waits=["negc_mm"], ms="negc_done")

            def epi_steps(ti, hp):
                pp = f"{ti}_{hp}"
                steps = []
                for k_, pv in ((0, PVa), (1, PVb)):
                    w_ev = [f"pv_{pp}_{NCH-1}"] if k_ == 0 else []
                    steps.append(("dve", lambda e, k_=k_, pv=pv: e.tensor_copy(
                        pvE[:, k_ * 260:(k_ + 1) * 260], pv[:, 0:260]),
                        w_ev, (f"pvev_{pp}" if k_ == 1 else None), None, False))
                steps.append(("dve", lambda e: e.reciprocal(
                    rlv[:, 0:8], pvE[:, 64:520:65]), [], None, None, False))
                for blk in range(8):
                    steps.append(("dve", lambda e, blk=blk: e.tensor_scalar(
                        ao_sb[:, blk * 64:(blk + 1) * 64],
                        pvE[:, blk * 65: blk * 65 + 64],
                        rlv[:, blk:blk + 1], None, ALU.mult),
                        [], (f"aos_{pp}" if blk == 7 else None), None, False))
                for blk in range(8):
                    w_btr = []
                    if blk == 0:
                        w_btr.append(f"aos_{pp}")
                        if hp >= 1:
                            w_btr.append(f"aoev_{ti}_{hp-1}")
                        elif ti >= 1:
                            w_btr.append(f"aoev_{ti-1}_{HP-1}")
                        else:
                            w_btr.append(f"k2red_{HP-1}_{NHALF-1}")
                    steps.append(("pe", lambda e, blk=blk: e.transpose(
                        PT2[0:64, blk * 128:(blk + 1) * 128],
                        ao_sb[:, blk * 64:(blk + 1) * 64], id_bf[:]),
                        w_btr, (f"btr_{pp}" if blk == 7 else None),
                        None, True))
                ao_ = aoT[ti % 2]
                for k_ in range(2):
                    w_ao = [f"btr_{pp}"] if k_ == 0 else []
                    if k_ == 0 and hp == 0 and ti >= 2:
                        w_ao.append(f"wo_{ti-2}_3")
                    steps.append(("dve", lambda e, k_=k_, hp=hp,
                                  ao_=ao_: e.tensor_copy(
                        ao_[k_ * 64:(k_ + 1) * 64, hp * 512:(hp + 1) * 512],
                        PT2[0:64, k_ * 512:(k_ + 1) * 512]),
                        w_ao, (f"aoev_{pp}" if k_ == 1 else None), None, False))
                return steps

            def sub_steps(ti):
                steps = []
                ao_ = aoT[ti % 2]
                for s_ in range(4):
                    tt = ti * 4 + s_
                    xsb = xs0
                    w_ldx = []
                    if s_ >= 1:
                        w_ldx.append(f"res_{ti}_{s_-1}")
                    elif ti >= 1:
                        w_ldx.append(f"res_{ti-1}_3")
                    else:
                        w_ldx.append("wfc_7")
                    steps.append(("sp", lambda e, tt=tt, xsb=xsb: e.dma_start(
                        xsb[:], x_hbm[tt * 128:(tt + 1) * 128, :]),
                        w_ldx, f"ldxs_{ti}_{s_}", "dX0", False))
                    w_xnf = [f"ldxs_{ti}_{s_}"]
                    if s_ >= 1:
                        w_xnf += [f"xnTb_{ti}_{s_-1}", f"xq_{ti}_{s_-1}"]
                    elif ti >= 1:
                        w_xnf += [f"xnTb_{ti-1}_3", f"xq_{ti-1}_3"]
                    steps.append(("dve", lambda e, tt=tt, xsb=xsb: e.tensor_scalar(
                        xnf[:], xsb[:], rstd1_res[:, tt:tt + 1],
                        nmr1_res[:, tt:tt + 1], ALU.mult, ALU.add),
                        w_xnf, f"xnf_{ti}_{s_}", None, False))
                    for h2 in range(2):
                        w_t = [f"xnf_{ti}_{s_}"]
                        if h2 == 0:
                            if s_ >= 1:
                                w_t.append(f"sgb_{ti}_{s_-1}")
                            elif ti >= 1:
                                w_t.append(f"sgb_{ti-1}_3")
                            else:
                                w_t.append("negc_done")
                        else:
                            w_t.append(f"q1a_{ti}_{s_}")
                        for b in range(4):
                            steps.append(("pe", lambda e, b=b, h2=h2: e.transpose(
                                PS4[:, b * 128:(b + 1) * 128],
                                xnf[:, h2 * 512 + b * 128:
                                    h2 * 512 + (b + 1) * 128], id_f32[:]),
                                (w_t if b == 0 else []),
                                (f"xnT{'ab'[h2]}_{ti}_{s_}" if b == 3 else None),
                                None, True))
                        steps.append(("dve", lambda e, h2=h2: e.tensor_scalar(
                            qtmp[:, :], PS4[:, :],
                            misc[:, MC_INVIS:MC_INVIS + 1],
                            240.0, ALU.mult, ALU.min),
                            [f"xnT{'ab'[h2]}_{ti}_{s_}", "invis_done"],
                            f"q1{'ab'[h2]}_{ti}_{s_}", None, False))
                        steps.append(("dve", lambda e, h2=h2: e.tensor_scalar(
                            qtmp2[:, h2 * 512:(h2 + 1) * 512], qtmp[:, :],
                            -240.0, None, ALU.max),
                            [], f"q2{'ab'[h2]}_{ti}_{s_}", None, False))
                    steps.append(("dve", lambda e: e.tensor_copy(
                        xqT[:], qtmp2[:]),
                        [f"q2b_{ti}_{s_}"], f"xq_{ti}_{s_}", None, False))
                    for nn in range(2):
                        if nn == 0:
                            w_ffn = [f"xq_{ti}_{s_}", "wfc_7", f"q1b_{ti}_{s_}"]
                        else:
                            w_ffn = [f"yva_{ti}_{s_}"]
                        for fc in range(8):
                            steps.append(("pe", lambda e, fc=fc, nn=nn: e.matmul(
                                PS4[:, :],
                                xqT[:, fc * 128:(fc + 1) * 128],
                                wft_sb[:, fc * DM + nn * 512:
                                       fc * DM + (nn + 1) * 512],
                                start=(fc == 0), stop=(fc == 7)),
                                (w_ffn if fc == 0 else []),
                                (f"ffn{'ab'[nn]}_{ti}_{s_}" if fc == 7
                                 else None), None, True))
                        w_yv = [f"ffn{'ab'[nn]}_{ti}_{s_}", "isws_done",
                                "bbc_done"]
                        if nn == 0:
                            if s_ >= 1:
                                w_yv.append(f"sto_{ti}_{s_-1}")
                            elif ti >= 1:
                                w_yv.append(f"sto_{ti-1}_3")
                        steps.append(("dve", lambda e, nn=nn: e.scalar_tensor_tensor(
                            yv_sb[:, nn * 512:(nn + 1) * 512], PS4[:, :],
                            misc[:, MC_ISWS:MC_ISWS + 1],
                            bias_bc[:, nn * 512:(nn + 1) * 512],
                            ALU.mult, ALU.add),
                            w_yv,
                            (f"yva_{ti}_{s_}" if nn == 0 else f"yv_{ti}_{s_}"),
                            None, False))
                    steps.append(("act", lambda e: e.activation(
                        qtmp2[:], yv_sb[:], AF.Tanh, scale=0.5),
                        [f"yv_{ti}_{s_}"], f"th_{ti}_{s_}", None, False))
                    steps.append(("dve", lambda e: e.tensor_scalar(
                        gbuf[:], qtmp2[:], 0.5, 0.5, ALU.mult, ALU.add),
                        [f"th_{ti}_{s_}"], f"sig_{ti}_{s_}", None, False))
                    for nn in range(2):
                        if nn == 0:
                            w_wo = [f"aoev_{ti}_{HP-1}", "ld_wo",
                                    f"yv_{ti}_{s_}"]
                        else:
                            w_wo = [f"u_{ti}_{s_}_0"]
                        for hp in range(HP):
                            steps.append(("pe", lambda e, hp=hp, nn=nn, s_=s_,
                                          ao_=ao_: e.matmul(
                                PS4[:, :],
                                ao_[:, hp * 512 + s_ * 128:
                                    hp * 512 + (s_ + 1) * 128],
                                wo_sb[:, hp * DM + nn * 512:
                                      hp * DM + (nn + 1) * 512],
                                start=(hp == 0), stop=(hp == HP - 1)),
                                (w_wo if hp == 0 else []),
                                (f"wo{'ab'[nn]}_{ti}_{s_}" if hp == HP - 1
                                 else None), None, True))
                        steps.append(("dve", lambda e, nn=nn: e.tensor_tensor(
                            qtmp[:, :], PS4[:, :],
                            gbuf[:, nn * 512:(nn + 1) * 512], ALU.mult),
                            [f"wo{'ab'[nn]}_{ti}_{s_}", f"sig_{ti}_{s_}"],
                            f"u_{ti}_{s_}_{nn}", None, False))
                        steps.append(("dve", lambda e, nn=nn: e.tensor_tensor(
                            gbuf[:, nn * 512:(nn + 1) * 512], qtmp[:, :],
                            yv_sb[:, nn * 512:(nn + 1) * 512], ALU.mult),
                            [], (f"sg_{ti}_{s_}" if nn == 0
                                 else f"sgb_{ti}_{s_}"), None, False))
                    steps.append(("dve", lambda e, xsb=xsb: e.tensor_tensor(
                        yv_sb[:], gbuf[:], xsb[:], ALU.add),
                        [f"sgb_{ti}_{s_}"], f"res_{ti}_{s_}", None, False))
                    steps.append(("sp", lambda e, tt=tt: e.dma_start(
                        out_hbm[tt * 128:(tt + 1) * 128, :], yv_sb[:]),
                        [f"res_{ti}_{s_}"], f"sto_{ti}_{s_}", "dSo", False))
                steps.append(("pe", lambda e: e.nop(),
                              [f"wob_{ti}_3"], f"wo_{ti}_3", None, True))
                return steps

            for ti in range(NT2):
                S.add("sp", lambda e, ti=ti: e.dma_start(
                    qT_t[:],
                    qT_hbm[:, :].rearrange("p (hp n) -> p hp n", n=seq)[
                        :, :, ti * 512:(ti + 1) * 512]),
                    waits=([f"st_{ti-1}_{HP-1}_{NCH-1}"] if ti >= 1 else []),
                    dma="dQ", ms=f"ldq_{ti}")
                pend_sub = sub_steps(ti - 1) if ti >= 1 else []
                for hp in range(HP):
                    pp = f"{ti}_{hp}"
                    if hp >= 1:
                        pend = epi_steps(ti, hp - 1)
                    elif ti >= 1:
                        pend = epi_steps(ti - 1, HP - 1)
                    else:
                        pend = []
                    for c in range(NCH):
                        buf = PA if c % 2 == 0 else PB
                        w_st = [f"ldq_{ti}"] if c == 0 else []
                        if c >= 2:
                            w_st.append(f"exp_{pp}_{c-2}")
                        elif hp >= 1:
                            w_st.append(f"exp_{ti}_{hp-1}_{NCH-2+c}")
                        elif ti >= 1:
                            w_st.append(f"exp_{ti-1}_{HP-1}_{NCH-2+c}")
                        elif c == 1:
                            w_st.append("bbc_done")
                        for par in range(2):
                            b0 = par * 64
                            S.add("pe", lambda e, c=c, hp=hp, b0=b0, par=par,
                                  buf=buf, ti=ti: e.matmul(
                                buf[:, par * 512:(par + 1) * 512],
                                kt_sb[b0:b0 + 64,
                                      hp * seq + c * 128: hp * seq + (c + 1) * 128],
                                qT_t[b0:b0 + 64, hp * 512:(hp + 1) * 512],
                                start=True, stop=True, tile_position=(b0, 0)),
                                waits=(w_st if par == 0 else []),
                                ms=(f"st_{pp}_{c}" if par == 1 else None))
                        for _ in range(2):
                            if pend and c >= 1:
                                st_ = pend.pop(0)
                                S.add(st_[0], st_[1], waits=st_[2], ms=st_[3],
                                      dma=st_[4], fence=st_[5])
                        for _ in range(1 if pend else 2):
                            if pend_sub and c >= 1:
                                st_ = pend_sub.pop(0)
                                S.add(st_[0], st_[1], waits=st_[2], ms=st_[3],
                                      dma=st_[4], fence=st_[5])
                        if c >= 1:
                            cc = c - 1
                            for par, pv in ((0, PVa), (1, PVb)):
                                w_pv = [f"exp_{pp}_{cc}"]
                                if ti == 0 and hp == 0 and cc == NCH // 2:
                                    w_pv.append("ld_v1")
                                if cc == 0:
                                    if hp >= 1:
                                        w_pv.append(f"pvev_{ti}_{hp-1}")
                                    elif ti >= 1:
                                        w_pv.append(f"pvev_{ti-1}_{HP-1}")
                                    else:
                                        w_pv.append("ld_v0")
                                        w_pv.append("wfc_7")
                                for sub in range(4):
                                    S.add("pe", lambda e, cc=cc, hp=hp, par=par,
                                          pv=pv, sub=sub: e.matmul(
                                        pv[:, sub * 65: sub * 65 + 65],
                                        ET[:, (cc % 2) * 1024 + par * 512 +
                                           sub * 128:
                                           (cc % 2) * 1024 + par * 512 +
                                           (sub + 1) * 128],
                                        v_sb[:, cc * VROW + (2 * hp + par) * V65:
                                             cc * VROW + (2 * hp + par) * V65 + V65],
                                        start=(cc == 0 and sub == 0),
                                        stop=(cc == NCH - 1 and sub == 3)),
                                        waits=(w_pv if par == 0 and sub == 0
                                               else []),
                                        ms=(f"pv_{pp}_{cc}"
                                            if par == 1 and sub == 3 else None))
                        w_exp = [f"st_{pp}_{c}"]
                        if c >= 2:
                            w_exp.append(f"pv_{pp}_{c-2}")
                        elif hp >= 1:
                            w_exp.append(f"pv_{ti}_{hp-1}_{NCH-2+c}")
                        elif ti >= 1:
                            w_exp.append(f"pv_{ti-1}_{HP-1}_{NCH-2+c}")
                        if ti == 0 and hp == 0 and c == 0:
                            w_exp.append("negc_done")
                        S.add("act", lambda e, c=c, buf=buf: e.activation(
                            ET[:, (c % 2) * 1024:(c % 2) * 1024 + 1024],
                            buf[:, :], AF.Exp, bias=negc_bc[:, 0:1]),
                            waits=w_exp, ms=f"exp_{pp}_{c}")
                    cc = NCH - 1
                    for par, pv in ((0, PVa), (1, PVb)):
                        for sub in range(4):
                            S.add("pe", lambda e, cc=cc, hp=hp, par=par,
                                  pv=pv, sub=sub: e.matmul(
                                pv[:, sub * 65: sub * 65 + 65],
                                ET[:, (cc % 2) * 1024 + par * 512 + sub * 128:
                                   (cc % 2) * 1024 + par * 512 +
                                   (sub + 1) * 128],
                                v_sb[:, cc * VROW + (2 * hp + par) * V65:
                                     cc * VROW + (2 * hp + par) * V65 + V65],
                                start=False, stop=(sub == 3)),
                                waits=([f"exp_{pp}_{cc}"]
                                       if par == 0 and sub == 0 else []),
                                ms=(f"pv_{pp}_{cc}"
                                    if par == 1 and sub == 3 else None))
                    for st_ in pend:
                        S.add(st_[0], st_[1], waits=st_[2], ms=st_[3],
                              dma=st_[4], fence=st_[5])
                for st_ in pend_sub:
                    S.add(st_[0], st_[1], waits=st_[2], ms=st_[3],
                          dma=st_[4], fence=st_[5])
                if ti == NT2 - 1:
                    for st_ in epi_steps(ti, HP - 1):
                        S.add(st_[0], st_[1], waits=st_[2], ms=st_[3],
                              dma=st_[4], fence=st_[5])
                    for st_ in sub_steps(ti):
                        S.add(st_[0], st_[1], waits=st_[2], ms=st_[3],
                              dma=st_[4], fence=st_[5])
                S.add("pe", lambda e: e.nop(),
                    waits=[f"exp_{ti}_{HP-1}_{NCH-1}"], ms=f"atn_{ti}",
                    fence=True)

            total_dSo = S.dma_counts["dSo"]
            S.add("act", lambda e: e.wait_ge(sems["dSo"], total_dSo), raw=True)
            S.add("sp", lambda e: e.wait_ge(sems["dSo"], total_dSo), raw=True)

            _lim = os.environ.get("KDBG_P2LIMIT")
            _lim = int(_lim) if _lim else None
            with nc.Block() as blockB:
                blockB.tensor(S.replay("pe", sems, _lim))
                blockB.scalar(S.replay("act", sems, _lim))
                blockB.vector(S.replay("dve", sems, _lim))
                blockB.gpsimd(S.replay("pool", sems, _lim))
                blockB.sync(S.replay("sp", sems, _lim))
            print(f"[kernel] phase2 ops: {len(S.ops)} (limit={_lim})", flush=True)

    return nc


_CACHE = {}


def _get_program(seq):
    if seq not in _CACHE:
        _CACHE[seq] = build_program(seq)
    return _CACHE[seq]


def kernel(L0_attn_x, L0_attn_gamma, L0_attn_Wq, L0_attn_Wkv, L0_attn_W_bilinear,
           L0_attn_Wo, L0_ffn_weight_fp8, L0_ffn_bias, L0_ffn_input_scale,
           L0_ffn_weight_scale):
    x = np.ascontiguousarray(np.asarray(L0_attn_x, dtype=np.float32))
    B, seq, dm = x.shape
    assert dm == DM
    nc = _get_program(seq)

    ct, st2 = host_tables(seq)
    wb = np.asarray(L0_attn_W_bilinear, dtype=np.float32)
    wb_packed = np.zeros((128, HP * DH), dtype=np.float32)
    for h in range(H):
        wb_packed[(h % 2) * 64:(h % 2) * 64 + 64,
                  (h // 2) * DH:(h // 2) * DH + DH] = wb[h]
    idn = np.eye(128, dtype=np.float32)

    common = {
        "gamma": np.ascontiguousarray(
            np.asarray(L0_attn_gamma, dtype=np.float32).reshape(1, DM)),
        "wq": np.ascontiguousarray(np.asarray(L0_attn_Wq, dtype=np.float32)),
        "wkv": np.ascontiguousarray(np.asarray(L0_attn_Wkv, dtype=np.float32)),
        "wb": wb_packed,
        "wo": np.ascontiguousarray(np.asarray(L0_attn_Wo, dtype=np.float32)),
        "wf": np.ascontiguousarray(np.asarray(L0_ffn_weight_fp8, dtype=np.float32)),
        "bias": np.ascontiguousarray(
            np.asarray(L0_ffn_bias, dtype=np.float32).reshape(1, DM)),
        "is_": np.asarray(L0_ffn_input_scale, dtype=np.float32).reshape(1, 1),
        "ws": np.asarray(L0_ffn_weight_scale, dtype=np.float32).reshape(1, 1),
        "ct": ct, "st": st2, "idn": idn,
    }
    in_maps = []
    for c in range(B):
        m = dict(common)
        m["x"] = np.ascontiguousarray(x[c])
        in_maps.append(m)

    res = run_bass_kernel_spmd(nc, in_maps, core_ids=list(range(B)))
    out = np.stack([np.asarray(r["out"], dtype=np.float32)
                    for r in res.results], axis=0)
    return out


# version 27
# speedup vs baseline: 1.0083x; 1.0083x over previous
import contextlib
import os
import numpy as np
import ml_dtypes
import concourse.bass as bass
import concourse.mybir as mybir
from concourse.bass_utils import run_bass_kernel_spmd

F32 = mybir.dt.float32
BF16 = mybir.dt.bfloat16
FP8 = mybir.dt.float8e4
AF = mybir.ActivationFunctionType
ALU = mybir.AluOpType
AX = mybir.AxisListType

DM = 1024
H = 16
DH = 64
HP = H // 2
LN_EPS = 1e-5
ROPE_BASE = 10000.0
V65 = DH + 1
VROW = H * V65

ENGINES = ("pe", "act", "dve", "pool", "sp")


class Sched:

    def __init__(self):
        self.ops = []
        self.counts = {e: 0 for e in ENGINES}
        self.dma_counts = {}
        self.ms = {}
        self.base = {}

    SELF_SYNC = ("act", "dve", "pool")

    def add(self, eng, emit, waits=(), ms=None, dma=None, raw=False,
            fence=False):
        waits = tuple(w for w in waits if w is not None)
        for w in waits:
            assert w in self.ms, f"wait on undefined milestone {w}"
        if raw:
            self.ops.append((eng, emit, waits, None, None, True, None))
            return
        if dma is None:
            self.counts[eng] += 1
            if ms is not None:
                self.ms[ms] = (eng, self.counts[eng])
        else:
            self.dma_counts[dma] = self.dma_counts.get(dma, 0) + 16
            if ms is not None:
                self.ms[ms] = (dma, self.dma_counts[dma])
        fence_val = self.counts.get(eng, 0) - (1 if dma is None else 0) \
            if (fence and ms is not None) else None
        self.ops.append((eng, emit, waits, ms, dma, False, fence_val))

    def replay(self, eng_name, sems, limit=None):
        src = self.ops if limit is None else self.ops[:limit]
        ops = [o for o in src if o[0] == eng_name]
        self_sync = eng_name in self.SELF_SYNC

        def f(e):
            waited = {}
            n_done = self.base.get(eng_name, 0)
            for (_, emit, waits, ms, dma, raw, fence_val) in ops:
                for w in waits:
                    sk, val = self.ms[w]
                    if sk == eng_name and self_sync:
                        continue
                    if waited.get(sk, -1) >= val:
                        continue
                    e.wait_ge(sems[sk], val)
                    waited[sk] = val
                if not raw and self_sync and n_done > 0:
                    e.wait_ge(sems[eng_name], n_done)
                elif fence_val is not None and fence_val > 0 and not self_sync:
                    e.wait_ge(sems[eng_name], fence_val)
                ins = emit(e)
                if raw:
                    continue
                if dma is not None:
                    ins.then_inc(sems[dma], 16)
                else:
                    ins.then_inc(sems[eng_name], 1)
                    n_done += 1
        return f


def host_tables(seq):
    theta = 1.0 / ROPE_BASE ** (np.arange(0, DH, 2, dtype=np.float64) / DH)
    pos = np.arange(seq, dtype=np.float64)
    fr = pos[:, None] * theta[None, :]
    emb = np.concatenate([fr, fr], axis=-1)
    cos = np.cos(emb)
    sin = np.sin(emb)
    ct = cos.astype(np.float32)
    st2 = np.empty_like(ct)
    st2[:, 0::2] = sin[:, 1::2].astype(np.float32)
    st2[:, 1::2] = (-sin[:, 0::2]).astype(np.float32)
    return ct, st2


def build_program(seq):
    NT = seq // 128
    NT2 = seq // 512
    NCH = seq // 128
    NSL = seq // 512

    nc = bass.Bass("TRN2", target_bir_lowering=False)

    x_hbm = nc.dram_tensor("x", [seq, DM], F32, kind="ExternalInput")
    gamma_h = nc.dram_tensor("gamma", [1, DM], F32, kind="ExternalInput")
    wq_h = nc.dram_tensor("wq", [DM, DM], F32, kind="ExternalInput")
    wkv_h = nc.dram_tensor("wkv", [DM, 2 * DM], F32, kind="ExternalInput")
    wb_h = nc.dram_tensor("wb", [128, HP * DH], F32, kind="ExternalInput")
    wo_h = nc.dram_tensor("wo", [DM, DM], F32, kind="ExternalInput")
    wf_h = nc.dram_tensor("wf", [DM, DM], F32, kind="ExternalInput")
    bias_h = nc.dram_tensor("bias", [1, DM], F32, kind="ExternalInput")
    is_h = nc.dram_tensor("is_", [1, 1], F32, kind="ExternalInput")
    ws_h = nc.dram_tensor("ws", [1, 1], F32, kind="ExternalInput")
    ct_h = nc.dram_tensor("ct", [seq, DH], F32, kind="ExternalInput")
    st_h = nc.dram_tensor("st", [seq, DH], F32, kind="ExternalInput")
    id_h = nc.dram_tensor("idn", [128, 128], F32, kind="ExternalInput")
    out_hbm = nc.dram_tensor("out", [seq, DM], F32, kind="ExternalOutput")

    qT_hbm = nc.dram_tensor("qT_s", [128, HP * seq], BF16, kind="Internal")
    v_hbm = nc.dram_tensor("v_s", [128, NT * VROW], BF16, kind="Internal")

    with contextlib.ExitStack() as stack:
        id_f32 = stack.enter_context(nc.sbuf_tensor("id_f32", [128, 128], F32))
        id_bf = stack.enter_context(nc.sbuf_tensor("id_bf", [128, 128], BF16))
        ones_r = stack.enter_context(nc.sbuf_tensor("ones_r", [1, 128], F32))
        rstd1_res = stack.enter_context(nc.sbuf_tensor("rstd1", [128, NT], F32))
        nmr1_res = stack.enter_context(nc.sbuf_tensor("nmr1", [128, NT], F32))
        qmax_run = stack.enter_context(nc.sbuf_tensor("qmaxr", [128, H], F32))
        misc = stack.enter_context(nc.sbuf_tensor("misc", [128, 16], F32))
        kt_sb = stack.enter_context(nc.sbuf_tensor("kt_sb", [128, HP * seq], BF16))
        (MC_SUM, MC_SSQ, MC_M, MC_T1, MC_T2, MC_T3, MC_R2, MC_N2,
         MC_ISWS, MC_INVIS, MC_EPS) = range(11)

        sems = {}
        for name in ("pe", "act", "dve", "pool", "sp", "bar", "dP", "dP0", "dG",
                     "dX0", "dX1", "dX2", "dSq0", "dSq1", "dSk0", "dSk1", "dSv0", "dSv1", "dP2", "dW0", "dW1",
                     "dQ", "dSo"):
            sems[name] = stack.enter_context(nc.semaphore("sem_" + name))

        with contextlib.ExitStack() as p1:
            wq_sb = p1.enter_context(nc.sbuf_tensor("wq_sb", [128, 8 * DM], BF16))
            wkv_sb = p1.enter_context(nc.sbuf_tensor("wkv_sb", [128, 16 * DM], BF16))
            wb_sb = p1.enter_context(nc.sbuf_tensor("wb_sb", [128, HP * DH], BF16))
            ct_sb = p1.enter_context(nc.sbuf_tensor("ct_sb", [128, NT * DH], F32))
            st_sb = p1.enter_context(nc.sbuf_tensor("st_sb", [128, NT * DH], F32))
            gamma_bc = p1.enter_context(nc.sbuf_tensor("gamma_bc", [128, DM], F32))
            grow = p1.enter_context(nc.sbuf_tensor("grow", [1, DM], F32))
            xbl = [p1.enter_context(nc.sbuf_tensor(f"xb{i}", [128, DM], F32)) for i in range(3)]
            xnn_sb = [p1.enter_context(nc.sbuf_tensor(f"xnn_sb{i}", [128, DM], F32)) for i in range(2)]
            sqdump = p1.enter_context(nc.sbuf_tensor("sqdump", [128, DM], BF16))
            xn2_sb = [p1.enter_context(nc.sbuf_tensor(f"xn2_sb{i}", [128, DM], BF16)) for i in range(2)]
            xn2T = [p1.enter_context(nc.sbuf_tensor(f"xn2T{i}", [128, DM], BF16)) for i in range(2)]
            ropeA = [p1.enter_context(nc.sbuf_tensor(f"ropeA{i}", [128, DM], BF16)) for i in range(2)]
            ropeB = [p1.enter_context(nc.sbuf_tensor(f"ropeB{i}", [128, DM], BF16)) for i in range(2)]
            qr_sb = [p1.enter_context(nc.sbuf_tensor(f"qr_sb{i}", [128, DM], BF16)) for i in range(2)]
            kr_sb = [p1.enter_context(nc.sbuf_tensor(f"kr_sb{i}", [128, DM], BF16)) for i in range(2)]
            qTs = [p1.enter_context(nc.sbuf_tensor(f"qTs{i}", [128, DM], BF16)) for i in range(2)]
            kTs = [p1.enter_context(nc.sbuf_tensor(f"kTs{i}", [128, DM], BF16)) for i in range(2)]
            vst = [p1.enter_context(nc.sbuf_tensor(f"vst{i}", [128, VROW], BF16)) for i in range(2)]
            qn_sb = p1.enter_context(nc.sbuf_tensor("qn_sb", [128, H], F32))
            stat = [p1.enter_context(nc.sbuf_tensor(f"stat{i}", [128, 16], F32)) for i in range(2)]

            P0 = p1.enter_context(nc.psum_tensor("P0", [128, 1024], F32))
            P1 = p1.enter_context(nc.psum_tensor("P1", [128, 1024], F32))
            P2 = p1.enter_context(nc.psum_tensor("P2", [128, 512], F32))
            P3 = p1.enter_context(nc.psum_tensor("P3", [128, 512], F32))
            PT = p1.enter_context(nc.psum_tensor("PT", [128, 1024], BF16))
            PTB = p1.enter_context(nc.psum_tensor("PTB", [128, 1024], BF16))

            S = Sched()

            S.add("dve", lambda e: e.memset(misc[0:1, 14:15], 0.0),
                ms="scr_init")
            S.add("act", lambda e: e.activation(
                misc[0:1, 15:16], misc[0:1, 14:15], AF.Square),
                waits=["scr_init"])
            S.add("pool", lambda e: e.dma_start(
                wq_sb[:, :].rearrange("p (fc o) -> p fc o", o=DM),
                wq_h[:, :].rearrange("(fc p) o -> p fc o", p=128)),
                dma="dG", ms="ld_wq_i")
            S.add("pool", lambda e: e.dma_start(
                wkv_sb[:, :].rearrange("p (fc o) -> p fc o", o=2 * DM),
                wkv_h[:, :].rearrange("(fc p) o -> p fc o", p=128)),
                dma="dG", ms="ld_wkv_i")
            S.add("pool", lambda e: e.dma_start(wb_sb[:], wb_h[:, :]),
                dma="dG", ms="wload")
            S.add("sp", lambda e: e.dma_start(id_f32[:], id_h[:, :]),
                dma="dP0", ms="ld_id_i")
            S.add("sp", lambda e: e.dma_start(grow[:], gamma_h[:, :]),
                dma="dP0", ms="p1_pre")
            S.add("sp", lambda e: e.dma_start(
                ct_sb[:, :].rearrange("p (t d) -> p t d", d=DH),
                ct_h[:, :].rearrange("(t p) d -> p t d", p=128)),
                dma="dP", ms="ld_ct_i")
            S.add("sp", lambda e: e.dma_start(
                st_sb[:, :].rearrange("p (t d) -> p t d", d=DH),
                st_h[:, :].rearrange("(t p) d -> p t d", p=128)),
                dma="dP", ms="tab_ld")
            S.add("dve", lambda e: e.memset(ones_r[:], 1.0), ms="ones_done")
            S.add("dve", lambda e: e.memset(
                misc[:, MC_EPS:MC_EPS + 1], LN_EPS), ms="eps_done")
            S.add("dve", lambda e: e.memset(vst[0][:], 1.0))
            S.add("dve", lambda e: e.memset(vst[1][:], 1.0), ms="vst_ones")
            S.add("dve", lambda e: e.memset(qmax_run[:], 0.0), ms="qmax_init")
            S.add("act", lambda e: e.activation(id_bf[:], id_f32[:], AF.Copy),
                waits=["p1_pre"], ms="idbf_done")
            S.add("dve", lambda e: e.tensor_scalar(
                wq_sb[:], wq_sb[:], 0.125, None, ALU.mult),
                waits=["wload"], ms="wq_scaled")
            for nn in range(2):
                S.add("pe", lambda e, nn=nn: e.matmul(
                    P0[:, nn * 512:(nn + 1) * 512], ones_r[:, :],
                    grow[0:1, nn * 512:(nn + 1) * 512], start=True, stop=True),
                    waits=["ones_done", "p1_pre"],
                    ms=("gbc_mm" if nn == 1 else None))
            S.add("dve", lambda e: e.tensor_copy(gamma_bc[:], P0[:, :]),
                waits=["gbc_mm"], ms="gbc_done")

            def stageAload(t):
                xb = xbl[t % 3]
                S.add("sp", lambda e, t=t, xb=xb: e.dma_start(
                    xb[:], x_hbm[t * 128:(t + 1) * 128, :]),
                    waits=([f"xnn_{t-3}", f"rs1_{t-3}"] if t >= 3 else []),
                    dma=("dX0", "dX1", "dX2")[t % 3], ms=f"ldx_{t}")

            def stageA(t):
                p = t % 2
                xb = xbl[t % 3]
                ST = stat[p]
                xnn_ = xnn_sb[p]; xn2_ = xn2_sb[p]
                S.add("dve", lambda e, xb=xb, ST=ST: e.reduce_sum(
                    ST[:, MC_SUM:MC_SUM + 1], xb[:], AX.X),
                    waits=[f"ldx_{t}"], ms=f"rs1_{t}")
                S.add("act", lambda e, xb=xb, ST=ST: e.activation(
                    sqdump[:], xb[:], AF.Square,
                    accum_out=ST[:, MC_SSQ:MC_SSQ + 1]),
                    waits=[f"ldx_{t}"] + ([f"mic_{t-2}"] if t >= 2 else []),
                    ms=f"sq1_{t}")
                S.add("dve", lambda e, ST=ST: e.tensor_scalar(
                    ST[:, MC_M:MC_M + 1], ST[:, MC_SUM:MC_SUM + 1],
                    1.0 / DM, None, ALU.mult), waits=[f"sq1_{t}"])
                S.add("dve", lambda e, ST=ST: e.tensor_tensor(
                    ST[:, MC_T1:MC_T1 + 1], ST[:, MC_M:MC_M + 1],
                    ST[:, MC_M:MC_M + 1], ALU.mult))
                S.add("dve", lambda e, ST=ST: e.tensor_scalar(
                    ST[:, MC_T2:MC_T2 + 1], ST[:, MC_SSQ:MC_SSQ + 1],
                    1.0 / DM, None, ALU.mult))
                S.add("dve", lambda e, ST=ST: e.tensor_tensor(
                    ST[:, MC_T2:MC_T2 + 1], ST[:, MC_T2:MC_T2 + 1],
                    ST[:, MC_T1:MC_T1 + 1], ALU.subtract), ms=f"var1_{t}")
                S.add("act", lambda e, ST=ST: e.activation(
                    ST[:, MC_T3:MC_T3 + 1], ST[:, MC_T2:MC_T2 + 1],
                    AF.Sqrt, bias=misc[:, MC_EPS:MC_EPS + 1]),
                    waits=[f"var1_{t}", "eps_done"], ms=f"sd1_{t}")
                S.add("dve", lambda e, t=t, ST=ST: e.reciprocal(
                    rstd1_res[:, t:t + 1], ST[:, MC_T3:MC_T3 + 1]),
                    waits=[f"sd1_{t}"])
                S.add("dve", lambda e, t=t, ST=ST: e.tensor_scalar(
                    nmr1_res[:, t:t + 1], ST[:, MC_M:MC_M + 1],
                    rstd1_res[:, t:t + 1], -1.0, ALU.mult, ALU.mult))
                S.add("dve", lambda e, ST=ST: e.tensor_tensor(
                    ST[:, MC_T1:MC_T1 + 1], ST[:, MC_T3:MC_T3 + 1],
                    ST[:, MC_T3:MC_T3 + 1], ALU.mult))
                S.add("dve", lambda e, ST=ST: e.reciprocal(
                    ST[:, MC_T1:MC_T1 + 1], ST[:, MC_T1:MC_T1 + 1]))
                S.add("dve", lambda e, ST=ST: e.tensor_scalar(
                    ST[:, MC_T1:MC_T1 + 1], ST[:, MC_T2:MC_T2 + 1],
                    ST[:, MC_T1:MC_T1 + 1], None, ALU.mult),
                    ms=f"v2_{t}")
                S.add("act", lambda e, ST=ST: e.activation(
                    ST[:, MC_T1:MC_T1 + 1], ST[:, MC_T1:MC_T1 + 1],
                    AF.Sqrt, bias=misc[:, MC_EPS:MC_EPS + 1]),
                    waits=[f"v2_{t}"], ms=f"sd2_{t}")
                S.add("dve", lambda e, ST=ST: e.reciprocal(
                    ST[:, MC_T1:MC_T1 + 1], ST[:, MC_T1:MC_T1 + 1]),
                    waits=[f"sd2_{t}"])
                S.add("dve", lambda e, t=t, ST=ST: e.tensor_scalar(
                    ST[:, MC_R2:MC_R2 + 1], rstd1_res[:, t:t + 1],
                    ST[:, MC_T1:MC_T1 + 1], None, ALU.mult))
                S.add("dve", lambda e, ST=ST: e.tensor_scalar(
                    ST[:, MC_N2:MC_N2 + 1], ST[:, MC_M:MC_M + 1],
                    ST[:, MC_R2:MC_R2 + 1], -1.0, ALU.mult, ALU.mult),
                    ms=f"mic_{t}")
                S.add("act", lambda e, xb=xb, xnn_=xnn_, ST=ST: e.activation(
                    xnn_[:], xb[:], AF.Identity,
                    bias=ST[:, MC_N2:MC_N2 + 1],
                    scale=ST[:, MC_R2:MC_R2 + 1]),
                    waits=[f"mic_{t}"] + ([f"xn2_{t-2}"] if t >= 2 else []),
                    ms=f"xnn_{t}")
                S.add("pool", lambda e, xnn_=xnn_, xn2_=xn2_: e.tensor_tensor(
                    xn2_[:], xnn_[:], gamma_bc[:], ALU.mult),
                    waits=[f"xnn_{t}", "gbc_done"] +
                          ([f"tr1_{t-2}"] if t >= 2 else []),
                    ms=f"xn2_{t}")

            def stageB(t, part):
                p = t % 2
                x2T = xn2T[p]; rA = ropeA[p]; rB = ropeB[p]
                qr_ = qr_sb[p]; kr_ = kr_sb[p]
                qT_ = qTs[p]; kT_ = kTs[p]; vs_ = vst[p]
                xn2_ = xn2_sb[p]
                ctv = ct_sb[:, t * DH:(t + 1) * DH].rearrange(
                    "p (o d) -> p o d", o=1).to_broadcast([128, H, DH])
                stv = st_sb[:, t * DH:(t + 1) * DH].rearrange(
                    "p (o d) -> p o d", o=1).to_broadcast([128, H, DH])

                def rope(src_ps, pev, ev_ms, dst, mm_ms, out_ms, guard):
                    S.add("dve", lambda e, src_ps=src_ps, ctv=ctv,
                          rA=rA: e.tensor_tensor(
                        rA[:, :].rearrange("p (h d) -> p h d", d=DH),
                        src_ps[:, :].rearrange("p (h d) -> p h d", d=DH),
                        ctv, ALU.mult),
                        waits=[mm_ms, "p1_pre", "tab_ld"] + guard)
                    S.add("dve", lambda e, src_ps=src_ps, stv=stv,
                          rB=rB: e.tensor_tensor(
                        rB[:, :].rearrange("p (h d) -> p h d", d=DH),
                        src_ps[:, :].rearrange("p (h d) -> p h d", d=DH),
                        stv, ALU.mult),
                        ms=ev_ms)
                    ra = rA[:, :].rearrange("p (n two) -> p n two", two=2)
                    rb = rB[:, :].rearrange("p (n two) -> p n two", two=2)
                    dd = dst[:, :].rearrange("p (n two) -> p n two", two=2)
                    S.add("dve", lambda e, ra=ra, rb=rb, dd=dd: e.tensor_tensor(
                        dd[:, :, 0:1], ra[:, :, 0:1], rb[:, :, 1:2], ALU.add))
                    S.add("dve", lambda e, ra=ra, rb=rb, dd=dd: e.tensor_tensor(
                        dd[:, :, 1:2], ra[:, :, 1:2], rb[:, :, 0:1], ALU.add),
                        ms=out_ms)

                if part == 1:
                    for b in range(8):
                        S.add("pe", lambda e, b=b, xn2_=xn2_: e.transpose(
                            PTB[:, b * 128:(b + 1) * 128],
                            xn2_[:, b * 128:(b + 1) * 128], id_bf[:]),
                            waits=(["idbf_done", f"xn2_{t}"] +
                                   ([f"ev2_{t-1}"] if b == 0 and t >= 1 else [])),
                            ms=(f"tr1_{t}" if b == 7 else None))
                    S.add("act", lambda e, x2T=x2T: e.activation(
                        x2T[:], PTB[:, :], AF.Copy),
                        waits=[f"tr1_{t}"] + ([f"vmm_{t-2}"] if t >= 2 else []),
                        ms=f"ev1_{t}")
                    return
                if part == 12:
                    for fc in range(8):
                        for nn in range(2):
                            S.add("pe", lambda e, fc=fc, nn=nn, x2T=x2T: e.matmul(
                                P0[:, nn * 512:(nn + 1) * 512],
                                x2T[:, fc * 128:(fc + 1) * 128],
                                wq_sb[:, fc * DM + nn * 512: fc * DM + (nn + 1) * 512],
                                start=(fc == 0), stop=(fc == 7)),
                                waits=([f"ev1_{t}", "wq_scaled"] +
                                       ([f"evq_{t-1}"] if t >= 1 else []) +
                                       (["gbc_done"] if t == 0 else [])
                                       if fc == 0 and nn == 0 else []),
                                ms=(f"qmm_{t}" if fc == 7 and nn == 1 else None))
                    for fc in range(8):
                        for nn in range(2):
                            S.add("pe", lambda e, fc=fc, nn=nn, x2T=x2T: e.matmul(
                                P1[:, nn * 512:(nn + 1) * 512],
                                x2T[:, fc * 128:(fc + 1) * 128],
                                wkv_sb[:, fc * 2 * DM + nn * 512:
                                       fc * 2 * DM + (nn + 1) * 512],
                                start=(fc == 0), stop=(fc == 7)),
                                waits=((["wload"] +
                                        ([f"evk_{t-1}"] if t >= 1 else []))
                                       if fc == 0 and nn == 0 else []),
                                ms=(f"kmm_{t}" if fc == 7 and nn == 1 else None))
                    rope(P0, None, f"evq_{t}", qr_, f"qmm_{t}", f"ropq_{t}",
                         [f"tr2_{t-2}"] if t >= 2 else [])
                    S.add("dve", lambda e, qr_=qr_, rA=rA: e.tensor_tensor(
                        rA[:], qr_[:], qr_[:], ALU.mult),
                        waits=[f"ropq_{t}"])
                    S.add("dve", lambda e, rA=rA: e.tensor_reduce(
                        qn_sb[:], rA[:, :].rearrange("p (h d) -> p h d", d=DH),
                        AX.X, ALU.add))
                    S.add("dve", lambda e: e.tensor_tensor(
                        qmax_run[:], qmax_run[:], qn_sb[:], ALU.max),
                        waits=["qmax_init"], ms=f"qn_{t}")
                    return
                if part == 15:
                    for b in range(8):
                        S.add("pe", lambda e, b=b, qr_=qr_: e.transpose(
                            PT[:, b * 128:(b + 1) * 128],
                            qr_[:, b * 128:(b + 1) * 128], id_bf[:]),
                            waits=([f"ropq_{t}"] +
                                   ([f"ev3_{t-1}"] if t >= 1 else [])
                                   if b == 0 else []),
                            ms=(f"tr2_{t}" if b == 7 else None))
                    S.add("act", lambda e, qT_=qT_: e.activation(
                        qT_[:], PT[:, :], AF.Copy),
                        waits=[f"tr2_{t}"] + ([f"stq_{t-2}"] if t >= 2 else []),
                        ms=f"ev2_{t}")
                    S.add("sp", lambda e, t=t, qT_=qT_: e.dma_start(
                        qT_hbm[:, :].rearrange("p (hp n) -> p hp n", n=seq)[
                            :, :, t * 128:(t + 1) * 128],
                        qT_[:, :].rearrange("p (hp n) -> p hp n", n=128)),
                        waits=[f"ev2_{t}"], dma=("dSq0" if t % 2 == 0 else "dSq1"), ms=f"stq_{t}")
                    rope(P1, None, f"evk_{t}", kr_, f"kmm_{t}", f"ropk_{t}",
                         [f"tr3_{t-2}"] if t >= 2 else [])
                    return
                if part == 2:
                    for fc in range(8):
                        for ps, nn in ((P2, 2), (P3, 3)):
                            S.add("pe", lambda e, fc=fc, nn=nn, ps=ps, x2T=x2T: e.matmul(
                                ps[:, :],
                                x2T[:, fc * 128:(fc + 1) * 128],
                                wkv_sb[:, fc * 2 * DM + nn * 512:
                                       fc * 2 * DM + (nn + 1) * 512],
                                start=(fc == 0), stop=(fc == 7)),
                                waits=(([f"ktev_{t-1}"] if t >= 1 else [])
                                       if fc == 0 and nn == 2 else []),
                                ms=(f"vmm_{t}" if fc == 7 and nn == 3 else None))
                    for b in range(8):
                        S.add("pe", lambda e, b=b, kr_=kr_: e.transpose(
                            PT[:, b * 128:(b + 1) * 128],
                            kr_[:, b * 128:(b + 1) * 128], id_bf[:]),
                            waits=([f"ropk_{t}", f"ev2_{t}"] if b == 0 else []),
                            ms=(f"tr3_{t}" if b == 7 else None))
                    S.add("act", lambda e, kT_=kT_: e.activation(
                        kT_[:], PT[:, :], AF.Copy),
                        waits=[f"tr3_{t}"] +
                              ([f"bil_done_{t-2}"] if t >= 2 else []),
                        ms=f"ev3_{t}")
                    vout = vs_[:, :].rearrange("p (h e) -> p h e", e=V65)
                    S.add("dve", lambda e, vout=vout: e.tensor_copy(
                        vout[:, 0:8, 0:DH],
                        P2[:, :].rearrange("p (h d) -> p h d", d=DH)),
                        waits=[f"vmm_{t}", "vst_ones"] +
                              ([f"stv_{t-2}"] if t >= 2 else []))
                    S.add("dve", lambda e, vout=vout: e.tensor_copy(
                        vout[:, 8:16, 0:DH],
                        P3[:, :].rearrange("p (h d) -> p h d", d=DH)),
                        ms=f"vev_{t}")
                    S.add("sp", lambda e, t=t, vs_=vs_: e.dma_start(
                        v_hbm[:, t * VROW:(t + 1) * VROW], vs_[:]),
                        waits=[f"vev_{t}"], dma=("dSv0" if t % 2 == 0 else "dSv1"),
                        ms=f"stv_{t}")
                    return
                for hp in range(HP):
                    pbk = P2 if hp % 2 == 0 else P3
                    slot = hp // 2
                    for par in range(2):
                        b0 = par * 64
                        S.add("pe", lambda e, hp=hp, b0=b0, slot=slot,
                              pbk=pbk, kT_=kT_: e.matmul(
                            pbk[b0:b0 + 64, slot * 128:(slot + 1) * 128],
                            wb_sb[b0:b0 + 64, hp * DH:(hp + 1) * DH],
                            kT_[b0:b0 + 64, hp * 128:(hp + 1) * 128],
                            start=True, stop=True, tile_position=(b0, b0)),
                            waits=(([f"ev3_{t}", "wload", f"vev_{t}"] +
                                    ([f"ktev_{t-1}"] if t >= 1 else []))
                                   if hp == 0 and par == 0 else []),
                            ms=(f"bil_{t}" if hp == HP - 1 and par == 1
                                else None))
                S.add("pe", lambda e: e.nop(), waits=[f"bil_{t}"],
                    ms=f"bil_done_{t}", fence=True)
                kview = kt_sb[:, :].rearrange("p (hp n) -> p hp n", n=seq)[
                    :, :, t * 128:(t + 1) * 128]
                S.add("dve", lambda e, kview=kview: e.tensor_copy(
                    kview[:, 0:8:2, :],
                    P2[:, :].rearrange("p (sl c) -> p sl c", c=128)),
                    waits=[f"bil_done_{t}"])
                S.add("dve", lambda e, kview=kview: e.tensor_copy(
                    kview[:, 1:8:2, :],
                    P3[:, :].rearrange("p (sl c) -> p sl c", c=128)),
                    ms=f"ktev_{t}")

            stageAload(0)
            stageAload(1)
            stageAload(2)
            stageA(0)
            stageA(1)
            for t in range(NT):
                stageB(t, 1)
                if t >= 1:
                    stageB(t - 1, 2)
                stageB(t, 12)
                if t >= 1:
                    stageB(t - 1, 25)
                stageB(t, 15)
                if t + 3 < NT:
                    stageAload(t + 3)
                if t + 2 < NT:
                    stageA(t + 2)
            stageB(NT - 1, 2)
            stageB(NT - 1, 25)

            store_streams = ["dSq0", "dSq1", "dSk0", "dSk1", "dSv0", "dSv1"]
            store_totals = {k: S.dma_counts.get(k, 0) for k in store_streams}
            for k in store_streams:
                S.add("act", lambda e, k=k: e.wait_ge(sems[k], store_totals[k]),
                      raw=True)
            for eng in ENGINES:
                S.add(eng, lambda e: e.nop().then_inc(sems["bar"], 1), raw=True)

            _lim1 = os.environ.get("KDBG_P1LIMIT")
            _lim1 = int(_lim1) if _lim1 else None
            with nc.Block() as blockA:
                blockA.tensor(S.replay("pe", sems, _lim1))
                blockA.scalar(S.replay("act", sems, _lim1))
                blockA.vector(S.replay("dve", sems, _lim1))
                blockA.gpsimd(S.replay("pool", sems, _lim1))
                blockA.sync(S.replay("sp", sems, _lim1))
            print(f"[kernel] phase1 ops: {len(S.ops)} (limit={_lim1})", flush=True)
            ph1_store_totals = dict(store_totals)
            ph1_dma_counts = dict(S.dma_counts)
            ph1_counts = dict(S.counts)

        with contextlib.ExitStack() as p2:
            v_sb = p2.enter_context(nc.sbuf_tensor("v_sb", [128, NT * VROW], BF16))
            wo_sb = p2.enter_context(nc.sbuf_tensor("wo_sb", [128, 8 * DM], BF16))
            wft_sb = p2.enter_context(nc.sbuf_tensor("wft_sb", [128, 8 * DM], FP8))
            qT_t = p2.enter_context(nc.sbuf_tensor("qT_t", [128, HP * 512], BF16))
            ET = p2.enter_context(nc.sbuf_tensor("ET", [128, 2048], BF16))
            pvE = p2.enter_context(nc.sbuf_tensor("pvE", [128, 1024], BF16))
            aoT = [p2.enter_context(nc.sbuf_tensor(f"aoT{i}", [128, HP * 512], BF16)) for i in range(2)]
            rlv = p2.enter_context(nc.sbuf_tensor("rlv", [128, 8], F32))
            xs0 = p2.enter_context(nc.sbuf_tensor("xs0", [128, DM], F32))
            xnf = p2.enter_context(nc.sbuf_tensor("xnf", [128, DM], F32))
            qtmp = p2.enter_context(nc.sbuf_tensor("qtmp", [128, 512], F32))
            qtmp2 = xnf
            xqT = p2.enter_context(nc.sbuf_tensor("xqT", [128, DM], FP8))
            yv_sb = p2.enter_context(nc.sbuf_tensor("yv_sb", [128, DM], F32))
            gbuf = p2.enter_context(nc.sbuf_tensor("gbuf", [128, DM], BF16))
            ao_sb = p2.enter_context(nc.sbuf_tensor("ao_sb", [128, 512], BF16))
            bias_bc = p2.enter_context(nc.sbuf_tensor("bias_bc", [128, DM], F32))
            srow = p2.enter_context(nc.sbuf_tensor("srow", [1, 8], F32))
            srow2 = p2.enter_context(nc.sbuf_tensor("srow2", [2, 1], F32))
            negc_bc = p2.enter_context(nc.sbuf_tensor("negc_bc", [128, 1], F32))
            k2run = p2.enter_context(nc.sbuf_tensor("k2run", [2, 1], F32))
            sel = p2.enter_context(nc.sbuf_tensor("sel", [128, 2], BF16))

            PA = p2.enter_context(nc.psum_tensor("PA", [128, 1024], F32))
            PB = p2.enter_context(nc.psum_tensor("PB", [128, 1024], F32))
            PVa = p2.enter_context(nc.psum_tensor("PVa", [128, 512], F32))
            PVb = p2.enter_context(nc.psum_tensor("PVb", [128, 512], F32))
            PT2 = p2.enter_context(nc.psum_tensor("PT2", [128, 1024], BF16))
            PS4 = p2.enter_context(nc.psum_tensor("PS4", [128, 512], F32))

            S = Sched()
            NBAR = len(ENGINES)
            for eng in ENGINES:
                S.add(eng, lambda e: e.wait_ge(sems["bar"], NBAR), raw=True)
            for k, v in ph1_store_totals.items():
                S.add("sp", lambda e, k=k, v=v: e.wait_ge(sems[k], v), raw=True)
            S.dma_counts.update(ph1_dma_counts)
            S.counts.update(ph1_counts)
            S.base = dict(ph1_counts)

            S.add("act", lambda e: e.activation(
                misc[0:1, 15:16], misc[0:1, MC_EPS:MC_EPS + 1], AF.Sqrt))
            VHALF = NT * VROW // 2
            for sl in range(2):
                S.add("sp", lambda e, sl=sl: e.dma_start(
                    v_sb[:, sl * VHALF:(sl + 1) * VHALF],
                    v_hbm[:, sl * VHALF:(sl + 1) * VHALF]),
                    dma=("dSv0", "dSv1")[sl], ms=f"ld_v{sl}")
            S.add("sp", lambda e: e.dma_start(bias_bc[0:1, :], bias_h[:, :]),
                dma="dP2", ms="ld_bias_i")
            S.add("sp", lambda e: e.dma_start(srow[0:1, 0:1], is_h[:, :]),
                dma="dP2", ms="ld_is_i")
            S.add("sp", lambda e: e.dma_start(srow[0:1, 1:2], ws_h[:, :]),
                dma="dP2", ms="p2_pre")
            S.add("pool", lambda e: e.dma_start(
                wo_sb[:, :].rearrange("p (fc o) -> p fc o", o=DM),
                wo_h[:, :].rearrange("(fc p) o -> p fc o", p=128)),
                dma="dG", ms="ld_wo")
            for oc in range(8):
                slab = xs0 if oc % 2 == 0 else yv_sb
                S.add("sp", lambda e, oc=oc, slab=slab: e.dma_start(
                    slab[:], wf_h[oc * 128:(oc + 1) * 128, :]),
                    waits=([f"wft_{oc-2}"] if oc >= 2 else []),
                    dma=("dW0" if oc % 2 == 0 else "dW1"), ms=f"ld_wf{oc}")
                for b in range(8):
                    pvw = PVa if b < 4 else PVb
                    S.add("pe", lambda e, b=b, slab=slab, pvw=pvw: e.transpose(
                        pvw[:, (b % 4) * 128:(b % 4 + 1) * 128],
                        slab[:, b * 128:(b + 1) * 128], id_f32[:]),
                        waits=([f"ld_wf{oc}"] +
                               ([f"wfc_{oc-1}"] if oc >= 1 else [])
                               if b == 0 else []),
                        ms=(f"wft_{oc}" if b == 7 else None))
                for kk, pvw in ((0, PVa), (1, PVb)):
                    S.add("act", lambda e, oc=oc, kk=kk, pvw=pvw: e.activation(
                        wft_sb[:, :].rearrange("p (fc oc o) -> p fc oc o",
                                               oc=8, o=128)[:, kk * 4:
                                                            (kk + 1) * 4, oc, :],
                        pvw[:, :].rearrange("p (b o) -> p b o", o=128),
                        AF.Copy),
                        waits=([f"wft_{oc}"] if kk == 0 else []),
                        ms=(f"wfc_{oc}" if kk == 1 else None))
            for nn in range(2):
                S.add("pe", lambda e, nn=nn: e.matmul(
                    PB[:, nn * 512:(nn + 1) * 512], ones_r[:, :],
                    bias_bc[0:1, nn * 512:(nn + 1) * 512], start=True, stop=True),
                    waits=["p2_pre"], ms=("bbc_mm" if nn == 1 else None))
            S.add("dve", lambda e: e.tensor_copy(bias_bc[:], PB[:, :]),
                waits=["bbc_mm"], ms="bbc_done")
            S.add("dve", lambda e: e.tensor_tensor(
                srow[0:1, 2:3], srow[0:1, 0:1], srow[0:1, 1:2], ALU.mult),
                waits=["p2_pre"], ms="isws_row")
            S.add("dve", lambda e: e.reciprocal(srow[0:1, 3:4], srow[0:1, 0:1]),
                ms="invis_row")
            S.add("pe", lambda e: e.matmul(
                PS4[:, 0:1], ones_r[:, :], srow[0:1, 2:3], start=True, stop=True),
                waits=["isws_row"], ms="isws_mm")
            S.add("pe", lambda e: e.matmul(
                PS4[:, 1:2], ones_r[:, :], srow[0:1, 3:4], start=True, stop=True),
                waits=["invis_row"], ms="invis_mm")
            S.add("dve", lambda e: e.tensor_copy(
                misc[:, MC_ISWS:MC_ISWS + 1], PS4[:, 0:1]),
                waits=["invis_mm"], ms="isws_done")
            S.add("dve", lambda e: e.tensor_copy(
                misc[:, MC_INVIS:MC_INVIS + 1], PS4[:, 1:2]),
                ms="invis_done")
            S.add("dve", lambda e: e.memset(sel[:], 0.0))
            S.add("dve", lambda e: e.memset(sel[0:64, 0:1], 1.0))
            S.add("dve", lambda e: e.memset(sel[64:128, 1:2], 1.0), ms="sel_done")
            S.add("dve", lambda e: e.memset(k2run[:], 0.0), ms="k2_init")
            prev_red = None
            prev_mm = None
            HLEN = min(2048, seq)
            NJ = HLEN // 512
            NHALF = seq // HLEN
            for hp in range(HP):
                for half in range(NHALF):
                    kh = f"{hp}_{half}"
                    w_sq = []
                    if prev_mm is not None:
                        w_sq.append(prev_mm)
                    S.add("dve", lambda e, hp=hp, half=half: e.tensor_tensor(
                        ET[:, 0:HLEN],
                        kt_sb[:, hp * seq + half * HLEN:
                              hp * seq + (half + 1) * HLEN],
                        kt_sb[:, hp * seq + half * HLEN:
                              hp * seq + (half + 1) * HLEN],
                        ALU.mult),
                        waits=w_sq, ms=f"k2sq_{kh}")
                    for j in range(NJ):
                        w_mm = []
                        if j == 0:
                            w_mm = [f"k2sq_{kh}", "sel_done"]
                            if prev_red is None:
                                w_mm += ["isws_done", "invis_done"]
                            else:
                                w_mm.append(prev_red)
                        else:
                            w_mm = [f"k2r_{kh}_{j-1}"]
                        gname = f"k2mm_{kh}_{j}"
                        S.add("pe", lambda e, j=j: e.matmul(
                            PS4[0:2, 0:512],
                            sel[:, :], ET[:, j * 512:(j + 1) * 512],
                            start=True, stop=True),
                            waits=w_mm, ms=gname)
                        prev_mm = gname
                        S.add("dve", lambda e: e.tensor_reduce(
                            srow2[:, :], PS4[0:2, 0:512],
                            AX.X, ALU.max),
                            waits=[gname])
                        rname = (f"k2red_{kh}" if j == NJ - 1
                                 else f"k2r_{kh}_{j}")
                        S.add("dve", lambda e: e.tensor_tensor(
                            k2run[:], k2run[:], srow2[:, :], ALU.max),
                            waits=["k2_init"], ms=rname)
                        prev_red = rname
            S.add("pe", lambda e: e.transpose(
                PS4[0:1, 0:2], k2run[:, :], id_f32[0:2, 0:2]),
                waits=[prev_red], ms="k2tr")
            S.add("dve", lambda e: e.tensor_reduce(
                srow[0:1, 4:5], PS4[0:1, 0:2], AX.X, ALU.max),
                waits=["k2tr"], ms="k2s_done")
            S.add("dve", lambda e: e.tensor_reduce(
                misc[:, MC_T1:MC_T1 + 1], qmax_run[:], AX.X, ALU.max),
                ms="qm1")
            S.add("pe", lambda e: e.transpose(
                PS4[0:1, 4:132], misc[:, MC_T1:MC_T1 + 1], id_f32[:]),
                waits=["qm1", "k2s_done"], ms="qmtr")
            S.add("dve", lambda e: e.tensor_reduce(
                srow[0:1, 5:6], PS4[0:1, 4:132], AX.X, ALU.max),
                waits=["qmtr"], ms="q2s_done")
            S.add("act", lambda e: e.activation(
                srow[0:1, 6:7], srow[0:1, 4:5], AF.Sqrt),
                waits=["k2s_done"], ms="ks_done")
            S.add("act", lambda e: e.activation(
                srow[0:1, 7:8], srow[0:1, 5:6], AF.Sqrt),
                waits=["q2s_done"], ms="qs_done")
            S.add("act", lambda e: e.activation(
                misc[0:1, 15:16], misc[0:1, MC_EPS:MC_EPS + 1], AF.Exp),
                waits=["qs_done"])
            S.add("dve", lambda e: e.tensor_tensor(
                srow[0:1, 4:5], srow[0:1, 6:7], srow[0:1, 7:8], ALU.mult),
                waits=["ks_done", "qs_done"])
            S.add("dve", lambda e: e.tensor_scalar(
                srow[0:1, 4:5], srow[0:1, 4:5], -1.02, None, ALU.mult),
                ms="negc_row")
            S.add("pe", lambda e: e.matmul(
                PS4[:, 2:3], ones_r[:, :], srow[0:1, 4:5], start=True, stop=True),
                waits=["negc_row"], ms="negc_mm")
            S.add("dve", lambda e: e.tensor_copy(negc_bc[:], PS4[:, 2:3]),
                waits=["negc_mm"], ms="negc_done")

            def epi_steps(ti, hp):
                pp = f"{ti}_{hp}"
                steps = []
                for k_, pv in ((0, PVa), (1, PVb)):
                    w_ev = [f"pv_{pp}_{NCH-1}"] if k_ == 0 else []
                    steps.append(("dve", lambda e, k_=k_, pv=pv: e.tensor_copy(
                        pvE[:, k_ * 260:(k_ + 1) * 260], pv[:, 0:260]),
                        w_ev, (f"pvev_{pp}" if k_ == 1 else None), None, False))
                steps.append(("dve", lambda e: e.reciprocal(
                    rlv[:, 0:8], pvE[:, 64:520:65]), [], None, None, False))
                for blk in range(8):
                    steps.append(("dve", lambda e, blk=blk: e.tensor_scalar(
                        ao_sb[:, blk * 64:(blk + 1) * 64],
                        pvE[:, blk * 65: blk * 65 + 64],
                        rlv[:, blk:blk + 1], None, ALU.mult),
                        [], (f"aos_{pp}" if blk == 7 else None), None, False))
                for blk in range(8):
                    w_btr = []
                    if blk == 0:
                        w_btr.append(f"aos_{pp}")
                        if hp >= 1:
                            w_btr.append(f"aoev_{ti}_{hp-1}")
                        elif ti >= 1:
                            w_btr.append(f"aoev_{ti-1}_{HP-1}")
                        else:
                            w_btr.append(f"k2red_{HP-1}_{NHALF-1}")
                    steps.append(("pe", lambda e, blk=blk: e.transpose(
                        PT2[0:64, blk * 128:(blk + 1) * 128],
                        ao_sb[:, blk * 64:(blk + 1) * 64], id_bf[:]),
                        w_btr, (f"btr_{pp}" if blk == 7 else None),
                        None, True))
                ao_ = aoT[ti % 2]
                for k_ in range(2):
                    w_ao = [f"btr_{pp}"] if k_ == 0 else []
                    if k_ == 0 and hp == 0 and ti >= 2:
                        w_ao.append(f"wo_{ti-2}_3")
                    steps.append(("dve", lambda e, k_=k_, hp=hp,
                                  ao_=ao_: e.tensor_copy(
                        ao_[k_ * 64:(k_ + 1) * 64, hp * 512:(hp + 1) * 512],
                        PT2[0:64, k_ * 512:(k_ + 1) * 512]),
                        w_ao, (f"aoev_{pp}" if k_ == 1 else None), None, False))
                return steps

            def sub_steps(ti):
                steps = []
                ao_ = aoT[ti % 2]
                for s_ in range(4):
                    tt = ti * 4 + s_
                    xsb = xs0
                    w_ldx = []
                    if s_ >= 1:
                        w_ldx.append(f"res_{ti}_{s_-1}")
                    elif ti >= 1:
                        w_ldx.append(f"res_{ti-1}_3")
                    else:
                        w_ldx.append("wfc_7")
                    steps.append(("sp", lambda e, tt=tt, xsb=xsb: e.dma_start(
                        xsb[:], x_hbm[tt * 128:(tt + 1) * 128, :]),
                        w_ldx, f"ldxs_{ti}_{s_}", "dX0", False))
                    w_xnf = [f"ldxs_{ti}_{s_}"]
                    if s_ >= 1:
                        w_xnf += [f"xnTb_{ti}_{s_-1}", f"xq_{ti}_{s_-1}"]
                    elif ti >= 1:
                        w_xnf += [f"xnTb_{ti-1}_3", f"xq_{ti-1}_3"]
                    steps.append(("dve", lambda e, tt=tt, xsb=xsb: e.tensor_scalar(
                        xnf[:], xsb[:], rstd1_res[:, tt:tt + 1],
                        nmr1_res[:, tt:tt + 1], ALU.mult, ALU.add),
                        w_xnf, f"xnf_{ti}_{s_}", None, False))
                    for h2 in range(2):
                        w_t = [f"xnf_{ti}_{s_}"]
                        if h2 == 0:
                            if s_ >= 1:
                                w_t.append(f"sgb_{ti}_{s_-1}")
                            elif ti >= 1:
                                w_t.append(f"sgb_{ti-1}_3")
                            else:
                                w_t.append("negc_done")
                        else:
                            w_t.append(f"q1a_{ti}_{s_}")
                        for b in range(4):
                            steps.append(("pe", lambda e, b=b, h2=h2: e.transpose(
                                PS4[:, b * 128:(b + 1) * 128],
                                xnf[:, h2 * 512 + b * 128:
                                    h2 * 512 + (b + 1) * 128], id_f32[:]),
                                (w_t if b == 0 else []),
                                (f"xnT{'ab'[h2]}_{ti}_{s_}" if b == 3 else None),
                                None, True))
                        steps.append(("dve", lambda e, h2=h2: e.tensor_scalar(
                            qtmp[:, :], PS4[:, :],
                            misc[:, MC_INVIS:MC_INVIS + 1],
                            240.0, ALU.mult, ALU.min),
                            [f"xnT{'ab'[h2]}_{ti}_{s_}", "invis_done"],
                            f"q1{'ab'[h2]}_{ti}_{s_}", None, False))
                        steps.append(("dve", lambda e, h2=h2: e.tensor_scalar(
                            qtmp2[:, h2 * 512:(h2 + 1) * 512], qtmp[:, :],
                            -240.0, None, ALU.max),
                            [], f"q2{'ab'[h2]}_{ti}_{s_}", None, False))
                    steps.append(("dve", lambda e: e.tensor_copy(
                        xqT[:], qtmp2[:]),
                        [f"q2b_{ti}_{s_}"], f"xq_{ti}_{s_}", None, False))
                    for nn in range(2):
                        if nn == 0:
                            w_ffn = [f"xq_{ti}_{s_}", "wfc_7", f"q1b_{ti}_{s_}"]
                        else:
                            w_ffn = [f"yva_{ti}_{s_}"]
                        for fc in range(8):
                            steps.append(("pe", lambda e, fc=fc, nn=nn: e.matmul(
                                PS4[:, :],
                                xqT[:, fc * 128:(fc + 1) * 128],
                                wft_sb[:, fc * DM + nn * 512:
                                       fc * DM + (nn + 1) * 512],
                                start=(fc == 0), stop=(fc == 7)),
                                (w_ffn if fc == 0 else []),
                                (f"ffn{'ab'[nn]}_{ti}_{s_}" if fc == 7
                                 else None), None, True))
                        w_yv = [f"ffn{'ab'[nn]}_{ti}_{s_}", "isws_done",
                                "bbc_done"]
                        if nn == 0:
                            if s_ >= 1:
                                w_yv.append(f"sto_{ti}_{s_-1}")
                            elif ti >= 1:
                                w_yv.append(f"sto_{ti-1}_3")
                        steps.append(("dve", lambda e, nn=nn: e.scalar_tensor_tensor(
                            yv_sb[:, nn * 512:(nn + 1) * 512], PS4[:, :],
                            misc[:, MC_ISWS:MC_ISWS + 1],
                            bias_bc[:, nn * 512:(nn + 1) * 512],
                            ALU.mult, ALU.add),
                            w_yv,
                            (f"yva_{ti}_{s_}" if nn == 0 else f"yv_{ti}_{s_}"),
                            None, False))
                    steps.append(("act", lambda e: e.activation(
                        qtmp2[:], yv_sb[:], AF.Tanh, scale=0.5),
                        [f"yv_{ti}_{s_}"], f"th_{ti}_{s_}", None, False))
                    steps.append(("dve", lambda e: e.tensor_scalar(
                        gbuf[:], qtmp2[:], 0.5, 0.5, ALU.mult, ALU.add),
                        [f"th_{ti}_{s_}"], f"sig_{ti}_{s_}", None, False))
                    for nn in range(2):
                        if nn == 0:
                            w_wo = [f"aoev_{ti}_{HP-1}", "ld_wo",
                                    f"yv_{ti}_{s_}"]
                        else:
                            w_wo = [f"u_{ti}_{s_}_0"]
                        for hp in range(HP):
                            steps.append(("pe", lambda e, hp=hp, nn=nn, s_=s_,
                                          ao_=ao_: e.matmul(
                                PS4[:, :],
                                ao_[:, hp * 512 + s_ * 128:
                                    hp * 512 + (s_ + 1) * 128],
                                wo_sb[:, hp * DM + nn * 512:
                                      hp * DM + (nn + 1) * 512],
                                start=(hp == 0), stop=(hp == HP - 1)),
                                (w_wo if hp == 0 else []),
                                (f"wo{'ab'[nn]}_{ti}_{s_}" if hp == HP - 1
                                 else None), None, True))
                        steps.append(("dve", lambda e, nn=nn: e.tensor_tensor(
                            qtmp[:, :], PS4[:, :],
                            gbuf[:, nn * 512:(nn + 1) * 512], ALU.mult),
                            [f"wo{'ab'[nn]}_{ti}_{s_}", f"sig_{ti}_{s_}"],
                            f"u_{ti}_{s_}_{nn}", None, False))
                        steps.append(("dve", lambda e, nn=nn: e.tensor_tensor(
                            gbuf[:, nn * 512:(nn + 1) * 512], qtmp[:, :],
                            yv_sb[:, nn * 512:(nn + 1) * 512], ALU.mult),
                            [], (f"sg_{ti}_{s_}" if nn == 0
                                 else f"sgb_{ti}_{s_}"), None, False))
                    steps.append(("dve", lambda e, xsb=xsb: e.tensor_tensor(
                        yv_sb[:], gbuf[:], xsb[:], ALU.add),
                        [f"sgb_{ti}_{s_}"], f"res_{ti}_{s_}", None, False))
                    steps.append(("sp", lambda e, tt=tt: e.dma_start(
                        out_hbm[tt * 128:(tt + 1) * 128, :], yv_sb[:]),
                        [f"res_{ti}_{s_}"], f"sto_{ti}_{s_}", "dSo", False))
                steps.append(("pe", lambda e: e.nop(),
                              [f"wob_{ti}_3"], f"wo_{ti}_3", None, True))
                return steps

            for ti in range(NT2):
                S.add("sp", lambda e, ti=ti: e.dma_start(
                    qT_t[:],
                    qT_hbm[:, :].rearrange("p (hp n) -> p hp n", n=seq)[
                        :, :, ti * 512:(ti + 1) * 512]),
                    waits=([f"st_{ti-1}_{HP-1}_{NCH-1}"] if ti >= 1 else []),
                    dma="dQ", ms=f"ldq_{ti}")
                pend_sub = sub_steps(ti - 1) if ti >= 1 else []
                for hp in range(HP):
                    pp = f"{ti}_{hp}"
                    if hp >= 1:
                        pend = epi_steps(ti, hp - 1)
                    elif ti >= 1:
                        pend = epi_steps(ti - 1, HP - 1)
                    else:
                        pend = []
                    for c in range(NCH):
                        buf = PA if c % 2 == 0 else PB
                        w_st = [f"ldq_{ti}"] if c == 0 else []
                        if c >= 2:
                            w_st.append(f"exp_{pp}_{c-2}")
                        elif hp >= 1:
                            w_st.append(f"exp_{ti}_{hp-1}_{NCH-2+c}")
                        elif ti >= 1:
                            w_st.append(f"exp_{ti-1}_{HP-1}_{NCH-2+c}")
                        elif c == 1:
                            w_st.append("bbc_done")
                        for par in range(2):
                            b0 = par * 64
                            S.add("pe", lambda e, c=c, hp=hp, b0=b0, par=par,
                                  buf=buf, ti=ti: e.matmul(
                                buf[:, par * 512:(par + 1) * 512],
                                kt_sb[b0:b0 + 64,
                                      hp * seq + c * 128: hp * seq + (c + 1) * 128],
                                qT_t[b0:b0 + 64, hp * 512:(hp + 1) * 512],
                                start=True, stop=True, tile_position=(b0, 0)),
                                waits=(w_st if par == 0 else []),
                                ms=(f"st_{pp}_{c}" if par == 1 else None))
                        w_exp = [f"st_{pp}_{c}"]
                        if c >= 2:
                            w_exp.append(f"pv_{pp}_{c-2}")
                        elif hp >= 1:
                            w_exp.append(f"pv_{ti}_{hp-1}_{NCH-2+c}")
                        elif ti >= 1:
                            w_exp.append(f"pv_{ti-1}_{HP-1}_{NCH-2+c}")
                        if ti == 0 and hp == 0 and c == 0:
                            w_exp.append("negc_done")
                        S.add("act", lambda e, c=c, buf=buf: e.activation(
                            ET[:, (c % 2) * 1024:(c % 2) * 1024 + 1024],
                            buf[:, :], AF.Exp, bias=negc_bc[:, 0:1]),
                            waits=w_exp, ms=f"exp_{pp}_{c}")
                        for _ in range(2):
                            if pend and c >= 1:
                                st_ = pend.pop(0)
                                S.add(st_[0], st_[1], waits=st_[2], ms=st_[3],
                                      dma=st_[4], fence=st_[5])
                        for _ in range(1 if pend else 2):
                            if pend_sub and c >= 1:
                                st_ = pend_sub.pop(0)
                                S.add(st_[0], st_[1], waits=st_[2], ms=st_[3],
                                      dma=st_[4], fence=st_[5])
                        if c >= 1:
                            cc = c - 1
                            for par, pv in ((0, PVa), (1, PVb)):
                                w_pv = [f"exp_{pp}_{cc}"]
                                if ti == 0 and hp == 0 and cc == NCH // 2:
                                    w_pv.append("ld_v1")
                                if cc == 0:
                                    if hp >= 1:
                                        w_pv.append(f"pvev_{ti}_{hp-1}")
                                    elif ti >= 1:
                                        w_pv.append(f"pvev_{ti-1}_{HP-1}")
                                    else:
                                        w_pv.append("ld_v0")
                                        w_pv.append("wfc_7")
                                for sub in range(4):
                                    S.add("pe", lambda e, cc=cc, hp=hp, par=par,
                                          pv=pv, sub=sub: e.matmul(
                                        pv[:, sub * 65: sub * 65 + 65],
                                        ET[:, (cc % 2) * 1024 + par * 512 +
                                           sub * 128:
                                           (cc % 2) * 1024 + par * 512 +
                                           (sub + 1) * 128],
                                        v_sb[:, cc * VROW + (2 * hp + par) * V65:
                                             cc * VROW + (2 * hp + par) * V65 + V65],
                                        start=(cc == 0 and sub == 0),
                                        stop=(cc == NCH - 1 and sub == 3)),
                                        waits=(w_pv if par == 0 and sub == 0
                                               else []),
                                        ms=(f"pv_{pp}_{cc}"
                                            if par == 1 and sub == 3 else None))
                    cc = NCH - 1
                    for par, pv in ((0, PVa), (1, PVb)):
                        for sub in range(4):
                            S.add("pe", lambda e, cc=cc, hp=hp, par=par,
                                  pv=pv, sub=sub: e.matmul(
                                pv[:, sub * 65: sub * 65 + 65],
                                ET[:, (cc % 2) * 1024 + par * 512 + sub * 128:
                                   (cc % 2) * 1024 + par * 512 +
                                   (sub + 1) * 128],
                                v_sb[:, cc * VROW + (2 * hp + par) * V65:
                                     cc * VROW + (2 * hp + par) * V65 + V65],
                                start=False, stop=(sub == 3)),
                                waits=([f"exp_{pp}_{cc}"]
                                       if par == 0 and sub == 0 else []),
                                ms=(f"pv_{pp}_{cc}"
                                    if par == 1 and sub == 3 else None))
                    for st_ in pend:
                        S.add(st_[0], st_[1], waits=st_[2], ms=st_[3],
                              dma=st_[4], fence=st_[5])
                for st_ in pend_sub:
                    S.add(st_[0], st_[1], waits=st_[2], ms=st_[3],
                          dma=st_[4], fence=st_[5])
                if ti == NT2 - 1:
                    for st_ in epi_steps(ti, HP - 1):
                        S.add(st_[0], st_[1], waits=st_[2], ms=st_[3],
                              dma=st_[4], fence=st_[5])
                    for st_ in sub_steps(ti):
                        S.add(st_[0], st_[1], waits=st_[2], ms=st_[3],
                              dma=st_[4], fence=st_[5])
                S.add("pe", lambda e: e.nop(),
                    waits=[f"exp_{ti}_{HP-1}_{NCH-1}"], ms=f"atn_{ti}",
                    fence=True)

            total_dSo = S.dma_counts["dSo"]
            S.add("act", lambda e: e.wait_ge(sems["dSo"], total_dSo), raw=True)
            S.add("sp", lambda e: e.wait_ge(sems["dSo"], total_dSo), raw=True)

            _lim = os.environ.get("KDBG_P2LIMIT")
            _lim = int(_lim) if _lim else None
            with nc.Block() as blockB:
                blockB.tensor(S.replay("pe", sems, _lim))
                blockB.scalar(S.replay("act", sems, _lim))
                blockB.vector(S.replay("dve", sems, _lim))
                blockB.gpsimd(S.replay("pool", sems, _lim))
                blockB.sync(S.replay("sp", sems, _lim))
            print(f"[kernel] phase2 ops: {len(S.ops)} (limit={_lim})", flush=True)

    return nc


_CACHE = {}


def _get_program(seq):
    if seq not in _CACHE:
        _CACHE[seq] = build_program(seq)
    return _CACHE[seq]


def kernel(L0_attn_x, L0_attn_gamma, L0_attn_Wq, L0_attn_Wkv, L0_attn_W_bilinear,
           L0_attn_Wo, L0_ffn_weight_fp8, L0_ffn_bias, L0_ffn_input_scale,
           L0_ffn_weight_scale):
    x = np.ascontiguousarray(np.asarray(L0_attn_x, dtype=np.float32))
    B, seq, dm = x.shape
    assert dm == DM
    nc = _get_program(seq)

    ct, st2 = host_tables(seq)
    wb = np.asarray(L0_attn_W_bilinear, dtype=np.float32)
    wb_packed = np.zeros((128, HP * DH), dtype=np.float32)
    for h in range(H):
        wb_packed[(h % 2) * 64:(h % 2) * 64 + 64,
                  (h // 2) * DH:(h // 2) * DH + DH] = wb[h]
    idn = np.eye(128, dtype=np.float32)

    common = {
        "gamma": np.ascontiguousarray(
            np.asarray(L0_attn_gamma, dtype=np.float32).reshape(1, DM)),
        "wq": np.ascontiguousarray(np.asarray(L0_attn_Wq, dtype=np.float32)),
        "wkv": np.ascontiguousarray(np.asarray(L0_attn_Wkv, dtype=np.float32)),
        "wb": wb_packed,
        "wo": np.ascontiguousarray(np.asarray(L0_attn_Wo, dtype=np.float32)),
        "wf": np.ascontiguousarray(np.asarray(L0_ffn_weight_fp8, dtype=np.float32)),
        "bias": np.ascontiguousarray(
            np.asarray(L0_ffn_bias, dtype=np.float32).reshape(1, DM)),
        "is_": np.asarray(L0_ffn_input_scale, dtype=np.float32).reshape(1, 1),
        "ws": np.asarray(L0_ffn_weight_scale, dtype=np.float32).reshape(1, 1),
        "ct": ct, "st": st2, "idn": idn,
    }
    in_maps = []
    for c in range(B):
        m = dict(common)
        m["x"] = np.ascontiguousarray(x[c])
        in_maps.append(m)

    res = run_bass_kernel_spmd(nc, in_maps, core_ids=list(range(B)))
    out = np.stack([np.asarray(r["out"], dtype=np.float32)
                    for r in res.results], axis=0)
    return out


# version 28
# speedup vs baseline: 1.0148x; 1.0064x over previous
import contextlib
import os
import numpy as np
import ml_dtypes
import concourse.bass as bass
import concourse.mybir as mybir
from concourse.bass_utils import run_bass_kernel_spmd

F32 = mybir.dt.float32
BF16 = mybir.dt.bfloat16
FP8 = mybir.dt.float8e4
AF = mybir.ActivationFunctionType
ALU = mybir.AluOpType
AX = mybir.AxisListType

DM = 1024
H = 16
DH = 64
HP = H // 2
LN_EPS = 1e-5
ROPE_BASE = 10000.0
V65 = DH + 1
VROW = H * V65

ENGINES = ("pe", "act", "dve", "pool", "sp")


class Sched:

    def __init__(self):
        self.ops = []
        self.counts = {e: 0 for e in ENGINES}
        self.dma_counts = {}
        self.ms = {}
        self.base = {}

    SELF_SYNC = ("act", "dve", "pool")

    def add(self, eng, emit, waits=(), ms=None, dma=None, raw=False,
            fence=False):
        waits = tuple(w for w in waits if w is not None)
        for w in waits:
            assert w in self.ms, f"wait on undefined milestone {w}"
        if raw:
            self.ops.append((eng, emit, waits, None, None, True, None))
            return
        if dma is None:
            self.counts[eng] += 1
            if ms is not None:
                self.ms[ms] = (eng, self.counts[eng])
        else:
            self.dma_counts[dma] = self.dma_counts.get(dma, 0) + 16
            if ms is not None:
                self.ms[ms] = (dma, self.dma_counts[dma])
        fence_val = self.counts.get(eng, 0) - (1 if dma is None else 0) \
            if (fence and ms is not None) else None
        self.ops.append((eng, emit, waits, ms, dma, False, fence_val))

    def replay(self, eng_name, sems, limit=None):
        src = self.ops if limit is None else self.ops[:limit]
        ops = [o for o in src if o[0] == eng_name]
        self_sync = eng_name in self.SELF_SYNC

        def f(e):
            waited = {}
            n_done = self.base.get(eng_name, 0)
            for (_, emit, waits, ms, dma, raw, fence_val) in ops:
                for w in waits:
                    sk, val = self.ms[w]
                    if sk == eng_name and self_sync:
                        continue
                    if waited.get(sk, -1) >= val:
                        continue
                    e.wait_ge(sems[sk], val)
                    waited[sk] = val
                if not raw and self_sync and n_done > 0:
                    e.wait_ge(sems[eng_name], n_done)
                elif fence_val is not None and fence_val > 0 and not self_sync:
                    e.wait_ge(sems[eng_name], fence_val)
                ins = emit(e)
                if raw:
                    continue
                if dma is not None:
                    ins.then_inc(sems[dma], 16)
                else:
                    ins.then_inc(sems[eng_name], 1)
                    n_done += 1
        return f


def host_tables(seq):
    theta = 1.0 / ROPE_BASE ** (np.arange(0, DH, 2, dtype=np.float64) / DH)
    pos = np.arange(seq, dtype=np.float64)
    fr = pos[:, None] * theta[None, :]
    emb = np.concatenate([fr, fr], axis=-1)
    cos = np.cos(emb)
    sin = np.sin(emb)
    ct = cos.astype(np.float32)
    st2 = np.empty_like(ct)
    st2[:, 0::2] = sin[:, 1::2].astype(np.float32)
    st2[:, 1::2] = (-sin[:, 0::2]).astype(np.float32)
    return ct, st2


def build_program(seq):
    NT = seq // 128
    NT2 = seq // 512
    NCH = seq // 128
    NSL = seq // 512

    nc = bass.Bass("TRN2", target_bir_lowering=False)

    x_hbm = nc.dram_tensor("x", [seq, DM], F32, kind="ExternalInput")
    gamma_h = nc.dram_tensor("gamma", [1, DM], F32, kind="ExternalInput")
    wq_h = nc.dram_tensor("wq", [DM, DM], F32, kind="ExternalInput")
    wkv_h = nc.dram_tensor("wkv", [DM, 2 * DM], F32, kind="ExternalInput")
    wb_h = nc.dram_tensor("wb", [128, HP * DH], F32, kind="ExternalInput")
    wo_h = nc.dram_tensor("wo", [DM, DM], F32, kind="ExternalInput")
    wf_h = nc.dram_tensor("wf", [DM, DM], F32, kind="ExternalInput")
    bias_h = nc.dram_tensor("bias", [1, DM], F32, kind="ExternalInput")
    is_h = nc.dram_tensor("is_", [1, 1], F32, kind="ExternalInput")
    ws_h = nc.dram_tensor("ws", [1, 1], F32, kind="ExternalInput")
    ct_h = nc.dram_tensor("ct", [seq, DH], F32, kind="ExternalInput")
    st_h = nc.dram_tensor("st", [seq, DH], F32, kind="ExternalInput")
    id_h = nc.dram_tensor("idn", [128, 128], F32, kind="ExternalInput")
    out_hbm = nc.dram_tensor("out", [seq, DM], F32, kind="ExternalOutput")

    qT_hbm = nc.dram_tensor("qT_s", [128, HP * seq], BF16, kind="Internal")
    v_hbm = nc.dram_tensor("v_s", [128, NT * VROW], BF16, kind="Internal")

    with contextlib.ExitStack() as stack:
        id_f32 = stack.enter_context(nc.sbuf_tensor("id_f32", [128, 128], F32))
        id_bf = stack.enter_context(nc.sbuf_tensor("id_bf", [128, 128], BF16))
        ones_r = stack.enter_context(nc.sbuf_tensor("ones_r", [1, 128], F32))
        rstd1_res = stack.enter_context(nc.sbuf_tensor("rstd1", [128, NT], F32))
        nmr1_res = stack.enter_context(nc.sbuf_tensor("nmr1", [128, NT], F32))
        qmax_run = stack.enter_context(nc.sbuf_tensor("qmaxr", [128, H], F32))
        misc = stack.enter_context(nc.sbuf_tensor("misc", [128, 16], F32))
        kt_sb = stack.enter_context(nc.sbuf_tensor("kt_sb", [128, HP * seq], BF16))
        (MC_SUM, MC_SSQ, MC_M, MC_T1, MC_T2, MC_T3, MC_R2, MC_N2,
         MC_ISWS, MC_INVIS, MC_EPS) = range(11)

        sems = {}
        for name in ("pe", "act", "dve", "pool", "sp", "bar", "dP", "dP0", "dG",
                     "dX0", "dX1", "dX2", "dSq0", "dSq1", "dSk0", "dSk1", "dSv0", "dSv1", "dP2", "dW0", "dW1",
                     "dQ", "dSo"):
            sems[name] = stack.enter_context(nc.semaphore("sem_" + name))

        with contextlib.ExitStack() as p1:
            wq_sb = p1.enter_context(nc.sbuf_tensor("wq_sb", [128, 8 * DM], BF16))
            wkv_sb = p1.enter_context(nc.sbuf_tensor("wkv_sb", [128, 16 * DM], BF16))
            wb_sb = p1.enter_context(nc.sbuf_tensor("wb_sb", [128, HP * DH], BF16))
            ct_sb = p1.enter_context(nc.sbuf_tensor("ct_sb", [128, NT * DH], F32))
            st_sb = p1.enter_context(nc.sbuf_tensor("st_sb", [128, NT * DH], F32))
            gamma_bc = p1.enter_context(nc.sbuf_tensor("gamma_bc", [128, DM], F32))
            grow = p1.enter_context(nc.sbuf_tensor("grow", [1, DM], F32))
            xbl = [p1.enter_context(nc.sbuf_tensor(f"xb{i}", [128, DM], F32)) for i in range(3)]
            xnn_sb = [p1.enter_context(nc.sbuf_tensor(f"xnn_sb{i}", [128, DM], F32)) for i in range(2)]
            sqdump = p1.enter_context(nc.sbuf_tensor("sqdump", [128, DM], BF16))
            xn2_sb = [p1.enter_context(nc.sbuf_tensor(f"xn2_sb{i}", [128, DM], BF16)) for i in range(2)]
            xn2T = [p1.enter_context(nc.sbuf_tensor(f"xn2T{i}", [128, DM], BF16)) for i in range(2)]
            ropeA = [p1.enter_context(nc.sbuf_tensor(f"ropeA{i}", [128, DM], BF16)) for i in range(2)]
            ropeB = [p1.enter_context(nc.sbuf_tensor(f"ropeB{i}", [128, DM], BF16)) for i in range(2)]
            qr_sb = [p1.enter_context(nc.sbuf_tensor(f"qr_sb{i}", [128, DM], BF16)) for i in range(2)]
            kr_sb = [p1.enter_context(nc.sbuf_tensor(f"kr_sb{i}", [128, DM], BF16)) for i in range(2)]
            qTs = [p1.enter_context(nc.sbuf_tensor(f"qTs{i}", [128, DM], BF16)) for i in range(2)]
            kTs = [p1.enter_context(nc.sbuf_tensor(f"kTs{i}", [128, DM], BF16)) for i in range(2)]
            vst = [p1.enter_context(nc.sbuf_tensor(f"vst{i}", [128, VROW], BF16)) for i in range(2)]
            qn_sb = p1.enter_context(nc.sbuf_tensor("qn_sb", [128, H], F32))
            stat = [p1.enter_context(nc.sbuf_tensor(f"stat{i}", [128, 16], F32)) for i in range(2)]

            P0 = p1.enter_context(nc.psum_tensor("P0", [128, 1024], F32))
            P1 = p1.enter_context(nc.psum_tensor("P1", [128, 1024], F32))
            P2 = p1.enter_context(nc.psum_tensor("P2", [128, 512], F32))
            P3 = p1.enter_context(nc.psum_tensor("P3", [128, 512], F32))
            PT = p1.enter_context(nc.psum_tensor("PT", [128, 1024], BF16))
            PTB = p1.enter_context(nc.psum_tensor("PTB", [128, 1024], BF16))

            S = Sched()

            S.add("dve", lambda e: e.memset(misc[0:1, 14:15], 0.0),
                ms="scr_init")
            S.add("act", lambda e: e.activation(
                misc[0:1, 15:16], misc[0:1, 14:15], AF.Square),
                waits=["scr_init"])
            S.add("pool", lambda e: e.dma_start(
                wq_sb[:, :].rearrange("p (fc o) -> p fc o", o=DM),
                wq_h[:, :].rearrange("(fc p) o -> p fc o", p=128)),
                dma="dG", ms="ld_wq_i")
            S.add("pool", lambda e: e.dma_start(
                wkv_sb[:, :].rearrange("p (fc o) -> p fc o", o=2 * DM),
                wkv_h[:, :].rearrange("(fc p) o -> p fc o", p=128)),
                dma="dG", ms="ld_wkv_i")
            S.add("pool", lambda e: e.dma_start(wb_sb[:], wb_h[:, :]),
                dma="dG", ms="wload")
            S.add("sp", lambda e: e.dma_start(id_f32[:], id_h[:, :]),
                dma="dP0", ms="ld_id_i")
            S.add("sp", lambda e: e.dma_start(grow[:], gamma_h[:, :]),
                dma="dP0", ms="p1_pre")
            S.add("sp", lambda e: e.dma_start(
                ct_sb[:, :].rearrange("p (t d) -> p t d", d=DH),
                ct_h[:, :].rearrange("(t p) d -> p t d", p=128)),
                dma="dP", ms="ld_ct_i")
            S.add("sp", lambda e: e.dma_start(
                st_sb[:, :].rearrange("p (t d) -> p t d", d=DH),
                st_h[:, :].rearrange("(t p) d -> p t d", p=128)),
                dma="dP", ms="tab_ld")
            S.add("dve", lambda e: e.memset(ones_r[:], 1.0), ms="ones_done")
            S.add("dve", lambda e: e.memset(
                misc[:, MC_EPS:MC_EPS + 1], LN_EPS), ms="eps_done")
            S.add("dve", lambda e: e.memset(vst[0][:], 1.0))
            S.add("dve", lambda e: e.memset(vst[1][:], 1.0), ms="vst_ones")
            S.add("dve", lambda e: e.memset(qmax_run[:], 0.0), ms="qmax_init")
            S.add("act", lambda e: e.activation(id_bf[:], id_f32[:], AF.Copy),
                waits=["p1_pre"], ms="idbf_done")
            S.add("dve", lambda e: e.tensor_scalar(
                wq_sb[:], wq_sb[:], 0.125, None, ALU.mult),
                waits=["wload"], ms="wq_scaled")
            for nn in range(2):
                S.add("pe", lambda e, nn=nn: e.matmul(
                    P0[:, nn * 512:(nn + 1) * 512], ones_r[:, :],
                    grow[0:1, nn * 512:(nn + 1) * 512], start=True, stop=True),
                    waits=["ones_done", "p1_pre"],
                    ms=("gbc_mm" if nn == 1 else None))
            S.add("dve", lambda e: e.tensor_copy(gamma_bc[:], P0[:, :]),
                waits=["gbc_mm"], ms="gbc_done")

            def stageAload(t):
                xb = xbl[t % 3]
                S.add("sp", lambda e, t=t, xb=xb: e.dma_start(
                    xb[:], x_hbm[t * 128:(t + 1) * 128, :]),
                    waits=([f"xnn_{t-3}", f"rs1_{t-3}"] if t >= 3 else []),
                    dma=("dX0", "dX1", "dX2")[t % 3], ms=f"ldx_{t}")

            def stageA(t):
                p = t % 2
                xb = xbl[t % 3]
                ST = stat[p]
                xnn_ = xnn_sb[p]; xn2_ = xn2_sb[p]
                S.add("dve", lambda e, xb=xb, ST=ST: e.reduce_sum(
                    ST[:, MC_SUM:MC_SUM + 1], xb[:], AX.X),
                    waits=[f"ldx_{t}"], ms=f"rs1_{t}")
                S.add("act", lambda e, xb=xb, ST=ST: e.activation(
                    sqdump[:], xb[:], AF.Square,
                    accum_out=ST[:, MC_SSQ:MC_SSQ + 1]),
                    waits=[f"ldx_{t}"] + ([f"mic_{t-2}"] if t >= 2 else []),
                    ms=f"sq1_{t}")
                S.add("dve", lambda e, ST=ST: e.tensor_scalar(
                    ST[:, MC_M:MC_M + 1], ST[:, MC_SUM:MC_SUM + 1],
                    1.0 / DM, None, ALU.mult), waits=[f"sq1_{t}"])
                S.add("dve", lambda e, ST=ST: e.tensor_tensor(
                    ST[:, MC_T1:MC_T1 + 1], ST[:, MC_M:MC_M + 1],
                    ST[:, MC_M:MC_M + 1], ALU.mult))
                S.add("dve", lambda e, ST=ST: e.tensor_scalar(
                    ST[:, MC_T2:MC_T2 + 1], ST[:, MC_SSQ:MC_SSQ + 1],
                    1.0 / DM, None, ALU.mult))
                S.add("dve", lambda e, ST=ST: e.tensor_tensor(
                    ST[:, MC_T2:MC_T2 + 1], ST[:, MC_T2:MC_T2 + 1],
                    ST[:, MC_T1:MC_T1 + 1], ALU.subtract), ms=f"var1_{t}")
                S.add("act", lambda e, ST=ST: e.activation(
                    ST[:, MC_T3:MC_T3 + 1], ST[:, MC_T2:MC_T2 + 1],
                    AF.Sqrt, bias=misc[:, MC_EPS:MC_EPS + 1]),
                    waits=[f"var1_{t}", "eps_done"], ms=f"sd1_{t}")
                S.add("dve", lambda e, t=t, ST=ST: e.reciprocal(
                    rstd1_res[:, t:t + 1], ST[:, MC_T3:MC_T3 + 1]),
                    waits=[f"sd1_{t}"])
                S.add("dve", lambda e, t=t, ST=ST: e.tensor_scalar(
                    nmr1_res[:, t:t + 1], ST[:, MC_M:MC_M + 1],
                    rstd1_res[:, t:t + 1], -1.0, ALU.mult, ALU.mult))
                S.add("dve", lambda e, ST=ST: e.tensor_tensor(
                    ST[:, MC_T1:MC_T1 + 1], ST[:, MC_T3:MC_T3 + 1],
                    ST[:, MC_T3:MC_T3 + 1], ALU.mult))
                S.add("dve", lambda e, ST=ST: e.reciprocal(
                    ST[:, MC_T1:MC_T1 + 1], ST[:, MC_T1:MC_T1 + 1]))
                S.add("dve", lambda e, ST=ST: e.tensor_scalar(
                    ST[:, MC_T1:MC_T1 + 1], ST[:, MC_T2:MC_T2 + 1],
                    ST[:, MC_T1:MC_T1 + 1], None, ALU.mult),
                    ms=f"v2_{t}")
                S.add("act", lambda e, ST=ST: e.activation(
                    ST[:, MC_T1:MC_T1 + 1], ST[:, MC_T1:MC_T1 + 1],
                    AF.Sqrt, bias=misc[:, MC_EPS:MC_EPS + 1]),
                    waits=[f"v2_{t}"], ms=f"sd2_{t}")
                S.add("dve", lambda e, ST=ST: e.reciprocal(
                    ST[:, MC_T1:MC_T1 + 1], ST[:, MC_T1:MC_T1 + 1]),
                    waits=[f"sd2_{t}"])
                S.add("dve", lambda e, t=t, ST=ST: e.tensor_scalar(
                    ST[:, MC_R2:MC_R2 + 1], rstd1_res[:, t:t + 1],
                    ST[:, MC_T1:MC_T1 + 1], None, ALU.mult))
                S.add("dve", lambda e, ST=ST: e.tensor_scalar(
                    ST[:, MC_N2:MC_N2 + 1], ST[:, MC_M:MC_M + 1],
                    ST[:, MC_R2:MC_R2 + 1], -1.0, ALU.mult, ALU.mult),
                    ms=f"mic_{t}")
                S.add("act", lambda e, xb=xb, xnn_=xnn_, ST=ST: e.activation(
                    xnn_[:], xb[:], AF.Identity,
                    bias=ST[:, MC_N2:MC_N2 + 1],
                    scale=ST[:, MC_R2:MC_R2 + 1]),
                    waits=[f"mic_{t}"] + ([f"xn2_{t-2}"] if t >= 2 else []),
                    ms=f"xnn_{t}")
                S.add("pool", lambda e, xnn_=xnn_, xn2_=xn2_: e.tensor_tensor(
                    xn2_[:], xnn_[:], gamma_bc[:], ALU.mult),
                    waits=[f"xnn_{t}", "gbc_done"] +
                          ([f"tr1_{t-2}"] if t >= 2 else []),
                    ms=f"xn2_{t}")

            def stageB(t, part):
                p = t % 2
                x2T = xn2T[p]; rA = ropeA[p]; rB = ropeB[p]
                qr_ = qr_sb[p]; kr_ = kr_sb[p]
                qT_ = qTs[p]; kT_ = kTs[p]; vs_ = vst[p]
                xn2_ = xn2_sb[p]
                ctv = ct_sb[:, t * DH:(t + 1) * DH].rearrange(
                    "p (o d) -> p o d", o=1).to_broadcast([128, H, DH])
                stv = st_sb[:, t * DH:(t + 1) * DH].rearrange(
                    "p (o d) -> p o d", o=1).to_broadcast([128, H, DH])

                def rope(src_ps, pev, ev_ms, dst, mm_ms, out_ms, guard):
                    S.add("dve", lambda e, src_ps=src_ps, ctv=ctv,
                          rA=rA: e.tensor_tensor(
                        rA[:, :].rearrange("p (h d) -> p h d", d=DH),
                        src_ps[:, :].rearrange("p (h d) -> p h d", d=DH),
                        ctv, ALU.mult),
                        waits=[mm_ms, "p1_pre", "tab_ld"] + guard)
                    S.add("dve", lambda e, src_ps=src_ps, stv=stv,
                          rB=rB: e.tensor_tensor(
                        rB[:, :].rearrange("p (h d) -> p h d", d=DH),
                        src_ps[:, :].rearrange("p (h d) -> p h d", d=DH),
                        stv, ALU.mult),
                        ms=ev_ms)
                    ra = rA[:, :].rearrange("p (n two) -> p n two", two=2)
                    rb = rB[:, :].rearrange("p (n two) -> p n two", two=2)
                    dd = dst[:, :].rearrange("p (n two) -> p n two", two=2)
                    S.add("dve", lambda e, ra=ra, rb=rb, dd=dd: e.tensor_tensor(
                        dd[:, :, 0:1], ra[:, :, 0:1], rb[:, :, 1:2], ALU.add))
                    S.add("dve", lambda e, ra=ra, rb=rb, dd=dd: e.tensor_tensor(
                        dd[:, :, 1:2], ra[:, :, 1:2], rb[:, :, 0:1], ALU.add),
                        ms=out_ms)

                if part == 1:
                    for b in range(8):
                        S.add("pe", lambda e, b=b, xn2_=xn2_: e.transpose(
                            PTB[:, b * 128:(b + 1) * 128],
                            xn2_[:, b * 128:(b + 1) * 128], id_bf[:]),
                            waits=(["idbf_done", f"xn2_{t}"] +
                                   ([f"ev2_{t-1}"] if b == 0 and t >= 1 else [])),
                            ms=(f"tr1_{t}" if b == 7 else None))
                    S.add("act", lambda e, x2T=x2T: e.activation(
                        x2T[:], PTB[:, :], AF.Copy),
                        waits=[f"tr1_{t}"] + ([f"vmm_{t-2}"] if t >= 2 else []),
                        ms=f"ev1_{t}")
                    return
                if part == 12:
                    for fc in range(8):
                        for nn in range(2):
                            S.add("pe", lambda e, fc=fc, nn=nn, x2T=x2T: e.matmul(
                                P0[:, nn * 512:(nn + 1) * 512],
                                x2T[:, fc * 128:(fc + 1) * 128],
                                wq_sb[:, fc * DM + nn * 512: fc * DM + (nn + 1) * 512],
                                start=(fc == 0), stop=(fc == 7)),
                                waits=([f"ev1_{t}", "wq_scaled"] +
                                       ([f"evq_{t-1}"] if t >= 1 else []) +
                                       (["gbc_done"] if t == 0 else [])
                                       if fc == 0 and nn == 0 else []),
                                ms=(f"qmm_{t}" if fc == 7 and nn == 1 else None))
                    for fc in range(8):
                        for nn in range(2):
                            S.add("pe", lambda e, fc=fc, nn=nn, x2T=x2T: e.matmul(
                                P1[:, nn * 512:(nn + 1) * 512],
                                x2T[:, fc * 128:(fc + 1) * 128],
                                wkv_sb[:, fc * 2 * DM + nn * 512:
                                       fc * 2 * DM + (nn + 1) * 512],
                                start=(fc == 0), stop=(fc == 7)),
                                waits=((["wload"] +
                                        ([f"evk_{t-1}"] if t >= 1 else []))
                                       if fc == 0 and nn == 0 else []),
                                ms=(f"kmm_{t}" if fc == 7 and nn == 1 else None))
                    rope(P0, None, f"evq_{t}", qr_, f"qmm_{t}", f"ropq_{t}",
                         [f"tr2_{t-2}"] if t >= 2 else [])
                    S.add("dve", lambda e, qr_=qr_, rA=rA: e.tensor_tensor(
                        rA[:], qr_[:], qr_[:], ALU.mult),
                        waits=[f"ropq_{t}"])
                    S.add("dve", lambda e, rA=rA: e.tensor_reduce(
                        qn_sb[:], rA[:, :].rearrange("p (h d) -> p h d", d=DH),
                        AX.X, ALU.add))
                    S.add("dve", lambda e: e.tensor_tensor(
                        qmax_run[:], qmax_run[:], qn_sb[:], ALU.max),
                        waits=["qmax_init"], ms=f"qn_{t}")
                    return
                if part == 15:
                    for b in range(8):
                        S.add("pe", lambda e, b=b, qr_=qr_: e.transpose(
                            PT[:, b * 128:(b + 1) * 128],
                            qr_[:, b * 128:(b + 1) * 128], id_bf[:]),
                            waits=([f"ropq_{t}"] +
                                   ([f"ev3_{t-1}"] if t >= 1 else [])
                                   if b == 0 else []),
                            ms=(f"tr2_{t}" if b == 7 else None))
                    S.add("act", lambda e, qT_=qT_: e.activation(
                        qT_[:], PT[:, :], AF.Copy),
                        waits=[f"tr2_{t}"] + ([f"stq_{t-2}"] if t >= 2 else []),
                        ms=f"ev2_{t}")
                    S.add("sp", lambda e, t=t, qT_=qT_: e.dma_start(
                        qT_hbm[:, :].rearrange("p (hp n) -> p hp n", n=seq)[
                            :, :, t * 128:(t + 1) * 128],
                        qT_[:, :].rearrange("p (hp n) -> p hp n", n=128)),
                        waits=[f"ev2_{t}"], dma=("dSq0" if t % 2 == 0 else "dSq1"), ms=f"stq_{t}")
                    rope(P1, None, f"evk_{t}", kr_, f"kmm_{t}", f"ropk_{t}",
                         [f"tr3_{t-2}"] if t >= 2 else [])
                    return
                if part == 2:
                    for fc in range(8):
                        for ps, nn in ((P2, 2), (P3, 3)):
                            S.add("pe", lambda e, fc=fc, nn=nn, ps=ps, x2T=x2T: e.matmul(
                                ps[:, :],
                                x2T[:, fc * 128:(fc + 1) * 128],
                                wkv_sb[:, fc * 2 * DM + nn * 512:
                                       fc * 2 * DM + (nn + 1) * 512],
                                start=(fc == 0), stop=(fc == 7)),
                                waits=(([f"ktev_{t-1}"] if t >= 1 else [])
                                       if fc == 0 and nn == 2 else []),
                                ms=(f"vmm_{t}" if fc == 7 and nn == 3 else None))
                    for b in range(8):
                        S.add("pe", lambda e, b=b, kr_=kr_: e.transpose(
                            PT[:, b * 128:(b + 1) * 128],
                            kr_[:, b * 128:(b + 1) * 128], id_bf[:]),
                            waits=([f"ropk_{t}", f"ev2_{t}"] if b == 0 else []),
                            ms=(f"tr3_{t}" if b == 7 else None))
                    S.add("act", lambda e, kT_=kT_: e.activation(
                        kT_[:], PT[:, :], AF.Copy),
                        waits=[f"tr3_{t}"] +
                              ([f"bil_done_{t-2}"] if t >= 2 else []),
                        ms=f"ev3_{t}")
                    vout = vs_[:, :].rearrange("p (h e) -> p h e", e=V65)
                    S.add("act", lambda e, vout=vout: e.activation(
                        vout[:, 0:8, 0:DH],
                        P2[:, :].rearrange("p (h d) -> p h d", d=DH), AF.Copy),
                        waits=[f"vmm_{t}", "vst_ones"] +
                              ([f"stv_{t-2}"] if t >= 2 else []))
                    S.add("act", lambda e, vout=vout: e.activation(
                        vout[:, 8:16, 0:DH],
                        P3[:, :].rearrange("p (h d) -> p h d", d=DH), AF.Copy),
                        ms=f"vev_{t}")
                    S.add("sp", lambda e, t=t, vs_=vs_: e.dma_start(
                        v_hbm[:, t * VROW:(t + 1) * VROW], vs_[:]),
                        waits=[f"vev_{t}"], dma=("dSv0" if t % 2 == 0 else "dSv1"),
                        ms=f"stv_{t}")
                    return
                for hp in range(HP):
                    pbk = P2 if hp % 2 == 0 else P3
                    slot = hp // 2
                    for par in range(2):
                        b0 = par * 64
                        S.add("pe", lambda e, hp=hp, b0=b0, slot=slot,
                              pbk=pbk, kT_=kT_: e.matmul(
                            pbk[b0:b0 + 64, slot * 128:(slot + 1) * 128],
                            wb_sb[b0:b0 + 64, hp * DH:(hp + 1) * DH],
                            kT_[b0:b0 + 64, hp * 128:(hp + 1) * 128],
                            start=True, stop=True, tile_position=(b0, b0)),
                            waits=(([f"ev3_{t}", "wload", f"vev_{t}"] +
                                    ([f"ktev_{t-1}"] if t >= 1 else []))
                                   if hp == 0 and par == 0 else []),
                            ms=(f"bil_{t}" if hp == HP - 1 and par == 1
                                else None))
                S.add("pe", lambda e: e.nop(), waits=[f"bil_{t}"],
                    ms=f"bil_done_{t}", fence=True)
                kview = kt_sb[:, :].rearrange("p (hp n) -> p hp n", n=seq)[
                    :, :, t * 128:(t + 1) * 128]
                S.add("act", lambda e, kview=kview: e.activation(
                    kview[:, 0:8:2, :],
                    P2[:, :].rearrange("p (sl c) -> p sl c", c=128), AF.Copy),
                    waits=[f"bil_done_{t}"])
                S.add("act", lambda e, kview=kview: e.activation(
                    kview[:, 1:8:2, :],
                    P3[:, :].rearrange("p (sl c) -> p sl c", c=128), AF.Copy),
                    ms=f"ktev_{t}")

            stageAload(0)
            stageAload(1)
            stageAload(2)
            stageA(0)
            stageA(1)
            for t in range(NT):
                stageB(t, 1)
                if t >= 1:
                    stageB(t - 1, 2)
                stageB(t, 12)
                if t >= 1:
                    stageB(t - 1, 25)
                stageB(t, 15)
                if t + 3 < NT:
                    stageAload(t + 3)
                if t + 2 < NT:
                    stageA(t + 2)
            stageB(NT - 1, 2)
            stageB(NT - 1, 25)

            store_streams = ["dSq0", "dSq1", "dSk0", "dSk1", "dSv0", "dSv1"]
            store_totals = {k: S.dma_counts.get(k, 0) for k in store_streams}
            for k in store_streams:
                S.add("act", lambda e, k=k: e.wait_ge(sems[k], store_totals[k]),
                      raw=True)
            for eng in ENGINES:
                S.add(eng, lambda e: e.nop().then_inc(sems["bar"], 1), raw=True)

            _lim1 = os.environ.get("KDBG_P1LIMIT")
            _lim1 = int(_lim1) if _lim1 else None
            with nc.Block() as blockA:
                blockA.tensor(S.replay("pe", sems, _lim1))
                blockA.scalar(S.replay("act", sems, _lim1))
                blockA.vector(S.replay("dve", sems, _lim1))
                blockA.gpsimd(S.replay("pool", sems, _lim1))
                blockA.sync(S.replay("sp", sems, _lim1))
            print(f"[kernel] phase1 ops: {len(S.ops)} (limit={_lim1})", flush=True)
            ph1_store_totals = dict(store_totals)
            ph1_dma_counts = dict(S.dma_counts)
            ph1_counts = dict(S.counts)

        with contextlib.ExitStack() as p2:
            v_sb = p2.enter_context(nc.sbuf_tensor("v_sb", [128, NT * VROW], BF16))
            wo_sb = p2.enter_context(nc.sbuf_tensor("wo_sb", [128, 8 * DM], BF16))
            wft_sb = p2.enter_context(nc.sbuf_tensor("wft_sb", [128, 8 * DM], FP8))
            qT_t = p2.enter_context(nc.sbuf_tensor("qT_t", [128, HP * 512], BF16))
            ET = p2.enter_context(nc.sbuf_tensor("ET", [128, 2048], BF16))
            pvE = p2.enter_context(nc.sbuf_tensor("pvE", [128, 1024], BF16))
            aoT = [p2.enter_context(nc.sbuf_tensor(f"aoT{i}", [128, HP * 512], BF16)) for i in range(2)]
            rlv = p2.enter_context(nc.sbuf_tensor("rlv", [128, 8], F32))
            xs0 = p2.enter_context(nc.sbuf_tensor("xs0", [128, DM], F32))
            xnf = p2.enter_context(nc.sbuf_tensor("xnf", [128, DM], F32))
            qtmp = p2.enter_context(nc.sbuf_tensor("qtmp", [128, 512], F32))
            qtmp2 = xnf
            xqT = p2.enter_context(nc.sbuf_tensor("xqT", [128, DM], FP8))
            yv_sb = p2.enter_context(nc.sbuf_tensor("yv_sb", [128, DM], F32))
            gbuf = p2.enter_context(nc.sbuf_tensor("gbuf", [128, DM], BF16))
            ao_sb = p2.enter_context(nc.sbuf_tensor("ao_sb", [128, 512], BF16))
            bias_bc = p2.enter_context(nc.sbuf_tensor("bias_bc", [128, DM], F32))
            srow = p2.enter_context(nc.sbuf_tensor("srow", [1, 8], F32))
            srow2 = p2.enter_context(nc.sbuf_tensor("srow2", [2, 1], F32))
            negc_bc = p2.enter_context(nc.sbuf_tensor("negc_bc", [128, 1], F32))
            k2run = p2.enter_context(nc.sbuf_tensor("k2run", [2, 1], F32))
            sel = p2.enter_context(nc.sbuf_tensor("sel", [128, 2], BF16))

            PA = p2.enter_context(nc.psum_tensor("PA", [128, 1024], F32))
            PB = p2.enter_context(nc.psum_tensor("PB", [128, 1024], F32))
            PVa = p2.enter_context(nc.psum_tensor("PVa", [128, 512], F32))
            PVb = p2.enter_context(nc.psum_tensor("PVb", [128, 512], F32))
            PT2 = p2.enter_context(nc.psum_tensor("PT2", [128, 1024], BF16))
            PS4 = p2.enter_context(nc.psum_tensor("PS4", [128, 512], F32))

            S = Sched()
            NBAR = len(ENGINES)
            for eng in ENGINES:
                S.add(eng, lambda e: e.wait_ge(sems["bar"], NBAR), raw=True)
            for k, v in ph1_store_totals.items():
                S.add("sp", lambda e, k=k, v=v: e.wait_ge(sems[k], v), raw=True)
            S.dma_counts.update(ph1_dma_counts)
            S.counts.update(ph1_counts)
            S.base = dict(ph1_counts)

            S.add("act", lambda e: e.activation(
                misc[0:1, 15:16], misc[0:1, MC_EPS:MC_EPS + 1], AF.Sqrt))
            VHALF = NT * VROW // 2
            for sl in range(2):
                S.add("sp", lambda e, sl=sl: e.dma_start(
                    v_sb[:, sl * VHALF:(sl + 1) * VHALF],
                    v_hbm[:, sl * VHALF:(sl + 1) * VHALF]),
                    dma=("dSv0", "dSv1")[sl], ms=f"ld_v{sl}")
            S.add("sp", lambda e: e.dma_start(bias_bc[0:1, :], bias_h[:, :]),
                dma="dP2", ms="ld_bias_i")
            S.add("sp", lambda e: e.dma_start(srow[0:1, 0:1], is_h[:, :]),
                dma="dP2", ms="ld_is_i")
            S.add("sp", lambda e: e.dma_start(srow[0:1, 1:2], ws_h[:, :]),
                dma="dP2", ms="p2_pre")
            S.add("pool", lambda e: e.dma_start(
                wo_sb[:, :].rearrange("p (fc o) -> p fc o", o=DM),
                wo_h[:, :].rearrange("(fc p) o -> p fc o", p=128)),
                dma="dG", ms="ld_wo")
            for oc in range(8):
                slab = xs0 if oc % 2 == 0 else yv_sb
                S.add("sp", lambda e, oc=oc, slab=slab: e.dma_start(
                    slab[:], wf_h[oc * 128:(oc + 1) * 128, :]),
                    waits=([f"wft_{oc-2}"] if oc >= 2 else []),
                    dma=("dW0" if oc % 2 == 0 else "dW1"), ms=f"ld_wf{oc}")
                for b in range(8):
                    pvw = PVa if b < 4 else PVb
                    S.add("pe", lambda e, b=b, slab=slab, pvw=pvw: e.transpose(
                        pvw[:, (b % 4) * 128:(b % 4 + 1) * 128],
                        slab[:, b * 128:(b + 1) * 128], id_f32[:]),
                        waits=([f"ld_wf{oc}"] +
                               ([f"wfc_{oc-1}"] if oc >= 1 else [])
                               if b == 0 else []),
                        ms=(f"wft_{oc}" if b == 7 else None))
                for kk, pvw in ((0, PVa), (1, PVb)):
                    S.add("act", lambda e, oc=oc, kk=kk, pvw=pvw: e.activation(
                        wft_sb[:, :].rearrange("p (fc oc o) -> p fc oc o",
                                               oc=8, o=128)[:, kk * 4:
                                                            (kk + 1) * 4, oc, :],
                        pvw[:, :].rearrange("p (b o) -> p b o", o=128),
                        AF.Copy),
                        waits=([f"wft_{oc}"] if kk == 0 else []),
                        ms=(f"wfc_{oc}" if kk == 1 else None))
            for nn in range(2):
                S.add("pe", lambda e, nn=nn: e.matmul(
                    PB[:, nn * 512:(nn + 1) * 512], ones_r[:, :],
                    bias_bc[0:1, nn * 512:(nn + 1) * 512], start=True, stop=True),
                    waits=["p2_pre"], ms=("bbc_mm" if nn == 1 else None))
            S.add("dve", lambda e: e.tensor_copy(bias_bc[:], PB[:, :]),
                waits=["bbc_mm"], ms="bbc_done")
            S.add("dve", lambda e: e.tensor_tensor(
                srow[0:1, 2:3], srow[0:1, 0:1], srow[0:1, 1:2], ALU.mult),
                waits=["p2_pre"], ms="isws_row")
            S.add("dve", lambda e: e.reciprocal(srow[0:1, 3:4], srow[0:1, 0:1]),
                ms="invis_row")
            S.add("pe", lambda e: e.matmul(
                PS4[:, 0:1], ones_r[:, :], srow[0:1, 2:3], start=True, stop=True),
                waits=["isws_row"], ms="isws_mm")
            S.add("pe", lambda e: e.matmul(
                PS4[:, 1:2], ones_r[:, :], srow[0:1, 3:4], start=True, stop=True),
                waits=["invis_row"], ms="invis_mm")
            S.add("dve", lambda e: e.tensor_copy(
                misc[:, MC_ISWS:MC_ISWS + 1], PS4[:, 0:1]),
                waits=["invis_mm"], ms="isws_done")
            S.add("dve", lambda e: e.tensor_copy(
                misc[:, MC_INVIS:MC_INVIS + 1], PS4[:, 1:2]),
                ms="invis_done")
            S.add("dve", lambda e: e.memset(sel[:], 0.0))
            S.add("dve", lambda e: e.memset(sel[0:64, 0:1], 1.0))
            S.add("dve", lambda e: e.memset(sel[64:128, 1:2], 1.0), ms="sel_done")
            S.add("dve", lambda e: e.memset(k2run[:], 0.0), ms="k2_init")
            prev_red = None
            prev_mm = None
            HLEN = min(2048, seq)
            NJ = HLEN // 512
            NHALF = seq // HLEN
            for hp in range(HP):
                for half in range(NHALF):
                    kh = f"{hp}_{half}"
                    w_sq = []
                    if prev_mm is not None:
                        w_sq.append(prev_mm)
                    S.add("dve", lambda e, hp=hp, half=half: e.tensor_tensor(
                        ET[:, 0:HLEN],
                        kt_sb[:, hp * seq + half * HLEN:
                              hp * seq + (half + 1) * HLEN],
                        kt_sb[:, hp * seq + half * HLEN:
                              hp * seq + (half + 1) * HLEN],
                        ALU.mult),
                        waits=w_sq, ms=f"k2sq_{kh}")
                    for j in range(NJ):
                        w_mm = []
                        if j == 0:
                            w_mm = [f"k2sq_{kh}", "sel_done"]
                            if prev_red is None:
                                w_mm += ["isws_done", "invis_done"]
                            else:
                                w_mm.append(prev_red)
                        else:
                            w_mm = [f"k2r_{kh}_{j-1}"]
                        gname = f"k2mm_{kh}_{j}"
                        S.add("pe", lambda e, j=j: e.matmul(
                            PS4[0:2, 0:512],
                            sel[:, :], ET[:, j * 512:(j + 1) * 512],
                            start=True, stop=True),
                            waits=w_mm, ms=gname)
                        prev_mm = gname
                        S.add("dve", lambda e: e.tensor_reduce(
                            srow2[:, :], PS4[0:2, 0:512],
                            AX.X, ALU.max),
                            waits=[gname])
                        rname = (f"k2red_{kh}" if j == NJ - 1
                                 else f"k2r_{kh}_{j}")
                        S.add("dve", lambda e: e.tensor_tensor(
                            k2run[:], k2run[:], srow2[:, :], ALU.max),
                            waits=["k2_init"], ms=rname)
                        prev_red = rname
            S.add("pe", lambda e: e.transpose(
                PS4[0:1, 0:2], k2run[:, :], id_f32[0:2, 0:2]),
                waits=[prev_red], ms="k2tr")
            S.add("dve", lambda e: e.tensor_reduce(
                srow[0:1, 4:5], PS4[0:1, 0:2], AX.X, ALU.max),
                waits=["k2tr"], ms="k2s_done")
            S.add("dve", lambda e: e.tensor_reduce(
                misc[:, MC_T1:MC_T1 + 1], qmax_run[:], AX.X, ALU.max),
                ms="qm1")
            S.add("pe", lambda e: e.transpose(
                PS4[0:1, 4:132], misc[:, MC_T1:MC_T1 + 1], id_f32[:]),
                waits=["qm1", "k2s_done"], ms="qmtr")
            S.add("dve", lambda e: e.tensor_reduce(
                srow[0:1, 5:6], PS4[0:1, 4:132], AX.X, ALU.max),
                waits=["qmtr"], ms="q2s_done")
            S.add("act", lambda e: e.activation(
                srow[0:1, 6:7], srow[0:1, 4:5], AF.Sqrt),
                waits=["k2s_done"], ms="ks_done")
            S.add("act", lambda e: e.activation(
                srow[0:1, 7:8], srow[0:1, 5:6], AF.Sqrt),
                waits=["q2s_done"], ms="qs_done")
            S.add("act", lambda e: e.activation(
                misc[0:1, 15:16], misc[0:1, MC_EPS:MC_EPS + 1], AF.Exp),
                waits=["qs_done"])
            S.add("dve", lambda e: e.tensor_tensor(
                srow[0:1, 4:5], srow[0:1, 6:7], srow[0:1, 7:8], ALU.mult),
                waits=["ks_done", "qs_done"])
            S.add("dve", lambda e: e.tensor_scalar(
                srow[0:1, 4:5], srow[0:1, 4:5], -1.02, None, ALU.mult),
                ms="negc_row")
            S.add("pe", lambda e: e.matmul(
                PS4[:, 2:3], ones_r[:, :], srow[0:1, 4:5], start=True, stop=True),
                waits=["negc_row"], ms="negc_mm")
            S.add("dve", lambda e: e.tensor_copy(negc_bc[:], PS4[:, 2:3]),
                waits=["negc_mm"], ms="negc_done")

            def epi_steps(ti, hp):
                pp = f"{ti}_{hp}"
                steps = []
                for k_, pv in ((0, PVa), (1, PVb)):
                    w_ev = [f"pv_{pp}_{NCH-1}"] if k_ == 0 else []
                    steps.append(("dve", lambda e, k_=k_, pv=pv: e.tensor_copy(
                        pvE[:, k_ * 260:(k_ + 1) * 260], pv[:, 0:260]),
                        w_ev, (f"pvev_{pp}" if k_ == 1 else None), None, False))
                steps.append(("dve", lambda e: e.reciprocal(
                    rlv[:, 0:8], pvE[:, 64:520:65]), [], None, None, False))
                for blk in range(8):
                    steps.append(("dve", lambda e, blk=blk: e.tensor_scalar(
                        ao_sb[:, blk * 64:(blk + 1) * 64],
                        pvE[:, blk * 65: blk * 65 + 64],
                        rlv[:, blk:blk + 1], None, ALU.mult),
                        [], (f"aos_{pp}" if blk == 7 else None), None, False))
                for blk in range(8):
                    w_btr = []
                    if blk == 0:
                        w_btr.append(f"aos_{pp}")
                        if hp >= 1:
                            w_btr.append(f"aoev_{ti}_{hp-1}")
                        elif ti >= 1:
                            w_btr.append(f"aoev_{ti-1}_{HP-1}")
                        else:
                            w_btr.append(f"k2red_{HP-1}_{NHALF-1}")
                    steps.append(("pe", lambda e, blk=blk: e.transpose(
                        PT2[0:64, blk * 128:(blk + 1) * 128],
                        ao_sb[:, blk * 64:(blk + 1) * 64], id_bf[:]),
                        w_btr, (f"btr_{pp}" if blk == 7 else None),
                        None, True))
                ao_ = aoT[ti % 2]
                for k_ in range(2):
                    w_ao = [f"btr_{pp}"] if k_ == 0 else []
                    if k_ == 0 and hp == 0 and ti >= 2:
                        w_ao.append(f"wo_{ti-2}_3")
                    steps.append(("dve", lambda e, k_=k_, hp=hp,
                                  ao_=ao_: e.tensor_copy(
                        ao_[k_ * 64:(k_ + 1) * 64, hp * 512:(hp + 1) * 512],
                        PT2[0:64, k_ * 512:(k_ + 1) * 512]),
                        w_ao, (f"aoev_{pp}" if k_ == 1 else None), None, False))
                return steps

            def sub_steps(ti):
                steps = []
                ao_ = aoT[ti % 2]
                for s_ in range(4):
                    tt = ti * 4 + s_
                    xsb = xs0
                    w_ldx = []
                    if s_ >= 1:
                        w_ldx.append(f"res_{ti}_{s_-1}")
                    elif ti >= 1:
                        w_ldx.append(f"res_{ti-1}_3")
                    else:
                        w_ldx.append("wfc_7")
                    steps.append(("sp", lambda e, tt=tt, xsb=xsb: e.dma_start(
                        xsb[:], x_hbm[tt * 128:(tt + 1) * 128, :]),
                        w_ldx, f"ldxs_{ti}_{s_}", "dX0", False))
                    w_xnf = [f"ldxs_{ti}_{s_}"]
                    if s_ >= 1:
                        w_xnf += [f"xnTb_{ti}_{s_-1}", f"xq_{ti}_{s_-1}"]
                    elif ti >= 1:
                        w_xnf += [f"xnTb_{ti-1}_3", f"xq_{ti-1}_3"]
                    steps.append(("dve", lambda e, tt=tt, xsb=xsb: e.tensor_scalar(
                        xnf[:], xsb[:], rstd1_res[:, tt:tt + 1],
                        nmr1_res[:, tt:tt + 1], ALU.mult, ALU.add),
                        w_xnf, f"xnf_{ti}_{s_}", None, False))
                    for h2 in range(2):
                        w_t = [f"xnf_{ti}_{s_}"]
                        if h2 == 0:
                            if s_ >= 1:
                                w_t.append(f"sgb_{ti}_{s_-1}")
                            elif ti >= 1:
                                w_t.append(f"sgb_{ti-1}_3")
                            else:
                                w_t.append("negc_done")
                        else:
                            w_t.append(f"q1a_{ti}_{s_}")
                        for b in range(4):
                            steps.append(("pe", lambda e, b=b, h2=h2: e.transpose(
                                PS4[:, b * 128:(b + 1) * 128],
                                xnf[:, h2 * 512 + b * 128:
                                    h2 * 512 + (b + 1) * 128], id_f32[:]),
                                (w_t if b == 0 else []),
                                (f"xnT{'ab'[h2]}_{ti}_{s_}" if b == 3 else None),
                                None, True))
                        steps.append(("dve", lambda e, h2=h2: e.tensor_scalar(
                            qtmp[:, :], PS4[:, :],
                            misc[:, MC_INVIS:MC_INVIS + 1],
                            240.0, ALU.mult, ALU.min),
                            [f"xnT{'ab'[h2]}_{ti}_{s_}", "invis_done"],
                            f"q1{'ab'[h2]}_{ti}_{s_}", None, False))
                        steps.append(("dve", lambda e, h2=h2: e.tensor_scalar(
                            qtmp2[:, h2 * 512:(h2 + 1) * 512], qtmp[:, :],
                            -240.0, None, ALU.max),
                            [], f"q2{'ab'[h2]}_{ti}_{s_}", None, False))
                    steps.append(("dve", lambda e: e.tensor_copy(
                        xqT[:], qtmp2[:]),
                        [f"q2b_{ti}_{s_}"], f"xq_{ti}_{s_}", None, False))
                    for nn in range(2):
                        if nn == 0:
                            w_ffn = [f"xq_{ti}_{s_}", "wfc_7", f"q1b_{ti}_{s_}"]
                        else:
                            w_ffn = [f"yva_{ti}_{s_}"]
                        for fc in range(8):
                            steps.append(("pe", lambda e, fc=fc, nn=nn: e.matmul(
                                PS4[:, :],
                                xqT[:, fc * 128:(fc + 1) * 128],
                                wft_sb[:, fc * DM + nn * 512:
                                       fc * DM + (nn + 1) * 512],
                                start=(fc == 0), stop=(fc == 7)),
                                (w_ffn if fc == 0 else []),
                                (f"ffn{'ab'[nn]}_{ti}_{s_}" if fc == 7
                                 else None), None, True))
                        w_yv = [f"ffn{'ab'[nn]}_{ti}_{s_}", "isws_done",
                                "bbc_done"]
                        if nn == 0:
                            if s_ >= 1:
                                w_yv.append(f"sto_{ti}_{s_-1}")
                            elif ti >= 1:
                                w_yv.append(f"sto_{ti-1}_3")
                        steps.append(("dve", lambda e, nn=nn: e.scalar_tensor_tensor(
                            yv_sb[:, nn * 512:(nn + 1) * 512], PS4[:, :],
                            misc[:, MC_ISWS:MC_ISWS + 1],
                            bias_bc[:, nn * 512:(nn + 1) * 512],
                            ALU.mult, ALU.add),
                            w_yv,
                            (f"yva_{ti}_{s_}" if nn == 0 else f"yv_{ti}_{s_}"),
                            None, False))
                    steps.append(("act", lambda e: e.activation(
                        qtmp2[:], yv_sb[:], AF.Tanh, scale=0.5),
                        [f"yv_{ti}_{s_}"], f"th_{ti}_{s_}", None, False))
                    steps.append(("dve", lambda e: e.tensor_scalar(
                        gbuf[:], qtmp2[:], 0.5, 0.5, ALU.mult, ALU.add),
                        [f"th_{ti}_{s_}"], f"sig_{ti}_{s_}", None, False))
                    for nn in range(2):
                        if nn == 0:
                            w_wo = [f"aoev_{ti}_{HP-1}", "ld_wo",
                                    f"yv_{ti}_{s_}"]
                        else:
                            w_wo = [f"u_{ti}_{s_}_0"]
                        for hp in range(HP):
                            steps.append(("pe", lambda e, hp=hp, nn=nn, s_=s_,
                                          ao_=ao_: e.matmul(
                                PS4[:, :],
                                ao_[:, hp * 512 + s_ * 128:
                                    hp * 512 + (s_ + 1) * 128],
                                wo_sb[:, hp * DM + nn * 512:
                                      hp * DM + (nn + 1) * 512],
                                start=(hp == 0), stop=(hp == HP - 1)),
                                (w_wo if hp == 0 else []),
                                (f"wo{'ab'[nn]}_{ti}_{s_}" if hp == HP - 1
                                 else None), None, True))
                        steps.append(("dve", lambda e, nn=nn: e.tensor_tensor(
                            qtmp[:, :], PS4[:, :],
                            gbuf[:, nn * 512:(nn + 1) * 512], ALU.mult),
                            [f"wo{'ab'[nn]}_{ti}_{s_}", f"sig_{ti}_{s_}"],
                            f"u_{ti}_{s_}_{nn}", None, False))
                        steps.append(("dve", lambda e, nn=nn: e.tensor_tensor(
                            gbuf[:, nn * 512:(nn + 1) * 512], qtmp[:, :],
                            yv_sb[:, nn * 512:(nn + 1) * 512], ALU.mult),
                            [], (f"sg_{ti}_{s_}" if nn == 0
                                 else f"sgb_{ti}_{s_}"), None, False))
                    steps.append(("dve", lambda e, xsb=xsb: e.tensor_tensor(
                        yv_sb[:], gbuf[:], xsb[:], ALU.add),
                        [f"sgb_{ti}_{s_}"], f"res_{ti}_{s_}", None, False))
                    steps.append(("sp", lambda e, tt=tt: e.dma_start(
                        out_hbm[tt * 128:(tt + 1) * 128, :], yv_sb[:]),
                        [f"res_{ti}_{s_}"], f"sto_{ti}_{s_}", "dSo", False))
                steps.append(("pe", lambda e: e.nop(),
                              [f"wob_{ti}_3"], f"wo_{ti}_3", None, True))
                return steps

            for ti in range(NT2):
                S.add("sp", lambda e, ti=ti: e.dma_start(
                    qT_t[:],
                    qT_hbm[:, :].rearrange("p (hp n) -> p hp n", n=seq)[
                        :, :, ti * 512:(ti + 1) * 512]),
                    waits=([f"st_{ti-1}_{HP-1}_{NCH-1}"] if ti >= 1 else []),
                    dma="dQ", ms=f"ldq_{ti}")
                pend_sub = sub_steps(ti - 1) if ti >= 1 else []
                for hp in range(HP):
                    pp = f"{ti}_{hp}"
                    if hp >= 1:
                        pend = epi_steps(ti, hp - 1)
                    elif ti >= 1:
                        pend = epi_steps(ti - 1, HP - 1)
                    else:
                        pend = []
                    for c in range(NCH):
                        buf = PA if c % 2 == 0 else PB
                        w_st = [f"ldq_{ti}"] if c == 0 else []
                        if c >= 2:
                            w_st.append(f"exp_{pp}_{c-2}")
                        elif hp >= 1:
                            w_st.append(f"exp_{ti}_{hp-1}_{NCH-2+c}")
                        elif ti >= 1:
                            w_st.append(f"exp_{ti-1}_{HP-1}_{NCH-2+c}")
                        elif c == 1:
                            w_st.append("bbc_done")
                        for par in range(2):
                            b0 = par * 64
                            S.add("pe", lambda e, c=c, hp=hp, b0=b0, par=par,
                                  buf=buf, ti=ti: e.matmul(
                                buf[:, par * 512:(par + 1) * 512],
                                kt_sb[b0:b0 + 64,
                                      hp * seq + c * 128: hp * seq + (c + 1) * 128],
                                qT_t[b0:b0 + 64, hp * 512:(hp + 1) * 512],
                                start=True, stop=True, tile_position=(b0, 0)),
                                waits=(w_st if par == 0 else []),
                                ms=(f"st_{pp}_{c}" if par == 1 else None))
                        w_exp = [f"st_{pp}_{c}"]
                        if c >= 2:
                            w_exp.append(f"pv_{pp}_{c-2}")
                        elif hp >= 1:
                            w_exp.append(f"pv_{ti}_{hp-1}_{NCH-2+c}")
                        elif ti >= 1:
                            w_exp.append(f"pv_{ti-1}_{HP-1}_{NCH-2+c}")
                        if ti == 0 and hp == 0 and c == 0:
                            w_exp.append("negc_done")
                        S.add("act", lambda e, c=c, buf=buf: e.activation(
                            ET[:, (c % 2) * 1024:(c % 2) * 1024 + 1024],
                            buf[:, :], AF.Exp, bias=negc_bc[:, 0:1]),
                            waits=w_exp, ms=f"exp_{pp}_{c}")
                        for _ in range(2):
                            if pend and c >= 1:
                                st_ = pend.pop(0)
                                S.add(st_[0], st_[1], waits=st_[2], ms=st_[3],
                                      dma=st_[4], fence=st_[5])
                        for _ in range(1 if pend else 2):
                            if pend_sub and c >= 1:
                                st_ = pend_sub.pop(0)
                                S.add(st_[0], st_[1], waits=st_[2], ms=st_[3],
                                      dma=st_[4], fence=st_[5])
                        if c >= 1:
                            cc = c - 1
                            for par, pv in ((0, PVa), (1, PVb)):
                                w_pv = [f"exp_{pp}_{cc}"]
                                if ti == 0 and hp == 0 and cc == NCH // 2:
                                    w_pv.append("ld_v1")
                                if cc == 0:
                                    if hp >= 1:
                                        w_pv.append(f"pvev_{ti}_{hp-1}")
                                    elif ti >= 1:
                                        w_pv.append(f"pvev_{ti-1}_{HP-1}")
                                    else:
                                        w_pv.append("ld_v0")
                                        w_pv.append("wfc_7")
                                for sub in range(4):
                                    S.add("pe", lambda e, cc=cc, hp=hp, par=par,
                                          pv=pv, sub=sub: e.matmul(
                                        pv[:, sub * 65: sub * 65 + 65],
                                        ET[:, (cc % 2) * 1024 + par * 512 +
                                           sub * 128:
                                           (cc % 2) * 1024 + par * 512 +
                                           (sub + 1) * 128],
                                        v_sb[:, cc * VROW + (2 * hp + par) * V65:
                                             cc * VROW + (2 * hp + par) * V65 + V65],
                                        start=(cc == 0 and sub == 0),
                                        stop=(cc == NCH - 1 and sub == 3)),
                                        waits=(w_pv if par == 0 and sub == 0
                                               else []),
                                        ms=(f"pv_{pp}_{cc}"
                                            if par == 1 and sub == 3 else None))
                    cc = NCH - 1
                    for par, pv in ((0, PVa), (1, PVb)):
                        for sub in range(4):
                            S.add("pe", lambda e, cc=cc, hp=hp, par=par,
                                  pv=pv, sub=sub: e.matmul(
                                pv[:, sub * 65: sub * 65 + 65],
                                ET[:, (cc % 2) * 1024 + par * 512 + sub * 128:
                                   (cc % 2) * 1024 + par * 512 +
                                   (sub + 1) * 128],
                                v_sb[:, cc * VROW + (2 * hp + par) * V65:
                                     cc * VROW + (2 * hp + par) * V65 + V65],
                                start=False, stop=(sub == 3)),
                                waits=([f"exp_{pp}_{cc}"]
                                       if par == 0 and sub == 0 else []),
                                ms=(f"pv_{pp}_{cc}"
                                    if par == 1 and sub == 3 else None))
                    for st_ in pend:
                        S.add(st_[0], st_[1], waits=st_[2], ms=st_[3],
                              dma=st_[4], fence=st_[5])
                for st_ in pend_sub:
                    S.add(st_[0], st_[1], waits=st_[2], ms=st_[3],
                          dma=st_[4], fence=st_[5])
                if ti == NT2 - 1:
                    for st_ in epi_steps(ti, HP - 1):
                        S.add(st_[0], st_[1], waits=st_[2], ms=st_[3],
                              dma=st_[4], fence=st_[5])
                    for st_ in sub_steps(ti):
                        S.add(st_[0], st_[1], waits=st_[2], ms=st_[3],
                              dma=st_[4], fence=st_[5])
                S.add("pe", lambda e: e.nop(),
                    waits=[f"exp_{ti}_{HP-1}_{NCH-1}"], ms=f"atn_{ti}",
                    fence=True)

            total_dSo = S.dma_counts["dSo"]
            S.add("act", lambda e: e.wait_ge(sems["dSo"], total_dSo), raw=True)
            S.add("sp", lambda e: e.wait_ge(sems["dSo"], total_dSo), raw=True)

            _lim = os.environ.get("KDBG_P2LIMIT")
            _lim = int(_lim) if _lim else None
            with nc.Block() as blockB:
                blockB.tensor(S.replay("pe", sems, _lim))
                blockB.scalar(S.replay("act", sems, _lim))
                blockB.vector(S.replay("dve", sems, _lim))
                blockB.gpsimd(S.replay("pool", sems, _lim))
                blockB.sync(S.replay("sp", sems, _lim))
            print(f"[kernel] phase2 ops: {len(S.ops)} (limit={_lim})", flush=True)

    return nc


_CACHE = {}


def _get_program(seq):
    if seq not in _CACHE:
        _CACHE[seq] = build_program(seq)
    return _CACHE[seq]


def kernel(L0_attn_x, L0_attn_gamma, L0_attn_Wq, L0_attn_Wkv, L0_attn_W_bilinear,
           L0_attn_Wo, L0_ffn_weight_fp8, L0_ffn_bias, L0_ffn_input_scale,
           L0_ffn_weight_scale):
    x = np.ascontiguousarray(np.asarray(L0_attn_x, dtype=np.float32))
    B, seq, dm = x.shape
    assert dm == DM
    nc = _get_program(seq)

    ct, st2 = host_tables(seq)
    wb = np.asarray(L0_attn_W_bilinear, dtype=np.float32)
    wb_packed = np.zeros((128, HP * DH), dtype=np.float32)
    for h in range(H):
        wb_packed[(h % 2) * 64:(h % 2) * 64 + 64,
                  (h // 2) * DH:(h // 2) * DH + DH] = wb[h]
    idn = np.eye(128, dtype=np.float32)

    common = {
        "gamma": np.ascontiguousarray(
            np.asarray(L0_attn_gamma, dtype=np.float32).reshape(1, DM)),
        "wq": np.ascontiguousarray(np.asarray(L0_attn_Wq, dtype=np.float32)),
        "wkv": np.ascontiguousarray(np.asarray(L0_attn_Wkv, dtype=np.float32)),
        "wb": wb_packed,
        "wo": np.ascontiguousarray(np.asarray(L0_attn_Wo, dtype=np.float32)),
        "wf": np.ascontiguousarray(np.asarray(L0_ffn_weight_fp8, dtype=np.float32)),
        "bias": np.ascontiguousarray(
            np.asarray(L0_ffn_bias, dtype=np.float32).reshape(1, DM)),
        "is_": np.asarray(L0_ffn_input_scale, dtype=np.float32).reshape(1, 1),
        "ws": np.asarray(L0_ffn_weight_scale, dtype=np.float32).reshape(1, 1),
        "ct": ct, "st": st2, "idn": idn,
    }
    in_maps = []
    for c in range(B):
        m = dict(common)
        m["x"] = np.ascontiguousarray(x[c])
        in_maps.append(m)

    res = run_bass_kernel_spmd(nc, in_maps, core_ids=list(range(B)))
    out = np.stack([np.asarray(r["out"], dtype=np.float32)
                    for r in res.results], axis=0)
    return out


# version 29
# speedup vs baseline: 1.0461x; 1.0308x over previous
import contextlib
import os
import numpy as np
import ml_dtypes
import concourse.bass as bass
import concourse.mybir as mybir
from concourse.bass_utils import run_bass_kernel_spmd

F32 = mybir.dt.float32
BF16 = mybir.dt.bfloat16
FP8 = mybir.dt.float8e4
AF = mybir.ActivationFunctionType
ALU = mybir.AluOpType
AX = mybir.AxisListType

DM = 1024
H = 16
DH = 64
HP = H // 2
LN_EPS = 1e-5
ROPE_BASE = 10000.0
V65 = DH + 1
VROW = H * V65

ENGINES = ("pe", "act", "dve", "pool", "sp")


class Sched:

    def __init__(self):
        self.ops = []
        self.counts = {e: 0 for e in ENGINES}
        self.dma_counts = {}
        self.ms = {}
        self.base = {}

    SELF_SYNC = ("act", "dve", "pool")

    def add(self, eng, emit, waits=(), ms=None, dma=None, raw=False,
            fence=False):
        waits = tuple(w for w in waits if w is not None)
        for w in waits:
            assert w in self.ms, f"wait on undefined milestone {w}"
        if raw:
            self.ops.append((eng, emit, waits, None, None, True, None))
            return
        if dma is None:
            self.counts[eng] += 1
            if ms is not None:
                self.ms[ms] = (eng, self.counts[eng])
        else:
            self.dma_counts[dma] = self.dma_counts.get(dma, 0) + 16
            if ms is not None:
                self.ms[ms] = (dma, self.dma_counts[dma])
        fence_val = self.counts.get(eng, 0) - (1 if dma is None else 0) \
            if (fence and ms is not None) else None
        self.ops.append((eng, emit, waits, ms, dma, False, fence_val))

    def replay(self, eng_name, sems, limit=None):
        src = self.ops if limit is None else self.ops[:limit]
        ops = [o for o in src if o[0] == eng_name]
        self_sync = eng_name in self.SELF_SYNC

        def f(e):
            waited = {}
            n_done = self.base.get(eng_name, 0)
            for (_, emit, waits, ms, dma, raw, fence_val) in ops:
                for w in waits:
                    sk, val = self.ms[w]
                    if sk == eng_name and self_sync:
                        continue
                    if waited.get(sk, -1) >= val:
                        continue
                    e.wait_ge(sems[sk], val)
                    waited[sk] = val
                if not raw and self_sync and n_done > 0:
                    e.wait_ge(sems[eng_name], n_done)
                elif fence_val is not None and fence_val > 0 and not self_sync:
                    e.wait_ge(sems[eng_name], fence_val)
                ins = emit(e)
                if raw:
                    continue
                if dma is not None:
                    ins.then_inc(sems[dma], 16)
                else:
                    ins.then_inc(sems[eng_name], 1)
                    n_done += 1
        return f


def host_tables(seq):
    theta = 1.0 / ROPE_BASE ** (np.arange(0, DH, 2, dtype=np.float64) / DH)
    pos = np.arange(seq, dtype=np.float64)
    fr = pos[:, None] * theta[None, :]
    emb = np.concatenate([fr, fr], axis=-1)
    cos = np.cos(emb)
    sin = np.sin(emb)
    ct = cos.astype(np.float32)
    st2 = np.empty_like(ct)
    st2[:, 0::2] = sin[:, 1::2].astype(np.float32)
    st2[:, 1::2] = (-sin[:, 0::2]).astype(np.float32)
    return ct, st2


def build_program(seq):
    NT = seq // 128
    NT2 = seq // 512
    NCH = seq // 128
    NSL = seq // 512

    nc = bass.Bass("TRN2", target_bir_lowering=False)

    x_hbm = nc.dram_tensor("x", [seq, DM], F32, kind="ExternalInput")
    gamma_h = nc.dram_tensor("gamma", [1, DM], F32, kind="ExternalInput")
    wq_h = nc.dram_tensor("wq", [DM, DM], F32, kind="ExternalInput")
    wkv_h = nc.dram_tensor("wkv", [DM, 2 * DM], F32, kind="ExternalInput")
    wb_h = nc.dram_tensor("wb", [128, HP * DH], F32, kind="ExternalInput")
    wo_h = nc.dram_tensor("wo", [DM, DM], F32, kind="ExternalInput")
    wf_h = nc.dram_tensor("wf", [DM, DM], F32, kind="ExternalInput")
    bias_h = nc.dram_tensor("bias", [1, DM], F32, kind="ExternalInput")
    is_h = nc.dram_tensor("is_", [1, 1], F32, kind="ExternalInput")
    ws_h = nc.dram_tensor("ws", [1, 1], F32, kind="ExternalInput")
    ct_h = nc.dram_tensor("ct", [seq, DH], F32, kind="ExternalInput")
    st_h = nc.dram_tensor("st", [seq, DH], F32, kind="ExternalInput")
    id_h = nc.dram_tensor("idn", [128, 128], F32, kind="ExternalInput")
    out_hbm = nc.dram_tensor("out", [seq, DM], F32, kind="ExternalOutput")

    qT_hbm = nc.dram_tensor("qT_s", [128, HP * seq], BF16, kind="Internal")
    v_hbm = nc.dram_tensor("v_s", [128, NT * VROW], BF16, kind="Internal")

    with contextlib.ExitStack() as stack:
        id_f32 = stack.enter_context(nc.sbuf_tensor("id_f32", [128, 128], F32))
        id_bf = stack.enter_context(nc.sbuf_tensor("id_bf", [128, 128], BF16))
        ones_r = stack.enter_context(nc.sbuf_tensor("ones_r", [1, 128], F32))
        rstd1_res = stack.enter_context(nc.sbuf_tensor("rstd1", [128, NT], F32))
        nmr1_res = stack.enter_context(nc.sbuf_tensor("nmr1", [128, NT], F32))
        qmax_run = stack.enter_context(nc.sbuf_tensor("qmaxr", [128, H], F32))
        misc = stack.enter_context(nc.sbuf_tensor("misc", [128, 16], F32))
        kt_sb = stack.enter_context(nc.sbuf_tensor("kt_sb", [128, HP * seq], BF16))
        (MC_SUM, MC_SSQ, MC_M, MC_T1, MC_T2, MC_T3, MC_R2, MC_N2,
         MC_ISWS, MC_INVIS, MC_EPS) = range(11)

        sems = {}
        for name in ("pe", "act", "dve", "pool", "sp", "bar", "dP", "dP0", "dG",
                     "dX0", "dX1", "dX2", "dSq0", "dSq1", "dSk0", "dSk1", "dSv0", "dSv1", "dP2", "dW0", "dW1",
                     "dQ", "dSo"):
            sems[name] = stack.enter_context(nc.semaphore("sem_" + name))

        with contextlib.ExitStack() as p1:
            wq_sb = p1.enter_context(nc.sbuf_tensor("wq_sb", [128, 8 * DM], BF16))
            wkv_sb = p1.enter_context(nc.sbuf_tensor("wkv_sb", [128, 16 * DM], BF16))
            wb_sb = p1.enter_context(nc.sbuf_tensor("wb_sb", [128, HP * DH], BF16))
            ct_sb = p1.enter_context(nc.sbuf_tensor("ct_sb", [128, NT * DH], F32))
            st_sb = p1.enter_context(nc.sbuf_tensor("st_sb", [128, NT * DH], F32))
            gamma_bc = p1.enter_context(nc.sbuf_tensor("gamma_bc", [128, DM], F32))
            grow = p1.enter_context(nc.sbuf_tensor("grow", [1, DM], F32))
            xbl = [p1.enter_context(nc.sbuf_tensor(f"xb{i}", [128, DM], F32)) for i in range(3)]
            xnn_sb = [p1.enter_context(nc.sbuf_tensor(f"xnn_sb{i}", [128, DM], F32)) for i in range(2)]
            sqdump = p1.enter_context(nc.sbuf_tensor("sqdump", [128, DM], BF16))
            xn2_sb = [p1.enter_context(nc.sbuf_tensor(f"xn2_sb{i}", [128, DM], BF16)) for i in range(2)]
            xn2T = [p1.enter_context(nc.sbuf_tensor(f"xn2T{i}", [128, DM], BF16)) for i in range(2)]
            ropeA = [p1.enter_context(nc.sbuf_tensor(f"ropeA{i}", [128, DM], BF16)) for i in range(2)]
            ropeB = [p1.enter_context(nc.sbuf_tensor(f"ropeB{i}", [128, DM], BF16)) for i in range(2)]
            qr_sb = [p1.enter_context(nc.sbuf_tensor(f"qr_sb{i}", [128, DM], BF16)) for i in range(2)]
            kr_sb = [p1.enter_context(nc.sbuf_tensor(f"kr_sb{i}", [128, DM], BF16)) for i in range(2)]
            qTs = [p1.enter_context(nc.sbuf_tensor(f"qTs{i}", [128, DM], BF16)) for i in range(2)]
            kTs = [p1.enter_context(nc.sbuf_tensor(f"kTs{i}", [128, DM], BF16)) for i in range(2)]
            vst = [p1.enter_context(nc.sbuf_tensor(f"vst{i}", [128, VROW], BF16)) for i in range(2)]
            qn_sb = p1.enter_context(nc.sbuf_tensor("qn_sb", [128, H], F32))
            stat = [p1.enter_context(nc.sbuf_tensor(f"stat{i}", [128, 16], F32)) for i in range(2)]

            P0 = p1.enter_context(nc.psum_tensor("P0", [128, 1024], F32))
            P1 = p1.enter_context(nc.psum_tensor("P1", [128, 1024], F32))
            P2 = p1.enter_context(nc.psum_tensor("P2", [128, 512], F32))
            P3 = p1.enter_context(nc.psum_tensor("P3", [128, 512], F32))
            PT = p1.enter_context(nc.psum_tensor("PT", [128, 1024], BF16))
            PTB = p1.enter_context(nc.psum_tensor("PTB", [128, 1024], BF16))

            S = Sched()

            S.add("dve", lambda e: e.memset(misc[0:1, 14:15], 0.0),
                ms="scr_init")
            S.add("act", lambda e: e.activation(
                misc[0:1, 15:16], misc[0:1, 14:15], AF.Square),
                waits=["scr_init"])
            S.add("pool", lambda e: e.dma_start(
                wq_sb[:, :].rearrange("p (fc o) -> p fc o", o=DM),
                wq_h[:, :].rearrange("(fc p) o -> p fc o", p=128)),
                dma="dG", ms="ld_wq_i")
            S.add("pool", lambda e: e.dma_start(
                wkv_sb[:, :].rearrange("p (fc o) -> p fc o", o=2 * DM),
                wkv_h[:, :].rearrange("(fc p) o -> p fc o", p=128)),
                dma="dG", ms="ld_wkv_i")
            S.add("pool", lambda e: e.dma_start(wb_sb[:], wb_h[:, :]),
                dma="dG", ms="wload")
            S.add("sp", lambda e: e.dma_start(id_f32[:], id_h[:, :]),
                dma="dP0", ms="ld_id_i")
            S.add("sp", lambda e: e.dma_start(grow[:], gamma_h[:, :]),
                dma="dP0", ms="p1_pre")
            S.add("sp", lambda e: e.dma_start(
                ct_sb[:, :].rearrange("p (t d) -> p t d", d=DH),
                ct_h[:, :].rearrange("(t p) d -> p t d", p=128)),
                dma="dP", ms="ld_ct_i")
            S.add("sp", lambda e: e.dma_start(
                st_sb[:, :].rearrange("p (t d) -> p t d", d=DH),
                st_h[:, :].rearrange("(t p) d -> p t d", p=128)),
                dma="dP", ms="tab_ld")
            S.add("dve", lambda e: e.memset(ones_r[:], 1.0), ms="ones_done")
            S.add("dve", lambda e: e.memset(
                misc[:, MC_EPS:MC_EPS + 1], LN_EPS), ms="eps_done")
            S.add("dve", lambda e: e.memset(vst[0][:], 1.0))
            S.add("dve", lambda e: e.memset(vst[1][:], 1.0), ms="vst_ones")
            S.add("dve", lambda e: e.memset(qmax_run[:], 0.0), ms="qmax_init")
            S.add("act", lambda e: e.activation(id_bf[:], id_f32[:], AF.Copy),
                waits=["p1_pre"], ms="idbf_done")
            S.add("dve", lambda e: e.tensor_scalar(
                wq_sb[:], wq_sb[:], 0.125, None, ALU.mult),
                waits=["wload"], ms="wq_scaled")
            for nn in range(2):
                S.add("pe", lambda e, nn=nn: e.matmul(
                    P0[:, nn * 512:(nn + 1) * 512], ones_r[:, :],
                    grow[0:1, nn * 512:(nn + 1) * 512], start=True, stop=True),
                    waits=["ones_done", "p1_pre"],
                    ms=("gbc_mm" if nn == 1 else None))
            S.add("dve", lambda e: e.tensor_copy(gamma_bc[:], P0[:, :]),
                waits=["gbc_mm"], ms="gbc_done")

            def stageAload(t):
                xb = xbl[t % 3]
                S.add("sp", lambda e, t=t, xb=xb: e.dma_start(
                    xb[:], x_hbm[t * 128:(t + 1) * 128, :]),
                    waits=([f"xnn_{t-3}", f"rs1_{t-3}"] if t >= 3 else []),
                    dma=("dX0", "dX1", "dX2")[t % 3], ms=f"ldx_{t}")

            def stageAearly(t):
                p = t % 2
                xb = xbl[t % 3]
                ST = stat[p]
                xnn_ = xnn_sb[p]; xn2_ = xn2_sb[p]
                S.add("dve", lambda e, xb=xb, ST=ST: e.reduce_sum(
                    ST[:, MC_SUM:MC_SUM + 1], xb[:], AX.X),
                    waits=[f"ldx_{t}"], ms=f"rs1_{t}")
                S.add("act", lambda e, xb=xb, ST=ST: e.activation(
                    sqdump[:], xb[:], AF.Square,
                    accum_out=ST[:, MC_SSQ:MC_SSQ + 1]),
                    waits=[f"ldx_{t}"] + ([f"mic_{t-2}"] if t >= 2 else []),
                    ms=f"sq1_{t}")
            def stageA(t):
                p = t % 2
                xb = xbl[t % 3]
                ST = stat[p]
                xnn_ = xnn_sb[p]; xn2_ = xn2_sb[p]
                S.add("dve", lambda e, ST=ST: e.tensor_scalar(
                    ST[:, MC_M:MC_M + 1], ST[:, MC_SUM:MC_SUM + 1],
                    1.0 / DM, None, ALU.mult), waits=[f"sq1_{t}"])
                S.add("dve", lambda e, ST=ST: e.tensor_tensor(
                    ST[:, MC_T1:MC_T1 + 1], ST[:, MC_M:MC_M + 1],
                    ST[:, MC_M:MC_M + 1], ALU.mult))
                S.add("dve", lambda e, ST=ST: e.tensor_scalar(
                    ST[:, MC_T2:MC_T2 + 1], ST[:, MC_SSQ:MC_SSQ + 1],
                    1.0 / DM, None, ALU.mult))
                S.add("dve", lambda e, ST=ST: e.tensor_tensor(
                    ST[:, MC_T2:MC_T2 + 1], ST[:, MC_T2:MC_T2 + 1],
                    ST[:, MC_T1:MC_T1 + 1], ALU.subtract), ms=f"var1_{t}")
                S.add("act", lambda e, ST=ST: e.activation(
                    ST[:, MC_T3:MC_T3 + 1], ST[:, MC_T2:MC_T2 + 1],
                    AF.Sqrt, bias=misc[:, MC_EPS:MC_EPS + 1]),
                    waits=[f"var1_{t}", "eps_done"], ms=f"sd1_{t}")
                S.add("dve", lambda e, t=t, ST=ST: e.reciprocal(
                    rstd1_res[:, t:t + 1], ST[:, MC_T3:MC_T3 + 1]),
                    waits=[f"sd1_{t}"])
                S.add("dve", lambda e, t=t, ST=ST: e.tensor_scalar(
                    nmr1_res[:, t:t + 1], ST[:, MC_M:MC_M + 1],
                    rstd1_res[:, t:t + 1], -1.0, ALU.mult, ALU.mult))
                S.add("dve", lambda e, ST=ST: e.tensor_tensor(
                    ST[:, MC_T1:MC_T1 + 1], ST[:, MC_T3:MC_T3 + 1],
                    ST[:, MC_T3:MC_T3 + 1], ALU.mult))
                S.add("dve", lambda e, ST=ST: e.reciprocal(
                    ST[:, MC_T1:MC_T1 + 1], ST[:, MC_T1:MC_T1 + 1]))
                S.add("dve", lambda e, ST=ST: e.tensor_scalar(
                    ST[:, MC_T1:MC_T1 + 1], ST[:, MC_T2:MC_T2 + 1],
                    ST[:, MC_T1:MC_T1 + 1], None, ALU.mult),
                    ms=f"v2_{t}")
                S.add("act", lambda e, ST=ST: e.activation(
                    ST[:, MC_T1:MC_T1 + 1], ST[:, MC_T1:MC_T1 + 1],
                    AF.Sqrt, bias=misc[:, MC_EPS:MC_EPS + 1]),
                    waits=[f"v2_{t}"], ms=f"sd2_{t}")
                S.add("dve", lambda e, ST=ST: e.reciprocal(
                    ST[:, MC_T1:MC_T1 + 1], ST[:, MC_T1:MC_T1 + 1]),
                    waits=[f"sd2_{t}"])
                S.add("dve", lambda e, t=t, ST=ST: e.tensor_scalar(
                    ST[:, MC_R2:MC_R2 + 1], rstd1_res[:, t:t + 1],
                    ST[:, MC_T1:MC_T1 + 1], None, ALU.mult))
                S.add("dve", lambda e, ST=ST: e.tensor_scalar(
                    ST[:, MC_N2:MC_N2 + 1], ST[:, MC_M:MC_M + 1],
                    ST[:, MC_R2:MC_R2 + 1], -1.0, ALU.mult, ALU.mult),
                    ms=f"mic_{t}")
                S.add("act", lambda e, xb=xb, xnn_=xnn_, ST=ST: e.activation(
                    xnn_[:], xb[:], AF.Identity,
                    bias=ST[:, MC_N2:MC_N2 + 1],
                    scale=ST[:, MC_R2:MC_R2 + 1]),
                    waits=[f"mic_{t}"] + ([f"xn2_{t-2}"] if t >= 2 else []),
                    ms=f"xnn_{t}")
                S.add("pool", lambda e, xnn_=xnn_, xn2_=xn2_: e.tensor_tensor(
                    xn2_[:], xnn_[:], gamma_bc[:], ALU.mult),
                    waits=[f"xnn_{t}", "gbc_done"] +
                          ([f"tr1_{t-2}"] if t >= 2 else []),
                    ms=f"xn2_{t}")

            def stageB(t, part):
                p = t % 2
                x2T = xn2T[p]; rA = ropeA[p]; rB = ropeB[p]
                qr_ = qr_sb[p]; kr_ = kr_sb[p]
                qT_ = qTs[p]; kT_ = kTs[p]; vs_ = vst[p]
                xn2_ = xn2_sb[p]
                ctv = ct_sb[:, t * DH:(t + 1) * DH].rearrange(
                    "p (o d) -> p o d", o=1).to_broadcast([128, H, DH])
                stv = st_sb[:, t * DH:(t + 1) * DH].rearrange(
                    "p (o d) -> p o d", o=1).to_broadcast([128, H, DH])

                def rope(src_ps, pev, ev_ms, dst, mm_ms, out_ms, guard):
                    S.add("dve", lambda e, src_ps=src_ps, ctv=ctv,
                          rA=rA: e.tensor_tensor(
                        rA[:, :].rearrange("p (h d) -> p h d", d=DH),
                        src_ps[:, :].rearrange("p (h d) -> p h d", d=DH),
                        ctv, ALU.mult),
                        waits=[mm_ms, "p1_pre", "tab_ld"] + guard)
                    S.add("dve", lambda e, src_ps=src_ps, stv=stv,
                          rB=rB: e.tensor_tensor(
                        rB[:, :].rearrange("p (h d) -> p h d", d=DH),
                        src_ps[:, :].rearrange("p (h d) -> p h d", d=DH),
                        stv, ALU.mult),
                        ms=ev_ms)
                    ra = rA[:, :].rearrange("p (n two) -> p n two", two=2)
                    rb = rB[:, :].rearrange("p (n two) -> p n two", two=2)
                    dd = dst[:, :].rearrange("p (n two) -> p n two", two=2)
                    S.add("dve", lambda e, ra=ra, rb=rb, dd=dd: e.tensor_tensor(
                        dd[:, :, 0:1], ra[:, :, 0:1], rb[:, :, 1:2], ALU.add))
                    S.add("dve", lambda e, ra=ra, rb=rb, dd=dd: e.tensor_tensor(
                        dd[:, :, 1:2], ra[:, :, 1:2], rb[:, :, 0:1], ALU.add),
                        ms=out_ms)

                if part == 1:
                    for b in range(8):
                        S.add("pe", lambda e, b=b, xn2_=xn2_: e.transpose(
                            PTB[:, b * 128:(b + 1) * 128],
                            xn2_[:, b * 128:(b + 1) * 128], id_bf[:]),
                            waits=(["idbf_done", f"xn2_{t}"] +
                                   ([f"ev2_{t-1}"] if b == 0 and t >= 1 else [])),
                            ms=(f"tr1_{t}" if b == 7 else None))
                    S.add("act", lambda e, x2T=x2T: e.activation(
                        x2T[:], PTB[:, :], AF.Copy),
                        waits=[f"tr1_{t}"] + ([f"vmm_{t-2}"] if t >= 2 else []),
                        ms=f"ev1_{t}")
                    return
                if part == 12:
                    for fc in range(8):
                        for nn in range(2):
                            S.add("pe", lambda e, fc=fc, nn=nn, x2T=x2T: e.matmul(
                                P0[:, nn * 512:(nn + 1) * 512],
                                x2T[:, fc * 128:(fc + 1) * 128],
                                wq_sb[:, fc * DM + nn * 512: fc * DM + (nn + 1) * 512],
                                start=(fc == 0), stop=(fc == 7)),
                                waits=([f"ev1_{t}", "wq_scaled"] +
                                       ([f"evq_{t-1}"] if t >= 1 else []) +
                                       (["gbc_done"] if t == 0 else [])
                                       if fc == 0 and nn == 0 else []),
                                ms=(f"qmm_{t}" if fc == 7 and nn == 1 else None))
                    for fc in range(8):
                        for nn in range(2):
                            S.add("pe", lambda e, fc=fc, nn=nn, x2T=x2T: e.matmul(
                                P1[:, nn * 512:(nn + 1) * 512],
                                x2T[:, fc * 128:(fc + 1) * 128],
                                wkv_sb[:, fc * 2 * DM + nn * 512:
                                       fc * 2 * DM + (nn + 1) * 512],
                                start=(fc == 0), stop=(fc == 7)),
                                waits=((["wload"] +
                                        ([f"evk_{t-1}"] if t >= 1 else []))
                                       if fc == 0 and nn == 0 else []),
                                ms=(f"kmm_{t}" if fc == 7 and nn == 1 else None))
                    rope(P0, None, f"evq_{t}", qr_, f"qmm_{t}", f"ropq_{t}",
                         [f"tr2_{t-2}"] if t >= 2 else [])
                    S.add("dve", lambda e, qr_=qr_, rA=rA: e.tensor_tensor(
                        rA[:], qr_[:], qr_[:], ALU.mult),
                        waits=[f"ropq_{t}"])
                    S.add("dve", lambda e, rA=rA: e.tensor_reduce(
                        qn_sb[:], rA[:, :].rearrange("p (h d) -> p h d", d=DH),
                        AX.X, ALU.add))
                    S.add("dve", lambda e: e.tensor_tensor(
                        qmax_run[:], qmax_run[:], qn_sb[:], ALU.max),
                        waits=["qmax_init"], ms=f"qn_{t}")
                    return
                if part == 15:
                    for b in range(8):
                        S.add("pe", lambda e, b=b, qr_=qr_: e.transpose(
                            PT[:, b * 128:(b + 1) * 128],
                            qr_[:, b * 128:(b + 1) * 128], id_bf[:]),
                            waits=([f"ropq_{t}"] +
                                   ([f"ev3_{t-1}"] if t >= 1 else [])
                                   if b == 0 else []),
                            ms=(f"tr2_{t}" if b == 7 else None))
                    S.add("act", lambda e, qT_=qT_: e.activation(
                        qT_[:], PT[:, :], AF.Copy),
                        waits=[f"tr2_{t}"] + ([f"stq_{t-2}"] if t >= 2 else []),
                        ms=f"ev2_{t}")
                    S.add("sp", lambda e, t=t, qT_=qT_: e.dma_start(
                        qT_hbm[:, :].rearrange("p (hp n) -> p hp n", n=seq)[
                            :, :, t * 128:(t + 1) * 128],
                        qT_[:, :].rearrange("p (hp n) -> p hp n", n=128)),
                        waits=[f"ev2_{t}"], dma=("dSq0" if t % 2 == 0 else "dSq1"), ms=f"stq_{t}")
                    rope(P1, None, f"evk_{t}", kr_, f"kmm_{t}", f"ropk_{t}",
                         [f"tr3_{t-2}"] if t >= 2 else [])
                    return
                if part == 2:
                    for fc in range(8):
                        for ps, nn in ((P2, 2), (P3, 3)):
                            S.add("pe", lambda e, fc=fc, nn=nn, ps=ps, x2T=x2T: e.matmul(
                                ps[:, :],
                                x2T[:, fc * 128:(fc + 1) * 128],
                                wkv_sb[:, fc * 2 * DM + nn * 512:
                                       fc * 2 * DM + (nn + 1) * 512],
                                start=(fc == 0), stop=(fc == 7)),
                                waits=(([f"ktev_{t-1}"] if t >= 1 else [])
                                       if fc == 0 and nn == 2 else []),
                                ms=(f"vmm_{t}" if fc == 7 and nn == 3 else None))
                    for b in range(8):
                        S.add("pe", lambda e, b=b, kr_=kr_: e.transpose(
                            PT[:, b * 128:(b + 1) * 128],
                            kr_[:, b * 128:(b + 1) * 128], id_bf[:]),
                            waits=([f"ropk_{t}", f"ev2_{t}"] if b == 0 else []),
                            ms=(f"tr3_{t}" if b == 7 else None))
                    S.add("act", lambda e, kT_=kT_: e.activation(
                        kT_[:], PT[:, :], AF.Copy),
                        waits=[f"tr3_{t}"] +
                              ([f"bil_done_{t-2}"] if t >= 2 else []),
                        ms=f"ev3_{t}")
                    vout = vs_[:, :].rearrange("p (h e) -> p h e", e=V65)
                    S.add("act", lambda e, vout=vout: e.activation(
                        vout[:, 0:8, 0:DH],
                        P2[:, :].rearrange("p (h d) -> p h d", d=DH), AF.Copy),
                        waits=[f"vmm_{t}", "vst_ones"] +
                              ([f"stv_{t-2}"] if t >= 2 else []))
                    S.add("act", lambda e, vout=vout: e.activation(
                        vout[:, 8:16, 0:DH],
                        P3[:, :].rearrange("p (h d) -> p h d", d=DH), AF.Copy),
                        ms=f"vev_{t}")
                    S.add("sp", lambda e, t=t, vs_=vs_: e.dma_start(
                        v_hbm[:, t * VROW:(t + 1) * VROW], vs_[:]),
                        waits=[f"vev_{t}"], dma=("dSv0" if t % 2 == 0 else "dSv1"),
                        ms=f"stv_{t}")
                    return
                for hp in range(HP):
                    pbk = P2 if hp % 2 == 0 else P3
                    slot = hp // 2
                    for par in range(2):
                        b0 = par * 64
                        S.add("pe", lambda e, hp=hp, b0=b0, slot=slot,
                              pbk=pbk, kT_=kT_: e.matmul(
                            pbk[b0:b0 + 64, slot * 128:(slot + 1) * 128],
                            wb_sb[b0:b0 + 64, hp * DH:(hp + 1) * DH],
                            kT_[b0:b0 + 64, hp * 128:(hp + 1) * 128],
                            start=True, stop=True, tile_position=(b0, b0)),
                            waits=(([f"ev3_{t}", "wload", f"vev_{t}"] +
                                    ([f"ktev_{t-1}"] if t >= 1 else []))
                                   if hp == 0 and par == 0 else []),
                            ms=(f"bil_{t}" if hp == HP - 1 and par == 1
                                else None))
                S.add("pe", lambda e: e.nop(), waits=[f"bil_{t}"],
                    ms=f"bil_done_{t}", fence=True)
                kview = kt_sb[:, :].rearrange("p (hp n) -> p hp n", n=seq)[
                    :, :, t * 128:(t + 1) * 128]
                S.add("act", lambda e, kview=kview: e.activation(
                    kview[:, 0:8:2, :],
                    P2[:, :].rearrange("p (sl c) -> p sl c", c=128), AF.Copy),
                    waits=[f"bil_done_{t}"])
                S.add("act", lambda e, kview=kview: e.activation(
                    kview[:, 1:8:2, :],
                    P3[:, :].rearrange("p (sl c) -> p sl c", c=128), AF.Copy),
                    ms=f"ktev_{t}")

            stageAload(0)
            stageAload(1)
            stageAload(2)
            stageAearly(0)
            stageA(0)
            stageAearly(1)
            stageA(1)
            for t in range(NT):
                if t + 2 < NT:
                    stageAearly(t + 2)
                stageB(t, 1)
                if t >= 1:
                    stageB(t - 1, 2)
                if t + 2 < NT:
                    stageA(t + 2)
                stageB(t, 12)
                if t >= 1:
                    stageB(t - 1, 25)
                stageB(t, 15)
                if t + 3 < NT:
                    stageAload(t + 3)
            stageB(NT - 1, 2)
            stageB(NT - 1, 25)

            store_streams = ["dSq0", "dSq1", "dSk0", "dSk1", "dSv0", "dSv1"]
            store_totals = {k: S.dma_counts.get(k, 0) for k in store_streams}
            for k in store_streams:
                S.add("act", lambda e, k=k: e.wait_ge(sems[k], store_totals[k]),
                      raw=True)
            for eng in ENGINES:
                S.add(eng, lambda e: e.nop().then_inc(sems["bar"], 1), raw=True)

            _lim1 = os.environ.get("KDBG_P1LIMIT")
            _lim1 = int(_lim1) if _lim1 else None
            with nc.Block() as blockA:
                blockA.tensor(S.replay("pe", sems, _lim1))
                blockA.scalar(S.replay("act", sems, _lim1))
                blockA.vector(S.replay("dve", sems, _lim1))
                blockA.gpsimd(S.replay("pool", sems, _lim1))
                blockA.sync(S.replay("sp", sems, _lim1))
            print(f"[kernel] phase1 ops: {len(S.ops)} (limit={_lim1})", flush=True)
            ph1_store_totals = dict(store_totals)
            ph1_dma_counts = dict(S.dma_counts)
            ph1_counts = dict(S.counts)

        with contextlib.ExitStack() as p2:
            v_sb = p2.enter_context(nc.sbuf_tensor("v_sb", [128, NT * VROW], BF16))
            wo_sb = p2.enter_context(nc.sbuf_tensor("wo_sb", [128, 8 * DM], BF16))
            wft_sb = p2.enter_context(nc.sbuf_tensor("wft_sb", [128, 8 * DM], FP8))
            qT_t = p2.enter_context(nc.sbuf_tensor("qT_t", [128, HP * 512], BF16))
            ET = p2.enter_context(nc.sbuf_tensor("ET", [128, 2048], BF16))
            pvE = p2.enter_context(nc.sbuf_tensor("pvE", [128, 1024], BF16))
            aoT = [p2.enter_context(nc.sbuf_tensor(f"aoT{i}", [128, HP * 512], BF16)) for i in range(2)]
            rlv = p2.enter_context(nc.sbuf_tensor("rlv", [128, 8], F32))
            xs0 = p2.enter_context(nc.sbuf_tensor("xs0", [128, DM], F32))
            xnf = p2.enter_context(nc.sbuf_tensor("xnf", [128, DM], F32))
            qtmp = p2.enter_context(nc.sbuf_tensor("qtmp", [128, 512], F32))
            qtmp2 = xnf
            xqT = p2.enter_context(nc.sbuf_tensor("xqT", [128, DM], FP8))
            yv_sb = p2.enter_context(nc.sbuf_tensor("yv_sb", [128, DM], F32))
            gbuf = p2.enter_context(nc.sbuf_tensor("gbuf", [128, DM], BF16))
            ao_sb = p2.enter_context(nc.sbuf_tensor("ao_sb", [128, 512], BF16))
            bias_bc = p2.enter_context(nc.sbuf_tensor("bias_bc", [128, DM], F32))
            srow = p2.enter_context(nc.sbuf_tensor("srow", [1, 8], F32))
            srow2 = p2.enter_context(nc.sbuf_tensor("srow2", [2, 1], F32))
            negc_bc = p2.enter_context(nc.sbuf_tensor("negc_bc", [128, 1], F32))
            k2run = p2.enter_context(nc.sbuf_tensor("k2run", [2, 1], F32))
            sel = p2.enter_context(nc.sbuf_tensor("sel", [128, 2], BF16))

            PA = p2.enter_context(nc.psum_tensor("PA", [128, 1024], F32))
            PB = p2.enter_context(nc.psum_tensor("PB", [128, 1024], F32))
            PVa = p2.enter_context(nc.psum_tensor("PVa", [128, 512], F32))
            PVb = p2.enter_context(nc.psum_tensor("PVb", [128, 512], F32))
            PT2 = p2.enter_context(nc.psum_tensor("PT2", [128, 1024], BF16))
            PS4 = p2.enter_context(nc.psum_tensor("PS4", [128, 512], F32))

            S = Sched()
            NBAR = len(ENGINES)
            for eng in ENGINES:
                S.add(eng, lambda e: e.wait_ge(sems["bar"], NBAR), raw=True)
            for k, v in ph1_store_totals.items():
                S.add("sp", lambda e, k=k, v=v: e.wait_ge(sems[k], v), raw=True)
            S.dma_counts.update(ph1_dma_counts)
            S.counts.update(ph1_counts)
            S.base = dict(ph1_counts)

            S.add("act", lambda e: e.activation(
                misc[0:1, 15:16], misc[0:1, MC_EPS:MC_EPS + 1], AF.Sqrt))
            VHALF = NT * VROW // 2
            for sl in range(2):
                S.add("sp", lambda e, sl=sl: e.dma_start(
                    v_sb[:, sl * VHALF:(sl + 1) * VHALF],
                    v_hbm[:, sl * VHALF:(sl + 1) * VHALF]),
                    dma=("dSv0", "dSv1")[sl], ms=f"ld_v{sl}")
            S.add("sp", lambda e: e.dma_start(bias_bc[0:1, :], bias_h[:, :]),
                dma="dP2", ms="ld_bias_i")
            S.add("sp", lambda e: e.dma_start(srow[0:1, 0:1], is_h[:, :]),
                dma="dP2", ms="ld_is_i")
            S.add("sp", lambda e: e.dma_start(srow[0:1, 1:2], ws_h[:, :]),
                dma="dP2", ms="p2_pre")
            S.add("pool", lambda e: e.dma_start(
                wo_sb[:, :].rearrange("p (fc o) -> p fc o", o=DM),
                wo_h[:, :].rearrange("(fc p) o -> p fc o", p=128)),
                dma="dG", ms="ld_wo")
            for oc in range(8):
                slab = xs0 if oc % 2 == 0 else yv_sb
                S.add("sp", lambda e, oc=oc, slab=slab: e.dma_start(
                    slab[:], wf_h[oc * 128:(oc + 1) * 128, :]),
                    waits=([f"wft_{oc-2}"] if oc >= 2 else []),
                    dma=("dW0" if oc % 2 == 0 else "dW1"), ms=f"ld_wf{oc}")
                for b in range(8):
                    pvw = PVa if b < 4 else PVb
                    S.add("pe", lambda e, b=b, slab=slab, pvw=pvw: e.transpose(
                        pvw[:, (b % 4) * 128:(b % 4 + 1) * 128],
                        slab[:, b * 128:(b + 1) * 128], id_f32[:]),
                        waits=([f"ld_wf{oc}"] +
                               ([f"wfc_{oc-1}"] if oc >= 1 else [])
                               if b == 0 else []),
                        ms=(f"wft_{oc}" if b == 7 else None))
                for kk, pvw in ((0, PVa), (1, PVb)):
                    S.add("act", lambda e, oc=oc, kk=kk, pvw=pvw: e.activation(
                        wft_sb[:, :].rearrange("p (fc oc o) -> p fc oc o",
                                               oc=8, o=128)[:, kk * 4:
                                                            (kk + 1) * 4, oc, :],
                        pvw[:, :].rearrange("p (b o) -> p b o", o=128),
                        AF.Copy),
                        waits=([f"wft_{oc}"] if kk == 0 else []),
                        ms=(f"wfc_{oc}" if kk == 1 else None))
            for nn in range(2):
                S.add("pe", lambda e, nn=nn: e.matmul(
                    PB[:, nn * 512:(nn + 1) * 512], ones_r[:, :],
                    bias_bc[0:1, nn * 512:(nn + 1) * 512], start=True, stop=True),
                    waits=["p2_pre"], ms=("bbc_mm" if nn == 1 else None))
            S.add("dve", lambda e: e.tensor_copy(bias_bc[:], PB[:, :]),
                waits=["bbc_mm"], ms="bbc_done")
            S.add("dve", lambda e: e.tensor_tensor(
                srow[0:1, 2:3], srow[0:1, 0:1], srow[0:1, 1:2], ALU.mult),
                waits=["p2_pre"], ms="isws_row")
            S.add("dve", lambda e: e.reciprocal(srow[0:1, 3:4], srow[0:1, 0:1]),
                ms="invis_row")
            S.add("pe", lambda e: e.matmul(
                PS4[:, 0:1], ones_r[:, :], srow[0:1, 2:3], start=True, stop=True),
                waits=["isws_row"], ms="isws_mm")
            S.add("pe", lambda e: e.matmul(
                PS4[:, 1:2], ones_r[:, :], srow[0:1, 3:4], start=True, stop=True),
                waits=["invis_row"], ms="invis_mm")
            S.add("dve", lambda e: e.tensor_copy(
                misc[:, MC_ISWS:MC_ISWS + 1], PS4[:, 0:1]),
                waits=["invis_mm"], ms="isws_done")
            S.add("dve", lambda e: e.tensor_copy(
                misc[:, MC_INVIS:MC_INVIS + 1], PS4[:, 1:2]),
                ms="invis_done")
            S.add("dve", lambda e: e.memset(sel[:], 0.0))
            S.add("dve", lambda e: e.memset(sel[0:64, 0:1], 1.0))
            S.add("dve", lambda e: e.memset(sel[64:128, 1:2], 1.0), ms="sel_done")
            S.add("dve", lambda e: e.memset(k2run[:], 0.0), ms="k2_init")
            prev_red = None
            prev_mm = None
            HLEN = min(2048, seq)
            NJ = HLEN // 512
            NHALF = seq // HLEN
            for hp in range(HP):
                for half in range(NHALF):
                    kh = f"{hp}_{half}"
                    w_sq = []
                    if prev_mm is not None:
                        w_sq.append(prev_mm)
                    S.add("dve", lambda e, hp=hp, half=half: e.tensor_tensor(
                        ET[:, 0:HLEN],
                        kt_sb[:, hp * seq + half * HLEN:
                              hp * seq + (half + 1) * HLEN],
                        kt_sb[:, hp * seq + half * HLEN:
                              hp * seq + (half + 1) * HLEN],
                        ALU.mult),
                        waits=w_sq, ms=f"k2sq_{kh}")
                    for j in range(NJ):
                        w_mm = []
                        if j == 0:
                            w_mm = [f"k2sq_{kh}", "sel_done"]
                            if prev_red is None:
                                w_mm += ["isws_done", "invis_done"]
                            else:
                                w_mm.append(prev_red)
                        else:
                            w_mm = [f"k2r_{kh}_{j-1}"]
                        gname = f"k2mm_{kh}_{j}"
                        S.add("pe", lambda e, j=j: e.matmul(
                            PS4[0:2, 0:512],
                            sel[:, :], ET[:, j * 512:(j + 1) * 512],
                            start=True, stop=True),
                            waits=w_mm, ms=gname)
                        prev_mm = gname
                        S.add("dve", lambda e: e.tensor_reduce(
                            srow2[:, :], PS4[0:2, 0:512],
                            AX.X, ALU.max),
                            waits=[gname])
                        rname = (f"k2red_{kh}" if j == NJ - 1
                                 else f"k2r_{kh}_{j}")
                        S.add("dve", lambda e: e.tensor_tensor(
                            k2run[:], k2run[:], srow2[:, :], ALU.max),
                            waits=["k2_init"], ms=rname)
                        prev_red = rname
            S.add("pe", lambda e: e.transpose(
                PS4[0:1, 0:2], k2run[:, :], id_f32[0:2, 0:2]),
                waits=[prev_red], ms="k2tr")
            S.add("dve", lambda e: e.tensor_reduce(
                srow[0:1, 4:5], PS4[0:1, 0:2], AX.X, ALU.max),
                waits=["k2tr"], ms="k2s_done")
            S.add("dve", lambda e: e.tensor_reduce(
                misc[:, MC_T1:MC_T1 + 1], qmax_run[:], AX.X, ALU.max),
                ms="qm1")
            S.add("pe", lambda e: e.transpose(
                PS4[0:1, 4:132], misc[:, MC_T1:MC_T1 + 1], id_f32[:]),
                waits=["qm1", "k2s_done"], ms="qmtr")
            S.add("dve", lambda e: e.tensor_reduce(
                srow[0:1, 5:6], PS4[0:1, 4:132], AX.X, ALU.max),
                waits=["qmtr"], ms="q2s_done")
            S.add("act", lambda e: e.activation(
                srow[0:1, 6:7], srow[0:1, 4:5], AF.Sqrt),
                waits=["k2s_done"], ms="ks_done")
            S.add("act", lambda e: e.activation(
                srow[0:1, 7:8], srow[0:1, 5:6], AF.Sqrt),
                waits=["q2s_done"], ms="qs_done")
            S.add("act", lambda e: e.activation(
                misc[0:1, 15:16], misc[0:1, MC_EPS:MC_EPS + 1], AF.Exp),
                waits=["qs_done"])
            S.add("dve", lambda e: e.tensor_tensor(
                srow[0:1, 4:5], srow[0:1, 6:7], srow[0:1, 7:8], ALU.mult),
                waits=["ks_done", "qs_done"])
            S.add("dve", lambda e: e.tensor_scalar(
                srow[0:1, 4:5], srow[0:1, 4:5], -1.02, None, ALU.mult),
                ms="negc_row")
            S.add("pe", lambda e: e.matmul(
                PS4[:, 2:3], ones_r[:, :], srow[0:1, 4:5], start=True, stop=True),
                waits=["negc_row"], ms="negc_mm")
            S.add("dve", lambda e: e.tensor_copy(negc_bc[:], PS4[:, 2:3]),
                waits=["negc_mm"], ms="negc_done")

            def epi_steps(ti, hp):
                pp = f"{ti}_{hp}"
                steps = []
                for k_, pv in ((0, PVa), (1, PVb)):
                    w_ev = [f"pv_{pp}_{NCH-1}"] if k_ == 0 else []
                    steps.append(("dve", lambda e, k_=k_, pv=pv: e.tensor_copy(
                        pvE[:, k_ * 260:(k_ + 1) * 260], pv[:, 0:260]),
                        w_ev, (f"pvev_{pp}" if k_ == 1 else None), None, False))
                steps.append(("dve", lambda e: e.reciprocal(
                    rlv[:, 0:8], pvE[:, 64:520:65]), [], None, None, False))
                for blk in range(8):
                    steps.append(("dve", lambda e, blk=blk: e.tensor_scalar(
                        ao_sb[:, blk * 64:(blk + 1) * 64],
                        pvE[:, blk * 65: blk * 65 + 64],
                        rlv[:, blk:blk + 1], None, ALU.mult),
                        [], (f"aos_{pp}" if blk == 7 else None), None, False))
                for blk in range(8):
                    w_btr = []
                    if blk == 0:
                        w_btr.append(f"aos_{pp}")
                        if hp >= 1:
                            w_btr.append(f"aoev_{ti}_{hp-1}")
                        elif ti >= 1:
                            w_btr.append(f"aoev_{ti-1}_{HP-1}")
                        else:
                            w_btr.append(f"k2red_{HP-1}_{NHALF-1}")
                    steps.append(("pe", lambda e, blk=blk: e.transpose(
                        PT2[0:64, blk * 128:(blk + 1) * 128],
                        ao_sb[:, blk * 64:(blk + 1) * 64], id_bf[:]),
                        w_btr, (f"btr_{pp}" if blk == 7 else None),
                        None, True))
                ao_ = aoT[ti % 2]
                for k_ in range(2):
                    w_ao = [f"btr_{pp}"] if k_ == 0 else []
                    if k_ == 0 and hp == 0 and ti >= 2:
                        w_ao.append(f"wo_{ti-2}_3")
                    steps.append(("dve", lambda e, k_=k_, hp=hp,
                                  ao_=ao_: e.tensor_copy(
                        ao_[k_ * 64:(k_ + 1) * 64, hp * 512:(hp + 1) * 512],
                        PT2[0:64, k_ * 512:(k_ + 1) * 512]),
                        w_ao, (f"aoev_{pp}" if k_ == 1 else None), None, False))
                return steps

            def sub_steps(ti):
                steps = []
                ao_ = aoT[ti % 2]
                for s_ in range(4):
                    tt = ti * 4 + s_
                    xsb = xs0
                    w_ldx = []
                    if s_ >= 1:
                        w_ldx.append(f"res_{ti}_{s_-1}")
                    elif ti >= 1:
                        w_ldx.append(f"res_{ti-1}_3")
                    else:
                        w_ldx.append("wfc_7")
                    steps.append(("sp", lambda e, tt=tt, xsb=xsb: e.dma_start(
                        xsb[:], x_hbm[tt * 128:(tt + 1) * 128, :]),
                        w_ldx, f"ldxs_{ti}_{s_}", "dX0", False))
                    w_xnf = [f"ldxs_{ti}_{s_}"]
                    if s_ >= 1:
                        w_xnf += [f"xnTb_{ti}_{s_-1}", f"xq_{ti}_{s_-1}"]
                    elif ti >= 1:
                        w_xnf += [f"xnTb_{ti-1}_3", f"xq_{ti-1}_3"]
                    steps.append(("dve", lambda e, tt=tt, xsb=xsb: e.tensor_scalar(
                        xnf[:], xsb[:], rstd1_res[:, tt:tt + 1],
                        nmr1_res[:, tt:tt + 1], ALU.mult, ALU.add),
                        w_xnf, f"xnf_{ti}_{s_}", None, False))
                    for h2 in range(2):
                        w_t = [f"xnf_{ti}_{s_}"]
                        if h2 == 0:
                            if s_ >= 1:
                                w_t.append(f"sgb_{ti}_{s_-1}")
                            elif ti >= 1:
                                w_t.append(f"sgb_{ti-1}_3")
                            else:
                                w_t.append("negc_done")
                        else:
                            w_t.append(f"q1a_{ti}_{s_}")
                        for b in range(4):
                            steps.append(("pe", lambda e, b=b, h2=h2: e.transpose(
                                PS4[:, b * 128:(b + 1) * 128],
                                xnf[:, h2 * 512 + b * 128:
                                    h2 * 512 + (b + 1) * 128], id_f32[:]),
                                (w_t if b == 0 else []),
                                (f"xnT{'ab'[h2]}_{ti}_{s_}" if b == 3 else None),
                                None, True))
                        steps.append(("dve", lambda e, h2=h2: e.tensor_scalar(
                            qtmp[:, :], PS4[:, :],
                            misc[:, MC_INVIS:MC_INVIS + 1],
                            240.0, ALU.mult, ALU.min),
                            [f"xnT{'ab'[h2]}_{ti}_{s_}", "invis_done"],
                            f"q1{'ab'[h2]}_{ti}_{s_}", None, False))
                        steps.append(("dve", lambda e, h2=h2: e.tensor_scalar(
                            qtmp2[:, h2 * 512:(h2 + 1) * 512], qtmp[:, :],
                            -240.0, None, ALU.max),
                            [], f"q2{'ab'[h2]}_{ti}_{s_}", None, False))
                    steps.append(("dve", lambda e: e.tensor_copy(
                        xqT[:], qtmp2[:]),
                        [f"q2b_{ti}_{s_}"], f"xq_{ti}_{s_}", None, False))
                    for nn in range(2):
                        if nn == 0:
                            w_ffn = [f"xq_{ti}_{s_}", "wfc_7", f"q1b_{ti}_{s_}"]
                        else:
                            w_ffn = [f"yva_{ti}_{s_}"]
                        for fc in range(8):
                            steps.append(("pe", lambda e, fc=fc, nn=nn: e.matmul(
                                PS4[:, :],
                                xqT[:, fc * 128:(fc + 1) * 128],
                                wft_sb[:, fc * DM + nn * 512:
                                       fc * DM + (nn + 1) * 512],
                                start=(fc == 0), stop=(fc == 7)),
                                (w_ffn if fc == 0 else []),
                                (f"ffn{'ab'[nn]}_{ti}_{s_}" if fc == 7
                                 else None), None, True))
                        w_yv = [f"ffn{'ab'[nn]}_{ti}_{s_}", "isws_done",
                                "bbc_done"]
                        if nn == 0:
                            if s_ >= 1:
                                w_yv.append(f"sto_{ti}_{s_-1}")
                            elif ti >= 1:
                                w_yv.append(f"sto_{ti-1}_3")
                        steps.append(("dve", lambda e, nn=nn: e.scalar_tensor_tensor(
                            yv_sb[:, nn * 512:(nn + 1) * 512], PS4[:, :],
                            misc[:, MC_ISWS:MC_ISWS + 1],
                            bias_bc[:, nn * 512:(nn + 1) * 512],
                            ALU.mult, ALU.add),
                            w_yv,
                            (f"yva_{ti}_{s_}" if nn == 0 else f"yv_{ti}_{s_}"),
                            None, False))
                    steps.append(("act", lambda e: e.activation(
                        qtmp2[:], yv_sb[:], AF.Tanh, scale=0.5),
                        [f"yv_{ti}_{s_}"], f"th_{ti}_{s_}", None, False))
                    steps.append(("dve", lambda e: e.tensor_scalar(
                        gbuf[:], qtmp2[:], 0.5, 0.5, ALU.mult, ALU.add),
                        [f"th_{ti}_{s_}"], f"sig_{ti}_{s_}", None, False))
                    for nn in range(2):
                        if nn == 0:
                            w_wo = [f"aoev_{ti}_{HP-1}", "ld_wo",
                                    f"yv_{ti}_{s_}"]
                        else:
                            w_wo = [f"u_{ti}_{s_}_0"]
                        for hp in range(HP):
                            steps.append(("pe", lambda e, hp=hp, nn=nn, s_=s_,
                                          ao_=ao_: e.matmul(
                                PS4[:, :],
                                ao_[:, hp * 512 + s_ * 128:
                                    hp * 512 + (s_ + 1) * 128],
                                wo_sb[:, hp * DM + nn * 512:
                                      hp * DM + (nn + 1) * 512],
                                start=(hp == 0), stop=(hp == HP - 1)),
                                (w_wo if hp == 0 else []),
                                (f"wo{'ab'[nn]}_{ti}_{s_}" if hp == HP - 1
                                 else None), None, True))
                        steps.append(("dve", lambda e, nn=nn: e.tensor_tensor(
                            qtmp[:, :], PS4[:, :],
                            gbuf[:, nn * 512:(nn + 1) * 512], ALU.mult),
                            [f"wo{'ab'[nn]}_{ti}_{s_}", f"sig_{ti}_{s_}"],
                            f"u_{ti}_{s_}_{nn}", None, False))
                        steps.append(("dve", lambda e, nn=nn: e.tensor_tensor(
                            gbuf[:, nn * 512:(nn + 1) * 512], qtmp[:, :],
                            yv_sb[:, nn * 512:(nn + 1) * 512], ALU.mult),
                            [], (f"sg_{ti}_{s_}" if nn == 0
                                 else f"sgb_{ti}_{s_}"), None, False))
                    steps.append(("dve", lambda e, xsb=xsb: e.tensor_tensor(
                        yv_sb[:], gbuf[:], xsb[:], ALU.add),
                        [f"sgb_{ti}_{s_}"], f"res_{ti}_{s_}", None, False))
                    steps.append(("sp", lambda e, tt=tt: e.dma_start(
                        out_hbm[tt * 128:(tt + 1) * 128, :], yv_sb[:]),
                        [f"res_{ti}_{s_}"], f"sto_{ti}_{s_}", "dSo", False))
                steps.append(("pe", lambda e: e.nop(),
                              [f"wob_{ti}_3"], f"wo_{ti}_3", None, True))
                return steps

            for ti in range(NT2):
                S.add("sp", lambda e, ti=ti: e.dma_start(
                    qT_t[:],
                    qT_hbm[:, :].rearrange("p (hp n) -> p hp n", n=seq)[
                        :, :, ti * 512:(ti + 1) * 512]),
                    waits=([f"st_{ti-1}_{HP-1}_{NCH-1}"] if ti >= 1 else []),
                    dma="dQ", ms=f"ldq_{ti}")
                pend_sub = sub_steps(ti - 1) if ti >= 1 else []
                for hp in range(HP):
                    pp = f"{ti}_{hp}"
                    if hp >= 1:
                        pend = epi_steps(ti, hp - 1)
                    elif ti >= 1:
                        pend = epi_steps(ti - 1, HP - 1)
                    else:
                        pend = []
                    for c in range(NCH):
                        buf = PA if c % 2 == 0 else PB
                        w_st = [f"ldq_{ti}"] if c == 0 else []
                        if c >= 2:
                            w_st.append(f"exp_{pp}_{c-2}")
                        elif hp >= 1:
                            w_st.append(f"exp_{ti}_{hp-1}_{NCH-2+c}")
                        elif ti >= 1:
                            w_st.append(f"exp_{ti-1}_{HP-1}_{NCH-2+c}")
                        elif c == 1:
                            w_st.append("bbc_done")
                        for par in range(2):
                            b0 = par * 64
                            S.add("pe", lambda e, c=c, hp=hp, b0=b0, par=par,
                                  buf=buf, ti=ti: e.matmul(
                                buf[:, par * 512:(par + 1) * 512],
                                kt_sb[b0:b0 + 64,
                                      hp * seq + c * 128: hp * seq + (c + 1) * 128],
                                qT_t[b0:b0 + 64, hp * 512:(hp + 1) * 512],
                                start=True, stop=True, tile_position=(b0, 0)),
                                waits=(w_st if par == 0 else []),
                                ms=(f"st_{pp}_{c}" if par == 1 else None))
                        w_exp = [f"st_{pp}_{c}"]
                        if c >= 2:
                            w_exp.append(f"pv_{pp}_{c-2}")
                        elif hp >= 1:
                            w_exp.append(f"pv_{ti}_{hp-1}_{NCH-2+c}")
                        elif ti >= 1:
                            w_exp.append(f"pv_{ti-1}_{HP-1}_{NCH-2+c}")
                        if ti == 0 and hp == 0 and c == 0:
                            w_exp.append("negc_done")
                        S.add("act", lambda e, c=c, buf=buf: e.activation(
                            ET[:, (c % 2) * 1024:(c % 2) * 1024 + 1024],
                            buf[:, :], AF.Exp, bias=negc_bc[:, 0:1]),
                            waits=w_exp, ms=f"exp_{pp}_{c}")
                        for _ in range(2):
                            if pend and c >= 1:
                                st_ = pend.pop(0)
                                S.add(st_[0], st_[1], waits=st_[2], ms=st_[3],
                                      dma=st_[4], fence=st_[5])
                        for _ in range(1 if pend else 2):
                            if pend_sub and c >= 1:
                                st_ = pend_sub.pop(0)
                                S.add(st_[0], st_[1], waits=st_[2], ms=st_[3],
                                      dma=st_[4], fence=st_[5])
                        if c >= 1:
                            cc = c - 1
                            for par, pv in ((0, PVa), (1, PVb)):
                                w_pv = [f"exp_{pp}_{cc}"]
                                if ti == 0 and hp == 0 and cc == NCH // 2:
                                    w_pv.append("ld_v1")
                                if cc == 0:
                                    if hp >= 1:
                                        w_pv.append(f"pvev_{ti}_{hp-1}")
                                    elif ti >= 1:
                                        w_pv.append(f"pvev_{ti-1}_{HP-1}")
                                    else:
                                        w_pv.append("ld_v0")
                                        w_pv.append("wfc_7")
                                for sub in range(4):
                                    S.add("pe", lambda e, cc=cc, hp=hp, par=par,
                                          pv=pv, sub=sub: e.matmul(
                                        pv[:, sub * 65: sub * 65 + 65],
                                        ET[:, (cc % 2) * 1024 + par * 512 +
                                           sub * 128:
                                           (cc % 2) * 1024 + par * 512 +
                                           (sub + 1) * 128],
                                        v_sb[:, cc * VROW + (2 * hp + par) * V65:
                                             cc * VROW + (2 * hp + par) * V65 + V65],
                                        start=(cc == 0 and sub == 0),
                                        stop=(cc == NCH - 1 and sub == 3)),
                                        waits=(w_pv if par == 0 and sub == 0
                                               else []),
                                        ms=(f"pv_{pp}_{cc}"
                                            if par == 1 and sub == 3 else None))
                    cc = NCH - 1
                    for par, pv in ((0, PVa), (1, PVb)):
                        for sub in range(4):
                            S.add("pe", lambda e, cc=cc, hp=hp, par=par,
                                  pv=pv, sub=sub: e.matmul(
                                pv[:, sub * 65: sub * 65 + 65],
                                ET[:, (cc % 2) * 1024 + par * 512 + sub * 128:
                                   (cc % 2) * 1024 + par * 512 +
                                   (sub + 1) * 128],
                                v_sb[:, cc * VROW + (2 * hp + par) * V65:
                                     cc * VROW + (2 * hp + par) * V65 + V65],
                                start=False, stop=(sub == 3)),
                                waits=([f"exp_{pp}_{cc}"]
                                       if par == 0 and sub == 0 else []),
                                ms=(f"pv_{pp}_{cc}"
                                    if par == 1 and sub == 3 else None))
                    for st_ in pend:
                        S.add(st_[0], st_[1], waits=st_[2], ms=st_[3],
                              dma=st_[4], fence=st_[5])
                for st_ in pend_sub:
                    S.add(st_[0], st_[1], waits=st_[2], ms=st_[3],
                          dma=st_[4], fence=st_[5])
                if ti == NT2 - 1:
                    for st_ in epi_steps(ti, HP - 1):
                        S.add(st_[0], st_[1], waits=st_[2], ms=st_[3],
                              dma=st_[4], fence=st_[5])
                    for st_ in sub_steps(ti):
                        S.add(st_[0], st_[1], waits=st_[2], ms=st_[3],
                              dma=st_[4], fence=st_[5])
                S.add("pe", lambda e: e.nop(),
                    waits=[f"exp_{ti}_{HP-1}_{NCH-1}"], ms=f"atn_{ti}",
                    fence=True)

            total_dSo = S.dma_counts["dSo"]
            S.add("act", lambda e: e.wait_ge(sems["dSo"], total_dSo), raw=True)
            S.add("sp", lambda e: e.wait_ge(sems["dSo"], total_dSo), raw=True)

            _lim = os.environ.get("KDBG_P2LIMIT")
            _lim = int(_lim) if _lim else None
            with nc.Block() as blockB:
                blockB.tensor(S.replay("pe", sems, _lim))
                blockB.scalar(S.replay("act", sems, _lim))
                blockB.vector(S.replay("dve", sems, _lim))
                blockB.gpsimd(S.replay("pool", sems, _lim))
                blockB.sync(S.replay("sp", sems, _lim))
            print(f"[kernel] phase2 ops: {len(S.ops)} (limit={_lim})", flush=True)

    return nc


_CACHE = {}


def _get_program(seq):
    if seq not in _CACHE:
        _CACHE[seq] = build_program(seq)
    return _CACHE[seq]


def kernel(L0_attn_x, L0_attn_gamma, L0_attn_Wq, L0_attn_Wkv, L0_attn_W_bilinear,
           L0_attn_Wo, L0_ffn_weight_fp8, L0_ffn_bias, L0_ffn_input_scale,
           L0_ffn_weight_scale):
    x = np.ascontiguousarray(np.asarray(L0_attn_x, dtype=np.float32))
    B, seq, dm = x.shape
    assert dm == DM
    nc = _get_program(seq)

    ct, st2 = host_tables(seq)
    wb = np.asarray(L0_attn_W_bilinear, dtype=np.float32)
    wb_packed = np.zeros((128, HP * DH), dtype=np.float32)
    for h in range(H):
        wb_packed[(h % 2) * 64:(h % 2) * 64 + 64,
                  (h // 2) * DH:(h // 2) * DH + DH] = wb[h]
    idn = np.eye(128, dtype=np.float32)

    common = {
        "gamma": np.ascontiguousarray(
            np.asarray(L0_attn_gamma, dtype=np.float32).reshape(1, DM)),
        "wq": np.ascontiguousarray(np.asarray(L0_attn_Wq, dtype=np.float32)),
        "wkv": np.ascontiguousarray(np.asarray(L0_attn_Wkv, dtype=np.float32)),
        "wb": wb_packed,
        "wo": np.ascontiguousarray(np.asarray(L0_attn_Wo, dtype=np.float32)),
        "wf": np.ascontiguousarray(np.asarray(L0_ffn_weight_fp8, dtype=np.float32)),
        "bias": np.ascontiguousarray(
            np.asarray(L0_ffn_bias, dtype=np.float32).reshape(1, DM)),
        "is_": np.asarray(L0_ffn_input_scale, dtype=np.float32).reshape(1, 1),
        "ws": np.asarray(L0_ffn_weight_scale, dtype=np.float32).reshape(1, 1),
        "ct": ct, "st": st2, "idn": idn,
    }
    in_maps = []
    for c in range(B):
        m = dict(common)
        m["x"] = np.ascontiguousarray(x[c])
        in_maps.append(m)

    res = run_bass_kernel_spmd(nc, in_maps, core_ids=list(range(B)))
    out = np.stack([np.asarray(r["out"], dtype=np.float32)
                    for r in res.results], axis=0)
    return out


# version 30 (final)
# speedup vs baseline: 1.0503x; 1.0040x over previous
import contextlib
import os
import numpy as np
import ml_dtypes
import concourse.bass as bass
import concourse.mybir as mybir
from concourse.bass_utils import run_bass_kernel_spmd

F32 = mybir.dt.float32
BF16 = mybir.dt.bfloat16
FP8 = mybir.dt.float8e4
AF = mybir.ActivationFunctionType
ALU = mybir.AluOpType
AX = mybir.AxisListType

DM = 1024
H = 16
DH = 64
HP = H // 2
LN_EPS = 1e-5
ROPE_BASE = 10000.0
V65 = DH + 1
VROW = H * V65

ENGINES = ("pe", "act", "dve", "pool", "sp")


class Sched:

    def __init__(self):
        self.ops = []
        self.counts = {e: 0 for e in ENGINES}
        self.dma_counts = {}
        self.ms = {}
        self.base = {}

    SELF_SYNC = ("act", "dve", "pool")

    def add(self, eng, emit, waits=(), ms=None, dma=None, raw=False,
            fence=False):
        waits = tuple(w for w in waits if w is not None)
        for w in waits:
            assert w in self.ms, f"wait on undefined milestone {w}"
        if raw:
            self.ops.append((eng, emit, waits, None, None, True, None))
            return
        if dma is None:
            self.counts[eng] += 1
            if ms is not None:
                self.ms[ms] = (eng, self.counts[eng])
        else:
            self.dma_counts[dma] = self.dma_counts.get(dma, 0) + 16
            if ms is not None:
                self.ms[ms] = (dma, self.dma_counts[dma])
        fence_val = self.counts.get(eng, 0) - (1 if dma is None else 0) \
            if (fence and ms is not None) else None
        self.ops.append((eng, emit, waits, ms, dma, False, fence_val))

    def replay(self, eng_name, sems, limit=None):
        src = self.ops if limit is None else self.ops[:limit]
        ops = [o for o in src if o[0] == eng_name]
        self_sync = eng_name in self.SELF_SYNC

        def f(e):
            waited = {}
            n_done = self.base.get(eng_name, 0)
            for (_, emit, waits, ms, dma, raw, fence_val) in ops:
                for w in waits:
                    sk, val = self.ms[w]
                    if sk == eng_name and self_sync:
                        continue
                    if waited.get(sk, -1) >= val:
                        continue
                    e.wait_ge(sems[sk], val)
                    waited[sk] = val
                if not raw and self_sync and n_done > 0:
                    e.wait_ge(sems[eng_name], n_done)
                elif fence_val is not None and fence_val > 0 and not self_sync:
                    e.wait_ge(sems[eng_name], fence_val)
                ins = emit(e)
                if raw:
                    continue
                if dma is not None:
                    ins.then_inc(sems[dma], 16)
                else:
                    ins.then_inc(sems[eng_name], 1)
                    n_done += 1
        return f


def host_tables(seq):
    theta = 1.0 / ROPE_BASE ** (np.arange(0, DH, 2, dtype=np.float64) / DH)
    pos = np.arange(seq, dtype=np.float64)
    fr = pos[:, None] * theta[None, :]
    emb = np.concatenate([fr, fr], axis=-1)
    cos = np.cos(emb)
    sin = np.sin(emb)
    ct = cos.astype(np.float32)
    st2 = np.empty_like(ct)
    st2[:, 0::2] = sin[:, 1::2].astype(np.float32)
    st2[:, 1::2] = (-sin[:, 0::2]).astype(np.float32)
    return ct, st2


def build_program(seq):
    NT = seq // 128
    NT2 = seq // 512
    NCH = seq // 128
    NSL = seq // 512

    nc = bass.Bass("TRN2", target_bir_lowering=False)

    x_hbm = nc.dram_tensor("x", [seq, DM], F32, kind="ExternalInput")
    gamma_h = nc.dram_tensor("gamma", [1, DM], F32, kind="ExternalInput")
    wq_h = nc.dram_tensor("wq", [DM, DM], F32, kind="ExternalInput")
    wkv_h = nc.dram_tensor("wkv", [DM, 2 * DM], F32, kind="ExternalInput")
    wb_h = nc.dram_tensor("wb", [128, HP * DH], F32, kind="ExternalInput")
    wo_h = nc.dram_tensor("wo", [DM, DM], F32, kind="ExternalInput")
    wf_h = nc.dram_tensor("wf", [DM, DM], F32, kind="ExternalInput")
    bias_h = nc.dram_tensor("bias", [1, DM], F32, kind="ExternalInput")
    is_h = nc.dram_tensor("is_", [1, 1], F32, kind="ExternalInput")
    ws_h = nc.dram_tensor("ws", [1, 1], F32, kind="ExternalInput")
    ct_h = nc.dram_tensor("ct", [seq, DH], F32, kind="ExternalInput")
    st_h = nc.dram_tensor("st", [seq, DH], F32, kind="ExternalInput")
    id_h = nc.dram_tensor("idn", [128, 128], F32, kind="ExternalInput")
    out_hbm = nc.dram_tensor("out", [seq, DM], F32, kind="ExternalOutput")

    qT_hbm = nc.dram_tensor("qT_s", [128, HP * seq], BF16, kind="Internal")
    v_hbm = nc.dram_tensor("v_s", [128, NT * VROW], BF16, kind="Internal")

    with contextlib.ExitStack() as stack:
        id_f32 = stack.enter_context(nc.sbuf_tensor("id_f32", [128, 128], F32))
        id_bf = stack.enter_context(nc.sbuf_tensor("id_bf", [128, 128], BF16))
        ones_r = stack.enter_context(nc.sbuf_tensor("ones_r", [1, 128], F32))
        rstd1_res = stack.enter_context(nc.sbuf_tensor("rstd1", [128, NT], F32))
        nmr1_res = stack.enter_context(nc.sbuf_tensor("nmr1", [128, NT], F32))
        qmax_run = stack.enter_context(nc.sbuf_tensor("qmaxr", [128, H], F32))
        misc = stack.enter_context(nc.sbuf_tensor("misc", [128, 16], F32))
        kt_sb = stack.enter_context(nc.sbuf_tensor("kt_sb", [128, HP * seq], BF16))
        (MC_SUM, MC_SSQ, MC_M, MC_T1, MC_T2, MC_T3, MC_R2, MC_N2,
         MC_ISWS, MC_INVIS, MC_EPS) = range(11)

        sems = {}
        for name in ("pe", "act", "dve", "pool", "sp", "bar", "dP", "dP0", "dG",
                     "dX0", "dX1", "dX2", "dSq0", "dSq1", "dSk0", "dSk1", "dSv0", "dSv1", "dP2", "dW0", "dW1",
                     "dQ", "dSo"):
            sems[name] = stack.enter_context(nc.semaphore("sem_" + name))

        with contextlib.ExitStack() as p1:
            wq_sb = p1.enter_context(nc.sbuf_tensor("wq_sb", [128, 8 * DM], BF16))
            wkv_sb = p1.enter_context(nc.sbuf_tensor("wkv_sb", [128, 16 * DM], BF16))
            wb_sb = p1.enter_context(nc.sbuf_tensor("wb_sb", [128, HP * DH], BF16))
            ct_sb = p1.enter_context(nc.sbuf_tensor("ct_sb", [128, NT * DH], F32))
            st_sb = p1.enter_context(nc.sbuf_tensor("st_sb", [128, NT * DH], F32))
            gamma_bc = p1.enter_context(nc.sbuf_tensor("gamma_bc", [128, DM], F32))
            grow = p1.enter_context(nc.sbuf_tensor("grow", [1, DM], F32))
            xbl = [p1.enter_context(nc.sbuf_tensor(f"xb{i}", [128, DM], F32)) for i in range(3)]
            xnn_sb = [p1.enter_context(nc.sbuf_tensor(f"xnn_sb{i}", [128, DM], F32)) for i in range(2)]
            sqdump = p1.enter_context(nc.sbuf_tensor("sqdump", [128, DM], BF16))
            xn2_sb = [p1.enter_context(nc.sbuf_tensor(f"xn2_sb{i}", [128, DM], BF16)) for i in range(2)]
            xn2T = [p1.enter_context(nc.sbuf_tensor(f"xn2T{i}", [128, DM], BF16)) for i in range(2)]
            ropeA = [p1.enter_context(nc.sbuf_tensor(f"ropeA{i}", [128, DM], BF16)) for i in range(2)]
            ropeB = [p1.enter_context(nc.sbuf_tensor(f"ropeB{i}", [128, DM], BF16)) for i in range(2)]
            qr_sb = [p1.enter_context(nc.sbuf_tensor(f"qr_sb{i}", [128, DM], BF16)) for i in range(2)]
            kr_sb = [p1.enter_context(nc.sbuf_tensor(f"kr_sb{i}", [128, DM], BF16)) for i in range(2)]
            qTs = [p1.enter_context(nc.sbuf_tensor(f"qTs{i}", [128, DM], BF16)) for i in range(2)]
            kTs = [p1.enter_context(nc.sbuf_tensor(f"kTs{i}", [128, DM], BF16)) for i in range(2)]
            vst = [p1.enter_context(nc.sbuf_tensor(f"vst{i}", [128, VROW], BF16)) for i in range(2)]
            qn_sb = p1.enter_context(nc.sbuf_tensor("qn_sb", [128, H], F32))
            stat = [p1.enter_context(nc.sbuf_tensor(f"stat{i}", [128, 16], F32)) for i in range(2)]

            P0 = p1.enter_context(nc.psum_tensor("P0", [128, 1024], F32))
            P1 = p1.enter_context(nc.psum_tensor("P1", [128, 1024], F32))
            P2 = p1.enter_context(nc.psum_tensor("P2", [128, 512], F32))
            P3 = p1.enter_context(nc.psum_tensor("P3", [128, 512], F32))
            PT = p1.enter_context(nc.psum_tensor("PT", [128, 1024], BF16))
            PTB = p1.enter_context(nc.psum_tensor("PTB", [128, 1024], BF16))

            S = Sched()

            S.add("dve", lambda e: e.memset(misc[0:1, 14:15], 0.0),
                ms="scr_init")
            S.add("act", lambda e: e.activation(
                misc[0:1, 15:16], misc[0:1, 14:15], AF.Square),
                waits=["scr_init"])
            S.add("pool", lambda e: e.dma_start(
                wq_sb[:, :].rearrange("p (fc o) -> p fc o", o=DM),
                wq_h[:, :].rearrange("(fc p) o -> p fc o", p=128)),
                dma="dG", ms="ld_wq_i")
            S.add("pool", lambda e: e.dma_start(
                wkv_sb[:, :].rearrange("p (fc o) -> p fc o", o=2 * DM),
                wkv_h[:, :].rearrange("(fc p) o -> p fc o", p=128)),
                dma="dG", ms="ld_wkv_i")
            S.add("pool", lambda e: e.dma_start(wb_sb[:], wb_h[:, :]),
                dma="dG", ms="wload")
            S.add("sp", lambda e: e.dma_start(id_f32[:], id_h[:, :]),
                dma="dP0", ms="ld_id_i")
            S.add("sp", lambda e: e.dma_start(grow[:], gamma_h[:, :]),
                dma="dP0", ms="p1_pre")
            S.add("sp", lambda e: e.dma_start(
                ct_sb[:, :].rearrange("p (t d) -> p t d", d=DH),
                ct_h[:, :].rearrange("(t p) d -> p t d", p=128)),
                dma="dP", ms="ld_ct_i")
            S.add("sp", lambda e: e.dma_start(
                st_sb[:, :].rearrange("p (t d) -> p t d", d=DH),
                st_h[:, :].rearrange("(t p) d -> p t d", p=128)),
                dma="dP", ms="tab_ld")
            S.add("dve", lambda e: e.memset(ones_r[:], 1.0), ms="ones_done")
            S.add("dve", lambda e: e.memset(
                misc[:, MC_EPS:MC_EPS + 1], LN_EPS), ms="eps_done")
            S.add("dve", lambda e: e.memset(vst[0][:], 1.0))
            S.add("dve", lambda e: e.memset(vst[1][:], 1.0), ms="vst_ones")
            S.add("dve", lambda e: e.memset(qmax_run[:], 0.0), ms="qmax_init")
            S.add("act", lambda e: e.activation(id_bf[:], id_f32[:], AF.Copy),
                waits=["p1_pre"], ms="idbf_done")
            S.add("dve", lambda e: e.tensor_scalar(
                wq_sb[:], wq_sb[:], 0.125, None, ALU.mult),
                waits=["wload"], ms="wq_scaled")
            for nn in range(2):
                S.add("pe", lambda e, nn=nn: e.matmul(
                    P0[:, nn * 512:(nn + 1) * 512], ones_r[:, :],
                    grow[0:1, nn * 512:(nn + 1) * 512], start=True, stop=True),
                    waits=["ones_done", "p1_pre"],
                    ms=("gbc_mm" if nn == 1 else None))
            S.add("dve", lambda e: e.tensor_copy(gamma_bc[:], P0[:, :]),
                waits=["gbc_mm"], ms="gbc_done")

            def stageAload(t):
                xb = xbl[t % 3]
                S.add("sp", lambda e, t=t, xb=xb: e.dma_start(
                    xb[:], x_hbm[t * 128:(t + 1) * 128, :]),
                    waits=([f"xnn_{t-3}", f"rs1_{t-3}"] if t >= 3 else []),
                    dma=("dX0", "dX1", "dX2")[t % 3], ms=f"ldx_{t}")

            def stageAearly(t):
                p = t % 2
                xb = xbl[t % 3]
                ST = stat[p]
                xnn_ = xnn_sb[p]; xn2_ = xn2_sb[p]
                S.add("dve", lambda e, xb=xb, ST=ST: e.reduce_sum(
                    ST[:, MC_SUM:MC_SUM + 1], xb[:], AX.X),
                    waits=[f"ldx_{t}"], ms=f"rs1_{t}")
                S.add("act", lambda e, xb=xb, ST=ST: e.activation(
                    sqdump[:], xb[:], AF.Square,
                    accum_out=ST[:, MC_SSQ:MC_SSQ + 1]),
                    waits=[f"ldx_{t}"] + ([f"mic_{t-2}"] if t >= 2 else []),
                    ms=f"sq1_{t}")
            def stageA(t):
                p = t % 2
                xb = xbl[t % 3]
                ST = stat[p]
                xnn_ = xnn_sb[p]; xn2_ = xn2_sb[p]
                S.add("dve", lambda e, ST=ST: e.tensor_scalar(
                    ST[:, MC_M:MC_M + 1], ST[:, MC_SUM:MC_SUM + 1],
                    1.0 / DM, None, ALU.mult), waits=[f"sq1_{t}"])
                S.add("dve", lambda e, ST=ST: e.tensor_tensor(
                    ST[:, MC_T1:MC_T1 + 1], ST[:, MC_M:MC_M + 1],
                    ST[:, MC_M:MC_M + 1], ALU.mult))
                S.add("dve", lambda e, ST=ST: e.scalar_tensor_tensor(
                    ST[:, MC_T2:MC_T2 + 1], ST[:, MC_SSQ:MC_SSQ + 1],
                    1.0 / DM, ST[:, MC_T1:MC_T1 + 1],
                    ALU.mult, ALU.subtract), ms=f"var1_{t}")
                S.add("act", lambda e, ST=ST: e.activation(
                    ST[:, MC_T3:MC_T3 + 1], ST[:, MC_T2:MC_T2 + 1],
                    AF.Sqrt, bias=misc[:, MC_EPS:MC_EPS + 1]),
                    waits=[f"var1_{t}", "eps_done"], ms=f"sd1_{t}")
                S.add("dve", lambda e, t=t, ST=ST: e.reciprocal(
                    rstd1_res[:, t:t + 1], ST[:, MC_T3:MC_T3 + 1]),
                    waits=[f"sd1_{t}"])
                S.add("dve", lambda e, t=t, ST=ST: e.tensor_scalar(
                    nmr1_res[:, t:t + 1], ST[:, MC_M:MC_M + 1],
                    rstd1_res[:, t:t + 1], -1.0, ALU.mult, ALU.mult))
                S.add("dve", lambda e, ST=ST: e.tensor_tensor(
                    ST[:, MC_T1:MC_T1 + 1], ST[:, MC_T3:MC_T3 + 1],
                    ST[:, MC_T3:MC_T3 + 1], ALU.mult))
                S.add("dve", lambda e, ST=ST: e.reciprocal(
                    ST[:, MC_T1:MC_T1 + 1], ST[:, MC_T1:MC_T1 + 1]))
                S.add("dve", lambda e, ST=ST: e.tensor_scalar(
                    ST[:, MC_T1:MC_T1 + 1], ST[:, MC_T2:MC_T2 + 1],
                    ST[:, MC_T1:MC_T1 + 1], None, ALU.mult),
                    ms=f"v2_{t}")
                S.add("act", lambda e, ST=ST: e.activation(
                    ST[:, MC_T1:MC_T1 + 1], ST[:, MC_T1:MC_T1 + 1],
                    AF.Sqrt, bias=misc[:, MC_EPS:MC_EPS + 1]),
                    waits=[f"v2_{t}"], ms=f"sd2_{t}")
                S.add("dve", lambda e, ST=ST: e.reciprocal(
                    ST[:, MC_T1:MC_T1 + 1], ST[:, MC_T1:MC_T1 + 1]),
                    waits=[f"sd2_{t}"])
                S.add("dve", lambda e, t=t, ST=ST: e.tensor_scalar(
                    ST[:, MC_R2:MC_R2 + 1], rstd1_res[:, t:t + 1],
                    ST[:, MC_T1:MC_T1 + 1], None, ALU.mult))
                S.add("dve", lambda e, ST=ST: e.tensor_scalar(
                    ST[:, MC_N2:MC_N2 + 1], ST[:, MC_M:MC_M + 1],
                    ST[:, MC_R2:MC_R2 + 1], -1.0, ALU.mult, ALU.mult),
                    ms=f"mic_{t}")
                S.add("act", lambda e, xb=xb, xnn_=xnn_, ST=ST: e.activation(
                    xnn_[:], xb[:], AF.Identity,
                    bias=ST[:, MC_N2:MC_N2 + 1],
                    scale=ST[:, MC_R2:MC_R2 + 1]),
                    waits=[f"mic_{t}"] + ([f"xn2_{t-2}"] if t >= 2 else []),
                    ms=f"xnn_{t}")
                S.add("pool", lambda e, xnn_=xnn_, xn2_=xn2_: e.tensor_tensor(
                    xn2_[:], xnn_[:], gamma_bc[:], ALU.mult),
                    waits=[f"xnn_{t}", "gbc_done"] +
                          ([f"tr1_{t-2}"] if t >= 2 else []),
                    ms=f"xn2_{t}")

            def stageB(t, part):
                p = t % 2
                x2T = xn2T[p]; rA = ropeA[p]; rB = ropeB[p]
                qr_ = qr_sb[p]; kr_ = kr_sb[p]
                qT_ = qTs[p]; kT_ = kTs[p]; vs_ = vst[p]
                xn2_ = xn2_sb[p]
                ctv = ct_sb[:, t * DH:(t + 1) * DH].rearrange(
                    "p (o d) -> p o d", o=1).to_broadcast([128, H, DH])
                stv = st_sb[:, t * DH:(t + 1) * DH].rearrange(
                    "p (o d) -> p o d", o=1).to_broadcast([128, H, DH])

                def rope(src_ps, pev, ev_ms, dst, mm_ms, out_ms, guard):
                    S.add("dve", lambda e, src_ps=src_ps, ctv=ctv,
                          rA=rA: e.tensor_tensor(
                        rA[:, :].rearrange("p (h d) -> p h d", d=DH),
                        src_ps[:, :].rearrange("p (h d) -> p h d", d=DH),
                        ctv, ALU.mult),
                        waits=[mm_ms, "p1_pre", "tab_ld"] + guard)
                    S.add("dve", lambda e, src_ps=src_ps, stv=stv,
                          rB=rB: e.tensor_tensor(
                        rB[:, :].rearrange("p (h d) -> p h d", d=DH),
                        src_ps[:, :].rearrange("p (h d) -> p h d", d=DH),
                        stv, ALU.mult),
                        ms=ev_ms)
                    ra = rA[:, :].rearrange("p (n two) -> p n two", two=2)
                    rb = rB[:, :].rearrange("p (n two) -> p n two", two=2)
                    dd = dst[:, :].rearrange("p (n two) -> p n two", two=2)
                    S.add("dve", lambda e, ra=ra, rb=rb, dd=dd: e.tensor_tensor(
                        dd[:, :, 0:1], ra[:, :, 0:1], rb[:, :, 1:2], ALU.add))
                    S.add("dve", lambda e, ra=ra, rb=rb, dd=dd: e.tensor_tensor(
                        dd[:, :, 1:2], ra[:, :, 1:2], rb[:, :, 0:1], ALU.add),
                        ms=out_ms)

                if part == 1:
                    for b in range(8):
                        S.add("pe", lambda e, b=b, xn2_=xn2_: e.transpose(
                            PTB[:, b * 128:(b + 1) * 128],
                            xn2_[:, b * 128:(b + 1) * 128], id_bf[:]),
                            waits=(["idbf_done", f"xn2_{t}"] +
                                   ([f"ev2_{t-1}"] if b == 0 and t >= 1 else [])),
                            ms=(f"tr1_{t}" if b == 7 else None))
                    S.add("act", lambda e, x2T=x2T: e.activation(
                        x2T[:], PTB[:, :], AF.Copy),
                        waits=[f"tr1_{t}"] + ([f"vmm_{t-2}"] if t >= 2 else []),
                        ms=f"ev1_{t}")
                    return
                if part == 12:
                    for fc in range(8):
                        for nn in range(2):
                            S.add("pe", lambda e, fc=fc, nn=nn, x2T=x2T: e.matmul(
                                P0[:, nn * 512:(nn + 1) * 512],
                                x2T[:, fc * 128:(fc + 1) * 128],
                                wq_sb[:, fc * DM + nn * 512: fc * DM + (nn + 1) * 512],
                                start=(fc == 0), stop=(fc == 7)),
                                waits=([f"ev1_{t}", "wq_scaled"] +
                                       ([f"evq_{t-1}"] if t >= 1 else []) +
                                       (["gbc_done"] if t == 0 else [])
                                       if fc == 0 and nn == 0 else []),
                                ms=(f"qmm_{t}" if fc == 7 and nn == 1 else None))
                    for fc in range(8):
                        for nn in range(2):
                            S.add("pe", lambda e, fc=fc, nn=nn, x2T=x2T: e.matmul(
                                P1[:, nn * 512:(nn + 1) * 512],
                                x2T[:, fc * 128:(fc + 1) * 128],
                                wkv_sb[:, fc * 2 * DM + nn * 512:
                                       fc * 2 * DM + (nn + 1) * 512],
                                start=(fc == 0), stop=(fc == 7)),
                                waits=((["wload"] +
                                        ([f"evk_{t-1}"] if t >= 1 else []))
                                       if fc == 0 and nn == 0 else []),
                                ms=(f"kmm_{t}" if fc == 7 and nn == 1 else None))
                    rope(P0, None, f"evq_{t}", qr_, f"qmm_{t}", f"ropq_{t}",
                         [f"tr2_{t-2}"] if t >= 2 else [])
                    S.add("dve", lambda e, qr_=qr_, rA=rA: e.tensor_tensor(
                        rA[:], qr_[:], qr_[:], ALU.mult),
                        waits=[f"ropq_{t}"])
                    S.add("dve", lambda e, rA=rA: e.tensor_reduce(
                        qn_sb[:], rA[:, :].rearrange("p (h d) -> p h d", d=DH),
                        AX.X, ALU.add))
                    S.add("dve", lambda e: e.tensor_tensor(
                        qmax_run[:], qmax_run[:], qn_sb[:], ALU.max),
                        waits=["qmax_init"], ms=f"qn_{t}")
                    return
                if part == 15:
                    for b in range(8):
                        S.add("pe", lambda e, b=b, qr_=qr_: e.transpose(
                            PT[:, b * 128:(b + 1) * 128],
                            qr_[:, b * 128:(b + 1) * 128], id_bf[:]),
                            waits=([f"ropq_{t}"] +
                                   ([f"ev3_{t-1}"] if t >= 1 else [])
                                   if b == 0 else []),
                            ms=(f"tr2_{t}" if b == 7 else None))
                    S.add("act", lambda e, qT_=qT_: e.activation(
                        qT_[:], PT[:, :], AF.Copy),
                        waits=[f"tr2_{t}"] + ([f"stq_{t-2}"] if t >= 2 else []),
                        ms=f"ev2_{t}")
                    S.add("sp", lambda e, t=t, qT_=qT_: e.dma_start(
                        qT_hbm[:, :].rearrange("p (hp n) -> p hp n", n=seq)[
                            :, :, t * 128:(t + 1) * 128],
                        qT_[:, :].rearrange("p (hp n) -> p hp n", n=128)),
                        waits=[f"ev2_{t}"], dma=("dSq0" if t % 2 == 0 else "dSq1"), ms=f"stq_{t}")
                    rope(P1, None, f"evk_{t}", kr_, f"kmm_{t}", f"ropk_{t}",
                         [f"tr3_{t-2}"] if t >= 2 else [])
                    return
                if part == 2:
                    for fc in range(8):
                        for ps, nn in ((P2, 2), (P3, 3)):
                            S.add("pe", lambda e, fc=fc, nn=nn, ps=ps, x2T=x2T: e.matmul(
                                ps[:, :],
                                x2T[:, fc * 128:(fc + 1) * 128],
                                wkv_sb[:, fc * 2 * DM + nn * 512:
                                       fc * 2 * DM + (nn + 1) * 512],
                                start=(fc == 0), stop=(fc == 7)),
                                waits=(([f"ktev_{t-1}"] if t >= 1 else [])
                                       if fc == 0 and nn == 2 else []),
                                ms=(f"vmm_{t}" if fc == 7 and nn == 3 else None))
                    for b in range(8):
                        S.add("pe", lambda e, b=b, kr_=kr_: e.transpose(
                            PT[:, b * 128:(b + 1) * 128],
                            kr_[:, b * 128:(b + 1) * 128], id_bf[:]),
                            waits=([f"ropk_{t}", f"ev2_{t}"] if b == 0 else []),
                            ms=(f"tr3_{t}" if b == 7 else None))
                    S.add("act", lambda e, kT_=kT_: e.activation(
                        kT_[:], PT[:, :], AF.Copy),
                        waits=[f"tr3_{t}"] +
                              ([f"bil_done_{t-2}"] if t >= 2 else []),
                        ms=f"ev3_{t}")
                    vout = vs_[:, :].rearrange("p (h e) -> p h e", e=V65)
                    S.add("act", lambda e, vout=vout: e.activation(
                        vout[:, 0:8, 0:DH],
                        P2[:, :].rearrange("p (h d) -> p h d", d=DH), AF.Copy),
                        waits=[f"vmm_{t}", "vst_ones"] +
                              ([f"stv_{t-2}"] if t >= 2 else []))
                    S.add("act", lambda e, vout=vout: e.activation(
                        vout[:, 8:16, 0:DH],
                        P3[:, :].rearrange("p (h d) -> p h d", d=DH), AF.Copy),
                        ms=f"vev_{t}")
                    S.add("sp", lambda e, t=t, vs_=vs_: e.dma_start(
                        v_hbm[:, t * VROW:(t + 1) * VROW], vs_[:]),
                        waits=[f"vev_{t}"], dma=("dSv0" if t % 2 == 0 else "dSv1"),
                        ms=f"stv_{t}")
                    return
                for hp in range(HP):
                    pbk = P2 if hp % 2 == 0 else P3
                    slot = hp // 2
                    for par in range(2):
                        b0 = par * 64
                        S.add("pe", lambda e, hp=hp, b0=b0, slot=slot,
                              pbk=pbk, kT_=kT_: e.matmul(
                            pbk[b0:b0 + 64, slot * 128:(slot + 1) * 128],
                            wb_sb[b0:b0 + 64, hp * DH:(hp + 1) * DH],
                            kT_[b0:b0 + 64, hp * 128:(hp + 1) * 128],
                            start=True, stop=True, tile_position=(b0, b0)),
                            waits=(([f"ev3_{t}", "wload", f"vev_{t}"] +
                                    ([f"ktev_{t-1}"] if t >= 1 else []))
                                   if hp == 0 and par == 0 else []),
                            ms=(f"bil_{t}" if hp == HP - 1 and par == 1
                                else None))
                S.add("pe", lambda e: e.nop(), waits=[f"bil_{t}"],
                    ms=f"bil_done_{t}", fence=True)
                kview = kt_sb[:, :].rearrange("p (hp n) -> p hp n", n=seq)[
                    :, :, t * 128:(t + 1) * 128]
                S.add("act", lambda e, kview=kview: e.activation(
                    kview[:, 0:8:2, :],
                    P2[:, :].rearrange("p (sl c) -> p sl c", c=128), AF.Copy),
                    waits=[f"bil_done_{t}"])
                S.add("act", lambda e, kview=kview: e.activation(
                    kview[:, 1:8:2, :],
                    P3[:, :].rearrange("p (sl c) -> p sl c", c=128), AF.Copy),
                    ms=f"ktev_{t}")

            stageAload(0)
            stageAload(1)
            stageAload(2)
            stageAearly(0)
            stageA(0)
            stageAearly(1)
            stageA(1)
            for t in range(NT):
                if t + 2 < NT:
                    stageAearly(t + 2)
                stageB(t, 1)
                if t >= 1:
                    stageB(t - 1, 2)
                if t + 2 < NT:
                    stageA(t + 2)
                stageB(t, 12)
                if t >= 1:
                    stageB(t - 1, 25)
                stageB(t, 15)
                if t + 3 < NT:
                    stageAload(t + 3)
            stageB(NT - 1, 2)
            stageB(NT - 1, 25)

            store_streams = ["dSq0", "dSq1", "dSk0", "dSk1", "dSv0", "dSv1"]
            store_totals = {k: S.dma_counts.get(k, 0) for k in store_streams}
            for k in store_streams:
                S.add("act", lambda e, k=k: e.wait_ge(sems[k], store_totals[k]),
                      raw=True)
            for eng in ENGINES:
                S.add(eng, lambda e: e.nop().then_inc(sems["bar"], 1), raw=True)

            _lim1 = os.environ.get("KDBG_P1LIMIT")
            _lim1 = int(_lim1) if _lim1 else None
            with nc.Block() as blockA:
                blockA.tensor(S.replay("pe", sems, _lim1))
                blockA.scalar(S.replay("act", sems, _lim1))
                blockA.vector(S.replay("dve", sems, _lim1))
                blockA.gpsimd(S.replay("pool", sems, _lim1))
                blockA.sync(S.replay("sp", sems, _lim1))
            print(f"[kernel] phase1 ops: {len(S.ops)} (limit={_lim1})", flush=True)
            ph1_store_totals = dict(store_totals)
            ph1_dma_counts = dict(S.dma_counts)
            ph1_counts = dict(S.counts)

        with contextlib.ExitStack() as p2:
            v_sb = p2.enter_context(nc.sbuf_tensor("v_sb", [128, NT * VROW], BF16))
            wo_sb = p2.enter_context(nc.sbuf_tensor("wo_sb", [128, 8 * DM], BF16))
            wft_sb = p2.enter_context(nc.sbuf_tensor("wft_sb", [128, 8 * DM], FP8))
            qT_t = p2.enter_context(nc.sbuf_tensor("qT_t", [128, HP * 512], BF16))
            ET = p2.enter_context(nc.sbuf_tensor("ET", [128, 2048], BF16))
            pvE = p2.enter_context(nc.sbuf_tensor("pvE", [128, 1024], BF16))
            aoT = [p2.enter_context(nc.sbuf_tensor(f"aoT{i}", [128, HP * 512], BF16)) for i in range(2)]
            rlv = p2.enter_context(nc.sbuf_tensor("rlv", [128, 8], F32))
            xs0 = p2.enter_context(nc.sbuf_tensor("xs0", [128, DM], F32))
            xnf = p2.enter_context(nc.sbuf_tensor("xnf", [128, DM], F32))
            qtmp = p2.enter_context(nc.sbuf_tensor("qtmp", [128, 512], F32))
            qtmp2 = xnf
            xqT = p2.enter_context(nc.sbuf_tensor("xqT", [128, DM], FP8))
            yv_sb = p2.enter_context(nc.sbuf_tensor("yv_sb", [128, DM], F32))
            gbuf = p2.enter_context(nc.sbuf_tensor("gbuf", [128, DM], BF16))
            ao_sb = p2.enter_context(nc.sbuf_tensor("ao_sb", [128, 512], BF16))
            bias_bc = p2.enter_context(nc.sbuf_tensor("bias_bc", [128, DM], F32))
            srow = p2.enter_context(nc.sbuf_tensor("srow", [1, 8], F32))
            srow2 = p2.enter_context(nc.sbuf_tensor("srow2", [2, 1], F32))
            negc_bc = p2.enter_context(nc.sbuf_tensor("negc_bc", [128, 1], F32))
            k2run = p2.enter_context(nc.sbuf_tensor("k2run", [2, 1], F32))
            sel = p2.enter_context(nc.sbuf_tensor("sel", [128, 2], BF16))

            PA = p2.enter_context(nc.psum_tensor("PA", [128, 1024], F32))
            PB = p2.enter_context(nc.psum_tensor("PB", [128, 1024], F32))
            PVa = p2.enter_context(nc.psum_tensor("PVa", [128, 512], F32))
            PVb = p2.enter_context(nc.psum_tensor("PVb", [128, 512], F32))
            PT2 = p2.enter_context(nc.psum_tensor("PT2", [128, 1024], BF16))
            PS4 = p2.enter_context(nc.psum_tensor("PS4", [128, 512], F32))

            S = Sched()
            NBAR = len(ENGINES)
            for eng in ENGINES:
                S.add(eng, lambda e: e.wait_ge(sems["bar"], NBAR), raw=True)
            for k, v in ph1_store_totals.items():
                S.add("sp", lambda e, k=k, v=v: e.wait_ge(sems[k], v), raw=True)
            S.dma_counts.update(ph1_dma_counts)
            S.counts.update(ph1_counts)
            S.base = dict(ph1_counts)

            S.add("act", lambda e: e.activation(
                misc[0:1, 15:16], misc[0:1, MC_EPS:MC_EPS + 1], AF.Sqrt))
            VHALF = NT * VROW // 2
            for sl in range(2):
                S.add("sp", lambda e, sl=sl: e.dma_start(
                    v_sb[:, sl * VHALF:(sl + 1) * VHALF],
                    v_hbm[:, sl * VHALF:(sl + 1) * VHALF]),
                    dma=("dSv0", "dSv1")[sl], ms=f"ld_v{sl}")
            S.add("sp", lambda e: e.dma_start(bias_bc[0:1, :], bias_h[:, :]),
                dma="dP2", ms="ld_bias_i")
            S.add("sp", lambda e: e.dma_start(srow[0:1, 0:1], is_h[:, :]),
                dma="dP2", ms="ld_is_i")
            S.add("sp", lambda e: e.dma_start(srow[0:1, 1:2], ws_h[:, :]),
                dma="dP2", ms="p2_pre")
            S.add("pool", lambda e: e.dma_start(
                wo_sb[:, :].rearrange("p (fc o) -> p fc o", o=DM),
                wo_h[:, :].rearrange("(fc p) o -> p fc o", p=128)),
                dma="dG", ms="ld_wo")
            for oc in range(8):
                slab = xs0 if oc % 2 == 0 else yv_sb
                S.add("sp", lambda e, oc=oc, slab=slab: e.dma_start(
                    slab[:], wf_h[oc * 128:(oc + 1) * 128, :]),
                    waits=([f"wft_{oc-2}"] if oc >= 2 else []),
                    dma=("dW0" if oc % 2 == 0 else "dW1"), ms=f"ld_wf{oc}")
                for b in range(8):
                    pvw = PVa if b < 4 else PVb
                    S.add("pe", lambda e, b=b, slab=slab, pvw=pvw: e.transpose(
                        pvw[:, (b % 4) * 128:(b % 4 + 1) * 128],
                        slab[:, b * 128:(b + 1) * 128], id_f32[:]),
                        waits=([f"ld_wf{oc}"] +
                               ([f"wfc_{oc-1}"] if oc >= 1 else [])
                               if b == 0 else []),
                        ms=(f"wft_{oc}" if b == 7 else None))
                for kk, pvw in ((0, PVa), (1, PVb)):
                    S.add("act", lambda e, oc=oc, kk=kk, pvw=pvw: e.activation(
                        wft_sb[:, :].rearrange("p (fc oc o) -> p fc oc o",
                                               oc=8, o=128)[:, kk * 4:
                                                            (kk + 1) * 4, oc, :],
                        pvw[:, :].rearrange("p (b o) -> p b o", o=128),
                        AF.Copy),
                        waits=([f"wft_{oc}"] if kk == 0 else []),
                        ms=(f"wfc_{oc}" if kk == 1 else None))
            for nn in range(2):
                S.add("pe", lambda e, nn=nn: e.matmul(
                    PB[:, nn * 512:(nn + 1) * 512], ones_r[:, :],
                    bias_bc[0:1, nn * 512:(nn + 1) * 512], start=True, stop=True),
                    waits=["p2_pre"], ms=("bbc_mm" if nn == 1 else None))
            S.add("dve", lambda e: e.tensor_copy(bias_bc[:], PB[:, :]),
                waits=["bbc_mm"], ms="bbc_done")
            S.add("dve", lambda e: e.tensor_tensor(
                srow[0:1, 2:3], srow[0:1, 0:1], srow[0:1, 1:2], ALU.mult),
                waits=["p2_pre"], ms="isws_row")
            S.add("dve", lambda e: e.reciprocal(srow[0:1, 3:4], srow[0:1, 0:1]),
                ms="invis_row")
            S.add("pe", lambda e: e.matmul(
                PS4[:, 0:1], ones_r[:, :], srow[0:1, 2:3], start=True, stop=True),
                waits=["isws_row"], ms="isws_mm")
            S.add("pe", lambda e: e.matmul(
                PS4[:, 1:2], ones_r[:, :], srow[0:1, 3:4], start=True, stop=True),
                waits=["invis_row"], ms="invis_mm")
            S.add("dve", lambda e: e.tensor_copy(
                misc[:, MC_ISWS:MC_ISWS + 1], PS4[:, 0:1]),
                waits=["invis_mm"], ms="isws_done")
            S.add("dve", lambda e: e.tensor_copy(
                misc[:, MC_INVIS:MC_INVIS + 1], PS4[:, 1:2]),
                ms="invis_done")
            S.add("dve", lambda e: e.memset(sel[:], 0.0))
            S.add("dve", lambda e: e.memset(sel[0:64, 0:1], 1.0))
            S.add("dve", lambda e: e.memset(sel[64:128, 1:2], 1.0), ms="sel_done")
            S.add("dve", lambda e: e.memset(k2run[:], 0.0), ms="k2_init")
            prev_red = None
            prev_mm = None
            HLEN = min(2048, seq)
            NJ = HLEN // 512
            NHALF = seq // HLEN
            for hp in range(HP):
                for half in range(NHALF):
                    kh = f"{hp}_{half}"
                    w_sq = []
                    if prev_mm is not None:
                        w_sq.append(prev_mm)
                    S.add("dve", lambda e, hp=hp, half=half: e.tensor_tensor(
                        ET[:, 0:HLEN],
                        kt_sb[:, hp * seq + half * HLEN:
                              hp * seq + (half + 1) * HLEN],
                        kt_sb[:, hp * seq + half * HLEN:
                              hp * seq + (half + 1) * HLEN],
                        ALU.mult),
                        waits=w_sq, ms=f"k2sq_{kh}")
                    for j in range(NJ):
                        w_mm = []
                        if j == 0:
                            w_mm = [f"k2sq_{kh}", "sel_done"]
                            if prev_red is None:
                                w_mm += ["isws_done", "invis_done"]
                            else:
                                w_mm.append(prev_red)
                        else:
                            w_mm = [f"k2r_{kh}_{j-1}"]
                        gname = f"k2mm_{kh}_{j}"
                        S.add("pe", lambda e, j=j: e.matmul(
                            PS4[0:2, 0:512],
                            sel[:, :], ET[:, j * 512:(j + 1) * 512],
                            start=True, stop=True),
                            waits=w_mm, ms=gname)
                        prev_mm = gname
                        S.add("dve", lambda e: e.tensor_reduce(
                            srow2[:, :], PS4[0:2, 0:512],
                            AX.X, ALU.max),
                            waits=[gname])
                        rname = (f"k2red_{kh}" if j == NJ - 1
                                 else f"k2r_{kh}_{j}")
                        S.add("dve", lambda e: e.tensor_tensor(
                            k2run[:], k2run[:], srow2[:, :], ALU.max),
                            waits=["k2_init"], ms=rname)
                        prev_red = rname
            S.add("pe", lambda e: e.transpose(
                PS4[0:1, 0:2], k2run[:, :], id_f32[0:2, 0:2]),
                waits=[prev_red], ms="k2tr")
            S.add("dve", lambda e: e.tensor_reduce(
                srow[0:1, 4:5], PS4[0:1, 0:2], AX.X, ALU.max),
                waits=["k2tr"], ms="k2s_done")
            S.add("dve", lambda e: e.tensor_reduce(
                misc[:, MC_T1:MC_T1 + 1], qmax_run[:], AX.X, ALU.max),
                ms="qm1")
            S.add("pe", lambda e: e.transpose(
                PS4[0:1, 4:132], misc[:, MC_T1:MC_T1 + 1], id_f32[:]),
                waits=["qm1", "k2s_done"], ms="qmtr")
            S.add("dve", lambda e: e.tensor_reduce(
                srow[0:1, 5:6], PS4[0:1, 4:132], AX.X, ALU.max),
                waits=["qmtr"], ms="q2s_done")
            S.add("act", lambda e: e.activation(
                srow[0:1, 6:7], srow[0:1, 4:5], AF.Sqrt),
                waits=["k2s_done"], ms="ks_done")
            S.add("act", lambda e: e.activation(
                srow[0:1, 7:8], srow[0:1, 5:6], AF.Sqrt),
                waits=["q2s_done"], ms="qs_done")
            S.add("act", lambda e: e.activation(
                misc[0:1, 15:16], misc[0:1, MC_EPS:MC_EPS + 1], AF.Exp),
                waits=["qs_done"])
            S.add("dve", lambda e: e.tensor_tensor(
                srow[0:1, 4:5], srow[0:1, 6:7], srow[0:1, 7:8], ALU.mult),
                waits=["ks_done", "qs_done"])
            S.add("dve", lambda e: e.tensor_scalar(
                srow[0:1, 4:5], srow[0:1, 4:5], -1.02, None, ALU.mult),
                ms="negc_row")
            S.add("pe", lambda e: e.matmul(
                PS4[:, 2:3], ones_r[:, :], srow[0:1, 4:5], start=True, stop=True),
                waits=["negc_row"], ms="negc_mm")
            S.add("dve", lambda e: e.tensor_copy(negc_bc[:], PS4[:, 2:3]),
                waits=["negc_mm"], ms="negc_done")

            def epi_steps(ti, hp):
                pp = f"{ti}_{hp}"
                steps = []
                for k_, pv in ((0, PVa), (1, PVb)):
                    w_ev = [f"pv_{pp}_{NCH-1}"] if k_ == 0 else []
                    steps.append(("dve", lambda e, k_=k_, pv=pv: e.tensor_copy(
                        pvE[:, k_ * 260:(k_ + 1) * 260], pv[:, 0:260]),
                        w_ev, (f"pvev_{pp}" if k_ == 1 else None), None, False))
                steps.append(("dve", lambda e: e.reciprocal(
                    rlv[:, 0:8], pvE[:, 64:520:65]), [], None, None, False))
                for blk in range(8):
                    steps.append(("dve", lambda e, blk=blk: e.tensor_scalar(
                        ao_sb[:, blk * 64:(blk + 1) * 64],
                        pvE[:, blk * 65: blk * 65 + 64],
                        rlv[:, blk:blk + 1], None, ALU.mult),
                        [], (f"aos_{pp}" if blk == 7 else None), None, False))
                for blk in range(8):
                    w_btr = []
                    if blk == 0:
                        w_btr.append(f"aos_{pp}")
                        if hp >= 1:
                            w_btr.append(f"aoev_{ti}_{hp-1}")
                        elif ti >= 1:
                            w_btr.append(f"aoev_{ti-1}_{HP-1}")
                        else:
                            w_btr.append(f"k2red_{HP-1}_{NHALF-1}")
                    steps.append(("pe", lambda e, blk=blk: e.transpose(
                        PT2[0:64, blk * 128:(blk + 1) * 128],
                        ao_sb[:, blk * 64:(blk + 1) * 64], id_bf[:]),
                        w_btr, (f"btr_{pp}" if blk == 7 else None),
                        None, True))
                ao_ = aoT[ti % 2]
                for k_ in range(2):
                    w_ao = [f"btr_{pp}"] if k_ == 0 else []
                    if k_ == 0 and hp == 0 and ti >= 2:
                        w_ao.append(f"wo_{ti-2}_3")
                    steps.append(("dve", lambda e, k_=k_, hp=hp,
                                  ao_=ao_: e.tensor_copy(
                        ao_[k_ * 64:(k_ + 1) * 64, hp * 512:(hp + 1) * 512],
                        PT2[0:64, k_ * 512:(k_ + 1) * 512]),
                        w_ao, (f"aoev_{pp}" if k_ == 1 else None), None, False))
                return steps

            def sub_steps(ti):
                steps = []
                ao_ = aoT[ti % 2]
                for s_ in range(4):
                    tt = ti * 4 + s_
                    xsb = xs0
                    w_ldx = []
                    if s_ >= 1:
                        w_ldx.append(f"res_{ti}_{s_-1}")
                    elif ti >= 1:
                        w_ldx.append(f"res_{ti-1}_3")
                    else:
                        w_ldx.append("wfc_7")
                    steps.append(("sp", lambda e, tt=tt, xsb=xsb: e.dma_start(
                        xsb[:], x_hbm[tt * 128:(tt + 1) * 128, :]),
                        w_ldx, f"ldxs_{ti}_{s_}", "dX0", False))
                    w_xnf = [f"ldxs_{ti}_{s_}"]
                    if s_ >= 1:
                        w_xnf += [f"xnTb_{ti}_{s_-1}", f"xq_{ti}_{s_-1}"]
                    elif ti >= 1:
                        w_xnf += [f"xnTb_{ti-1}_3", f"xq_{ti-1}_3"]
                    steps.append(("dve", lambda e, tt=tt, xsb=xsb: e.tensor_scalar(
                        xnf[:], xsb[:], rstd1_res[:, tt:tt + 1],
                        nmr1_res[:, tt:tt + 1], ALU.mult, ALU.add),
                        w_xnf, f"xnf_{ti}_{s_}", None, False))
                    for h2 in range(2):
                        w_t = [f"xnf_{ti}_{s_}"]
                        if h2 == 0:
                            if s_ >= 1:
                                w_t.append(f"sgb_{ti}_{s_-1}")
                            elif ti >= 1:
                                w_t.append(f"sgb_{ti-1}_3")
                            else:
                                w_t.append("negc_done")
                        else:
                            w_t.append(f"q1a_{ti}_{s_}")
                        for b in range(4):
                            steps.append(("pe", lambda e, b=b, h2=h2: e.transpose(
                                PS4[:, b * 128:(b + 1) * 128],
                                xnf[:, h2 * 512 + b * 128:
                                    h2 * 512 + (b + 1) * 128], id_f32[:]),
                                (w_t if b == 0 else []),
                                (f"xnT{'ab'[h2]}_{ti}_{s_}" if b == 3 else None),
                                None, True))
                        steps.append(("dve", lambda e, h2=h2: e.tensor_scalar(
                            qtmp[:, :], PS4[:, :],
                            misc[:, MC_INVIS:MC_INVIS + 1],
                            240.0, ALU.mult, ALU.min),
                            [f"xnT{'ab'[h2]}_{ti}_{s_}", "invis_done"],
                            f"q1{'ab'[h2]}_{ti}_{s_}", None, False))
                        steps.append(("dve", lambda e, h2=h2: e.tensor_scalar(
                            qtmp2[:, h2 * 512:(h2 + 1) * 512], qtmp[:, :],
                            -240.0, None, ALU.max),
                            [], f"q2{'ab'[h2]}_{ti}_{s_}", None, False))
                    steps.append(("dve", lambda e: e.tensor_copy(
                        xqT[:], qtmp2[:]),
                        [f"q2b_{ti}_{s_}"], f"xq_{ti}_{s_}", None, False))
                    for nn in range(2):
                        if nn == 0:
                            w_ffn = [f"xq_{ti}_{s_}", "wfc_7", f"q1b_{ti}_{s_}"]
                        else:
                            w_ffn = [f"yva_{ti}_{s_}"]
                        for fc in range(8):
                            steps.append(("pe", lambda e, fc=fc, nn=nn: e.matmul(
                                PS4[:, :],
                                xqT[:, fc * 128:(fc + 1) * 128],
                                wft_sb[:, fc * DM + nn * 512:
                                       fc * DM + (nn + 1) * 512],
                                start=(fc == 0), stop=(fc == 7)),
                                (w_ffn if fc == 0 else []),
                                (f"ffn{'ab'[nn]}_{ti}_{s_}" if fc == 7
                                 else None), None, True))
                        w_yv = [f"ffn{'ab'[nn]}_{ti}_{s_}", "isws_done",
                                "bbc_done"]
                        if nn == 0:
                            if s_ >= 1:
                                w_yv.append(f"sto_{ti}_{s_-1}")
                            elif ti >= 1:
                                w_yv.append(f"sto_{ti-1}_3")
                        steps.append(("dve", lambda e, nn=nn: e.scalar_tensor_tensor(
                            yv_sb[:, nn * 512:(nn + 1) * 512], PS4[:, :],
                            misc[:, MC_ISWS:MC_ISWS + 1],
                            bias_bc[:, nn * 512:(nn + 1) * 512],
                            ALU.mult, ALU.add),
                            w_yv,
                            (f"yva_{ti}_{s_}" if nn == 0 else f"yv_{ti}_{s_}"),
                            None, False))
                    steps.append(("act", lambda e: e.activation(
                        qtmp2[:], yv_sb[:], AF.Tanh, scale=0.5),
                        [f"yv_{ti}_{s_}"], f"th_{ti}_{s_}", None, False))
                    steps.append(("dve", lambda e: e.tensor_scalar(
                        gbuf[:], qtmp2[:], 0.5, 0.5, ALU.mult, ALU.add),
                        [f"th_{ti}_{s_}"], f"sig_{ti}_{s_}", None, False))
                    for nn in range(2):
                        if nn == 0:
                            w_wo = [f"aoev_{ti}_{HP-1}", "ld_wo",
                                    f"yv_{ti}_{s_}"]
                        else:
                            w_wo = [f"u_{ti}_{s_}_0"]
                        for hp in range(HP):
                            steps.append(("pe", lambda e, hp=hp, nn=nn, s_=s_,
                                          ao_=ao_: e.matmul(
                                PS4[:, :],
                                ao_[:, hp * 512 + s_ * 128:
                                    hp * 512 + (s_ + 1) * 128],
                                wo_sb[:, hp * DM + nn * 512:
                                      hp * DM + (nn + 1) * 512],
                                start=(hp == 0), stop=(hp == HP - 1)),
                                (w_wo if hp == 0 else []),
                                (f"wo{'ab'[nn]}_{ti}_{s_}" if hp == HP - 1
                                 else None), None, True))
                        steps.append(("dve", lambda e, nn=nn: e.tensor_tensor(
                            qtmp[:, :], PS4[:, :],
                            gbuf[:, nn * 512:(nn + 1) * 512], ALU.mult),
                            [f"wo{'ab'[nn]}_{ti}_{s_}", f"sig_{ti}_{s_}"],
                            f"u_{ti}_{s_}_{nn}", None, False))
                        steps.append(("dve", lambda e, nn=nn: e.tensor_tensor(
                            gbuf[:, nn * 512:(nn + 1) * 512], qtmp[:, :],
                            yv_sb[:, nn * 512:(nn + 1) * 512], ALU.mult),
                            [], (f"sg_{ti}_{s_}" if nn == 0
                                 else f"sgb_{ti}_{s_}"), None, False))
                    steps.append(("dve", lambda e, xsb=xsb: e.tensor_tensor(
                        yv_sb[:], gbuf[:], xsb[:], ALU.add),
                        [f"sgb_{ti}_{s_}"], f"res_{ti}_{s_}", None, False))
                    steps.append(("sp", lambda e, tt=tt: e.dma_start(
                        out_hbm[tt * 128:(tt + 1) * 128, :], yv_sb[:]),
                        [f"res_{ti}_{s_}"], f"sto_{ti}_{s_}", "dSo", False))
                steps.append(("pe", lambda e: e.nop(),
                              [f"wob_{ti}_3"], f"wo_{ti}_3", None, True))
                return steps

            for ti in range(NT2):
                S.add("sp", lambda e, ti=ti: e.dma_start(
                    qT_t[:],
                    qT_hbm[:, :].rearrange("p (hp n) -> p hp n", n=seq)[
                        :, :, ti * 512:(ti + 1) * 512]),
                    waits=([f"st_{ti-1}_{HP-1}_{NCH-1}"] if ti >= 1 else []),
                    dma="dQ", ms=f"ldq_{ti}")
                pend_sub = sub_steps(ti - 1) if ti >= 1 else []
                for hp in range(HP):
                    pp = f"{ti}_{hp}"
                    if hp >= 1:
                        pend = epi_steps(ti, hp - 1)
                    elif ti >= 1:
                        pend = epi_steps(ti - 1, HP - 1)
                    else:
                        pend = []
                    for c in range(NCH):
                        buf = PA if c % 2 == 0 else PB
                        w_st = [f"ldq_{ti}"] if c == 0 else []
                        if c >= 2:
                            w_st.append(f"exp_{pp}_{c-2}")
                        elif hp >= 1:
                            w_st.append(f"exp_{ti}_{hp-1}_{NCH-2+c}")
                        elif ti >= 1:
                            w_st.append(f"exp_{ti-1}_{HP-1}_{NCH-2+c}")
                        elif c == 1:
                            w_st.append("bbc_done")
                        for par in range(2):
                            b0 = par * 64
                            S.add("pe", lambda e, c=c, hp=hp, b0=b0, par=par,
                                  buf=buf, ti=ti: e.matmul(
                                buf[:, par * 512:(par + 1) * 512],
                                kt_sb[b0:b0 + 64,
                                      hp * seq + c * 128: hp * seq + (c + 1) * 128],
                                qT_t[b0:b0 + 64, hp * 512:(hp + 1) * 512],
                                start=True, stop=True, tile_position=(b0, 0)),
                                waits=(w_st if par == 0 else []),
                                ms=(f"st_{pp}_{c}" if par == 1 else None))
                        w_exp = [f"st_{pp}_{c}"]
                        if c >= 2:
                            w_exp.append(f"pv_{pp}_{c-2}")
                        elif hp >= 1:
                            w_exp.append(f"pv_{ti}_{hp-1}_{NCH-2+c}")
                        elif ti >= 1:
                            w_exp.append(f"pv_{ti-1}_{HP-1}_{NCH-2+c}")
                        if ti == 0 and hp == 0 and c == 0:
                            w_exp.append("negc_done")
                        S.add("act", lambda e, c=c, buf=buf: e.activation(
                            ET[:, (c % 2) * 1024:(c % 2) * 1024 + 1024],
                            buf[:, :], AF.Exp, bias=negc_bc[:, 0:1]),
                            waits=w_exp, ms=f"exp_{pp}_{c}")
                        for _ in range(2):
                            if pend and c >= 1:
                                st_ = pend.pop(0)
                                S.add(st_[0], st_[1], waits=st_[2], ms=st_[3],
                                      dma=st_[4], fence=st_[5])
                        for _ in range(1 if pend else 2):
                            if pend_sub and c >= 1:
                                st_ = pend_sub.pop(0)
                                S.add(st_[0], st_[1], waits=st_[2], ms=st_[3],
                                      dma=st_[4], fence=st_[5])
                        if c >= 1:
                            cc = c - 1
                            for par, pv in ((0, PVa), (1, PVb)):
                                w_pv = [f"exp_{pp}_{cc}"]
                                if ti == 0 and hp == 0 and cc == NCH // 2:
                                    w_pv.append("ld_v1")
                                if cc == 0:
                                    if hp >= 1:
                                        w_pv.append(f"pvev_{ti}_{hp-1}")
                                    elif ti >= 1:
                                        w_pv.append(f"pvev_{ti-1}_{HP-1}")
                                    else:
                                        w_pv.append("ld_v0")
                                        w_pv.append("wfc_7")
                                for sub in range(4):
                                    S.add("pe", lambda e, cc=cc, hp=hp, par=par,
                                          pv=pv, sub=sub: e.matmul(
                                        pv[:, sub * 65: sub * 65 + 65],
                                        ET[:, (cc % 2) * 1024 + par * 512 +
                                           sub * 128:
                                           (cc % 2) * 1024 + par * 512 +
                                           (sub + 1) * 128],
                                        v_sb[:, cc * VROW + (2 * hp + par) * V65:
                                             cc * VROW + (2 * hp + par) * V65 + V65],
                                        start=(cc == 0 and sub == 0),
                                        stop=(cc == NCH - 1 and sub == 3)),
                                        waits=(w_pv if par == 0 and sub == 0
                                               else []),
                                        ms=(f"pv_{pp}_{cc}"
                                            if par == 1 and sub == 3 else None))
                    cc = NCH - 1
                    for par, pv in ((0, PVa), (1, PVb)):
                        for sub in range(4):
                            S.add("pe", lambda e, cc=cc, hp=hp, par=par,
                                  pv=pv, sub=sub: e.matmul(
                                pv[:, sub * 65: sub * 65 + 65],
                                ET[:, (cc % 2) * 1024 + par * 512 + sub * 128:
                                   (cc % 2) * 1024 + par * 512 +
                                   (sub + 1) * 128],
                                v_sb[:, cc * VROW + (2 * hp + par) * V65:
                                     cc * VROW + (2 * hp + par) * V65 + V65],
                                start=False, stop=(sub == 3)),
                                waits=([f"exp_{pp}_{cc}"]
                                       if par == 0 and sub == 0 else []),
                                ms=(f"pv_{pp}_{cc}"
                                    if par == 1 and sub == 3 else None))
                    for st_ in pend:
                        S.add(st_[0], st_[1], waits=st_[2], ms=st_[3],
                              dma=st_[4], fence=st_[5])
                for st_ in pend_sub:
                    S.add(st_[0], st_[1], waits=st_[2], ms=st_[3],
                          dma=st_[4], fence=st_[5])
                if ti == NT2 - 1:
                    for st_ in epi_steps(ti, HP - 1):
                        S.add(st_[0], st_[1], waits=st_[2], ms=st_[3],
                              dma=st_[4], fence=st_[5])
                    for st_ in sub_steps(ti):
                        S.add(st_[0], st_[1], waits=st_[2], ms=st_[3],
                              dma=st_[4], fence=st_[5])
                S.add("pe", lambda e: e.nop(),
                    waits=[f"exp_{ti}_{HP-1}_{NCH-1}"], ms=f"atn_{ti}",
                    fence=True)

            total_dSo = S.dma_counts["dSo"]
            S.add("act", lambda e: e.wait_ge(sems["dSo"], total_dSo), raw=True)
            S.add("sp", lambda e: e.wait_ge(sems["dSo"], total_dSo), raw=True)

            _lim = os.environ.get("KDBG_P2LIMIT")
            _lim = int(_lim) if _lim else None
            with nc.Block() as blockB:
                blockB.tensor(S.replay("pe", sems, _lim))
                blockB.scalar(S.replay("act", sems, _lim))
                blockB.vector(S.replay("dve", sems, _lim))
                blockB.gpsimd(S.replay("pool", sems, _lim))
                blockB.sync(S.replay("sp", sems, _lim))
            print(f"[kernel] phase2 ops: {len(S.ops)} (limit={_lim})", flush=True)

    return nc


_CACHE = {}


def _get_program(seq):
    if seq not in _CACHE:
        _CACHE[seq] = build_program(seq)
    return _CACHE[seq]


def kernel(L0_attn_x, L0_attn_gamma, L0_attn_Wq, L0_attn_Wkv, L0_attn_W_bilinear,
           L0_attn_Wo, L0_ffn_weight_fp8, L0_ffn_bias, L0_ffn_input_scale,
           L0_ffn_weight_scale):
    x = np.ascontiguousarray(np.asarray(L0_attn_x, dtype=np.float32))
    B, seq, dm = x.shape
    assert dm == DM
    nc = _get_program(seq)

    ct, st2 = host_tables(seq)
    wb = np.asarray(L0_attn_W_bilinear, dtype=np.float32)
    wb_packed = np.zeros((128, HP * DH), dtype=np.float32)
    for h in range(H):
        wb_packed[(h % 2) * 64:(h % 2) * 64 + 64,
                  (h // 2) * DH:(h // 2) * DH + DH] = wb[h]
    idn = np.eye(128, dtype=np.float32)

    common = {
        "gamma": np.ascontiguousarray(
            np.asarray(L0_attn_gamma, dtype=np.float32).reshape(1, DM)),
        "wq": np.ascontiguousarray(np.asarray(L0_attn_Wq, dtype=np.float32)),
        "wkv": np.ascontiguousarray(np.asarray(L0_attn_Wkv, dtype=np.float32)),
        "wb": wb_packed,
        "wo": np.ascontiguousarray(np.asarray(L0_attn_Wo, dtype=np.float32)),
        "wf": np.ascontiguousarray(np.asarray(L0_ffn_weight_fp8, dtype=np.float32)),
        "bias": np.ascontiguousarray(
            np.asarray(L0_ffn_bias, dtype=np.float32).reshape(1, DM)),
        "is_": np.asarray(L0_ffn_input_scale, dtype=np.float32).reshape(1, 1),
        "ws": np.asarray(L0_ffn_weight_scale, dtype=np.float32).reshape(1, 1),
        "ct": ct, "st": st2, "idn": idn,
    }
    in_maps = []
    for c in range(B):
        m = dict(common)
        m["x"] = np.ascontiguousarray(x[c])
        in_maps.append(m)

    res = run_bass_kernel_spmd(nc, in_maps, core_ids=list(range(B)))
    out = np.stack([np.asarray(r["out"], dtype=np.float32)
                    for r in res.results], axis=0)
    return out
